# version 06
# baseline (speedup 1.0000x reference)
import numpy as np
import ml_dtypes
from contextlib import ExitStack
import concourse.bass as bass
import concourse.mybir as mybir
from concourse.bass_utils import run_bass_kernel_spmd

F32 = mybir.dt.float32
BF16 = mybir.dt.bfloat16
U32 = mybir.dt.uint32
AF = mybir.ActivationFunctionType
ALU = mybir.AluOpType
AX = mybir.AxisListType

T = 8192
D = 2048
NC = 8
TL = 1024
NT = 8
FC = 16
NH = 16
HD = 128
OCQ = 18
E = 16
EL = 2
CAP = 1024
ICE = 11
ICS = 22
OC = 16
SCALE = 1.0 / float(np.sqrt(128.0))
NMAIN = 288

ENGINES = ("pe", "act", "dve", "pool", "sp")


class Sched:

    def __init__(self, nc, es):
        self.nc = nc
        self.es = es
        self.cmds = {e: [] for e in ENGINES}
        self.semh = {}
        self.semcnt = {}
        self.last_write = {}
        self.readers = {}
        self.waited = {e: {} for e in ENGINES}
        self.esem = {e: self.sem("S_" + e) for e in ENGINES}

    def sem(self, name):
        if name not in self.semh:
            self.semh[name] = self.es.enter_context(self.nc.semaphore(name))
            self.semcnt[name] = 0
        return name

    def _wait(self, eng, semname, val):
        if val <= 0 or self.waited[eng].get(semname, 0) >= val:
            return
        self.waited[eng][semname] = val
        self.cmds[eng].append(("wait", semname, val))

    def _dep_waits(self, eng, reads, writes):
        for tag in reads:
            lw = self.last_write.get(tag)
            if lw is not None:
                self._wait(eng, lw[0], lw[1])
        for tag in writes:
            lw = self.last_write.get(tag)
            if lw is not None:
                self._wait(eng, lw[0], lw[1])
            for (s, v) in self.readers.get(tag, ()):
                self._wait(eng, s, v)

    def _record(self, semname, val, reads, writes):
        for tag in writes:
            self.last_write[tag] = (semname, val)
            self.readers[tag] = []
        for tag in reads:
            self.readers.setdefault(tag, []).append((semname, val))

    def emit(self, eng, fn, reads=(), writes=(), inc=True):
        self._dep_waits(eng, reads, writes)
        if inc:
            s = self.esem[eng]
            self.semcnt[s] += 1
            self.cmds[eng].append(("emit", fn, s, 1))
            self._record(s, self.semcnt[s], reads, writes)
        else:
            self.cmds[eng].append(("emit", fn, None, 0))

    def dma(self, eng, fn, sem, reads=(), writes=()):
        self.sem(sem)
        self._dep_waits(eng, reads, writes)
        self.semcnt[sem] += 16
        self.cmds[eng].append(("emit", fn, sem, 16))
        self._record(sem, self.semcnt[sem], reads, writes)

    def coll(self, fn, sem, reads=(), writes=()):
        self.sem(sem)
        self._dep_waits("pool", reads, writes)
        self.semcnt[sem] += 1
        self.cmds["pool"].append(("emit", fn, sem, 1))
        self._record(sem, self.semcnt[sem], reads, writes)

    def group_commit(self, sem, tags):
        tot = self.semcnt[sem]
        for tag in tags:
            self.last_write[tag] = (sem, tot)
            self.readers[tag] = []

    def barrier(self, name, engines=ENGINES):
        s = self.sem("B_" + name)
        for e in engines:
            self.semcnt[s] += 1
            self.cmds[e].append(("emit", None, s, 1))
        tot = self.semcnt[s]
        for e in engines:
            self._wait(e, s, tot)

    def final_wait(self, eng, sem):
        self.cmds[eng].append(("wait", sem, self.semcnt[sem]))

    def verify(self):
        pos = {e: 0 for e in ENGINES}
        cnt = {s: 0 for s in self.semh}
        progress = True
        while progress:
            progress = False
            for e in ENGINES:
                while pos[e] < len(self.cmds[e]):
                    c = self.cmds[e][pos[e]]
                    if c[0] == "wait":
                        if cnt[c[1]] < c[2]:
                            break
                    elif c[2] is not None:
                        cnt[c[2]] += c[3]
                    pos[e] += 1
                    progress = True
        stuck = {}
        for e in ENGINES:
            if pos[e] < len(self.cmds[e]):
                c = self.cmds[e][pos[e]]
                stuck[e] = f"cmd {pos[e]}/{len(self.cmds[e])}: wait {c[1]}>={c[2]} (at {cnt.get(c[1])})"
        if stuck:
            raise RuntimeError(f"DEADLOCK in schedule: {stuck}")

    def replay(self, block):
        proxies = {"pe": block.tensor, "act": block.scalar, "dve": block.vector,
                   "pool": block.gpsimd, "sp": block.sync}

        def mk(cmds):
            def body(eng):
                for c in cmds:
                    if c[0] == "wait":
                        eng.wait_ge(self.semh[c[1]], c[2])
                    else:
                        _, fn, s, amt = c
                        if fn is None:
                            eng.nop().then_inc(self.semh[s], amt)
                        else:
                            inst = fn(eng)
                            if s is not None:
                                inst.then_inc(self.semh[s], amt)
            return body
        for e in ENGINES:
            proxies[e](mk(self.cmds[e]))


def build(debug=False, upto=99, sub=99):
    nc = bass.Bass("TRN2", target_bir_lowering=False, debug=False)
    es = ExitStack()

    X = nc.dram_tensor("x", [TL, D], F32, kind="ExternalInput").ap()
    WTA = nc.dram_tensor("wta", [OCQ, 128, FC, 128], F32, kind="ExternalInput").ap()
    WTP = nc.dram_tensor("wtp", [OC, 128, FC, 128], F32, kind="ExternalInput").ap()
    GT = nc.dram_tensor("gt", [128, FC, E], F32, kind="ExternalInput").ap()
    BATTN = nc.dram_tensor("battn", [128, OCQ], F32, kind="ExternalInput").ap()
    BPROJ = nc.dram_tensor("bproj", [1, D], F32, kind="ExternalInput").ap()
    SWG = nc.dram_tensor("swg", [ICS, 128, FC, 128], F32, kind="ExternalInput").ap()
    SWU = nc.dram_tensor("swu", [ICS, 128, FC, 128], F32, kind="ExternalInput").ap()
    SWD = nc.dram_tensor("swd", [OC, 128, ICS, 128], F32, kind="ExternalInput").ap()
    WG = nc.dram_tensor("wg", [EL, ICE, 128, FC, 128], F32, kind="ExternalInput").ap()
    WU = nc.dram_tensor("wu", [EL, ICE, 128, FC, 128], F32, kind="ExternalInput").ap()
    WD = nc.dram_tensor("wd", [EL, OC, 128, ICE, 128], F32, kind="ExternalInput").ap()
    TRI = nc.dram_tensor("tri", [128, 8 * 128], BF16, kind="ExternalInput").ap()
    LTRI = nc.dram_tensor("ltri", [128, 128], BF16, kind="ExternalInput").ap()
    IDENT = nc.dram_tensor("ident", [128, 128], BF16, kind="ExternalInput").ap()
    ONESB = nc.dram_tensor("onesb", [128, 1], BF16, kind="ExternalInput").ap()
    ONESM = nc.dram_tensor("onesm", [128, 128], BF16, kind="ExternalInput").ap()
    ONESF = nc.dram_tensor("onesf", [1, 128], F32, kind="ExternalInput").ap()
    CF = nc.dram_tensor("cf", [128, NMAIN + 3], F32, kind="ExternalInput").ap()
    PSM = nc.dram_tensor("psm", [64, NT], BF16, kind="ExternalInput").ap()
    EJM = nc.dram_tensor("ejm", [NT, NT * 128], F32, kind="ExternalInput").ap()
    IOTA16 = nc.dram_tensor("iota16", [128, E], F32, kind="ExternalInput").ap()
    VALS16 = nc.dram_tensor("vals16", [128, E], F32, kind="ExternalInput").ap()
    ROWID = nc.dram_tensor("rowid", [128, NT], U32, kind="ExternalInput").ap()
    BUFIDX = nc.dram_tensor("bufidx", [128, EL * 8], U32, kind="ExternalInput").ap()
    Y = nc.dram_tensor("y", [TL, D], F32, kind="ExternalOutput").ap()

    xnag_in = nc.dram_tensor("xnag_in", [TL, D], BF16).ap()
    xnag_out = nc.dram_tensor("xnag_out", [T, D], BF16).ap()
    kag_in = nc.dram_tensor("kag_in", [128, TL], BF16).ap()
    kag_out = nc.dram_tensor("kag_out", [NC * 128, TL], BF16).ap()
    vag_in = nc.dram_tensor("vag_in", [128, NT * HD], BF16).ap()
    vag_out = nc.dram_tensor("vag_out", [NC * 128, NT * HD], BF16).ap()
    cnt_in = nc.dram_tensor("cnt_in", [NT, E], F32).ap()
    cnt_out = nc.dram_tensor("cnt_out", [64, E], F32).ap()
    buf_in = nc.dram_tensor("buf_in", [E * CAP, 1], U32).ap()
    buf_out = nc.dram_tensor("buf_out", [E * CAP, 1], U32).ap()
    acc_dram = nc.dram_tensor("acc_dram", [TL, D], F32).ap()
    xg_dram = nc.dram_tensor("xg_dram", [EL * CAP, D], BF16).ap()
    fout_in = nc.dram_tensor("fout_in", [EL * CAP, D], BF16).ap()
    fout_ag2 = nc.dram_tensor("fout_ag2", [2, NC * CAP, D], BF16).ap()

    dbg = {}
    if debug:
        for nm, shp, dt in [
            ("d_cnt", [64, E], F32), ("d_w", [128, NT * E], F32),
            ("d_rank", [128, NT * E], F32), ("d_idx", [128, NT * 4], U32),
            ("d_w4", [128, NT * 4], F32), ("d_buf", [E * CAP, 1], U32),
            ("d_q", [128, NH * TL], BF16), ("d_k", [128, TL], BF16),
            ("d_ctx", [128, NH * TL], BF16), ("d_acc", [TL, D], F32),
            ("d_xg", [EL * CAP, D], BF16), ("d_fout", [EL * CAP, D], BF16),
            ("d_idx2", [128, NT * 4], U32), ("d_fag", [2 * NC * CAP, D], BF16),
            ("d_bc", [128, NT * E], F32), ("d_excl", [128, NT * E], F32),
        ]:
            dbg[nm] = nc.dram_tensor(nm, shp, dt, kind="ExternalOutput").ap()

    S = Sched(nc, es)

    def _finish():
        S.verify()
        with nc.Block() as block:
            S.replay(block)
        return nc, dbg

    def sb(name, shape, dt, stack=es, side=None):
        kw = {} if side is None else {"side": side}
        return stack.enter_context(nc.sbuf_tensor(name, shape, dt, **kw))

    tri_sb = sb("tri_sb", [128, 8 * 128], BF16)
    ltri_sb = sb("ltri_sb", [128, 128], BF16)
    ident_sb = sb("ident_sb", [128, 128], BF16)
    onesb_sb = sb("onesb_sb", [128, 1], BF16)
    onesm_sb = sb("onesm_sb", [128, 128], BF16)
    onesf_sb = sb("onesf_sb", [1, 128], F32)
    cf_sb = sb("cf_sb", [128, NMAIN + 3], F32)
    psm_sb = sb("psm_sb", [64, NT], BF16)
    ejm_sb = sb("ejm_sb", [NT, NT * 128], F32)
    iota_sb = sb("iota_sb", [128, E], F32)
    vals_sb = sb("vals_sb", [128, E], F32)
    rowid_sb = sb("rowid_sb", [128, NT], U32)
    bufidx_sb = sb("bufidx_sb", [128, EL * 8], U32)
    gt_sb = sb("gt_sb", [128, FC, E], BF16)
    battn_sb = sb("battn_sb", [128, OCQ], F32)
    xn_fm = sb("xn_fm", [128, FC, TL], BF16)
    w_all = sb("w_all", [128, NT, E], F32)
    excl_all = sb("excl_all", [128, NT, E], F32)
    rank_all = sb("rank_all", [128, NT, E], F32)
    e4f = sb("e4f", [128, NT, 4], F32)
    w4k = sb("w4k", [128, NT, 4], F32)
    idxs = sb("idxs", [128, NT, 4], U32)
    idxs2 = sb("idxs2", [128, NT, 4], U32)
    r_t2 = sb("r_t2", [128, E], F32)
    r_t3 = sb("r_t3", [128, 1], F32)
    cntstage = sb("cntstage", [1, NT, E], F32)
    cnts_sb = sb("cnts_sb", [64, E], BF16)
    base_sb = sb("base_sb", [NT, E], F32)
    bc_sb = sb("bc_sb", [128, NT, E], F32)
    stat_sb = sb("stat_sb", [128, 4, 6], F32)
    mv_sb = sb("mv_sb", [128, 2], F32)
    rstd_sb = sb("rstd_sb", [128, 1], F32)
    negmr_sb = sb("negmr_sb", [128, 1], F32)
    r_nmax = sb("r_nmax", [128, 1], F32)
    r_ex = sb("r_ex", [128, E], F32)
    r_s1 = sb("r_s1", [128, 1], F32)
    r_probs = sb("r_probs", [128, E], F32)
    r_m8 = sb("r_m8", [128, 8], F32)
    r_idx8 = sb("r_idx8", [128, 8], U32)
    r_sel = sb("r_sel", [128, E], F32)
    r_selb = sb("r_selb", [128, E], BF16)
    r_kept = sb("r_kept", [128, E], F32)
    r_brank = sb("r_brank", [128, E], F32)
    r_oh = sb("r_oh", [128, E], F32)
    r_wk = sb("r_wk", [128, E], F32)
    r_t1 = sb("r_t1", [128, 1], F32)
    zero_u32 = sb("zero_u32", [128, 128], U32)
    fmsbA = sb("fmsbA", [128, 512], BF16)
    fmsbB = sb("fmsbB", [128, 512], BF16)
    sil_sb = sb("sil_sb", [128, 512], BF16)
    wpA = sb("wpA", [128, ICS, 128], BF16)
    wpB = sb("wpB", [128, ICS, 128], BF16)
    sgA = sb("sgA", [128, FC, 128], BF16)
    sgB = sb("sgB", [128, FC, 128], BF16)
    suA = sb("suA", [128, FC, 128], BF16)
    suB = sb("suB", [128, FC, 128], BF16)
    mybuf = sb("mybuf", [128, EL * 8], U32)
    xgA = sb("xgA", [128, D], BF16)
    xgB = sb("xgB", [128, D], BF16)
    es_ctx = ExitStack()
    ctx_all = sb("ctx_all", [128, NH, TL], BF16, stack=es_ctx)

    es_r1 = ExitStack()
    x_all = sb("x_all", [128, NT, D], F32, stack=es_r1, side="right")
    xnstA = sb("xnstA", [128, D], BF16, stack=es_r1, side="right")
    xnstB = sb("xnstB", [128, D], BF16, stack=es_r1, side="right")

    EPS = cf_sb[:, NMAIN + 1:NMAIN + 2]
    ZCOL = cf_sb[:, NMAIN + 2:NMAIN + 3]
    DIAGB = cf_sb[:, NMAIN:NMAIN + 1]

    for t, src in [(tri_sb, TRI), (ltri_sb, LTRI), (ident_sb, IDENT),
                   (onesb_sb, ONESB), (onesm_sb, ONESM), (onesf_sb, ONESF), (cf_sb, CF),
                   (psm_sb, PSM), (ejm_sb, EJM), (iota_sb, IOTA16), (vals_sb, VALS16),
                   (rowid_sb, ROWID), (bufidx_sb, BUFIDX), (battn_sb, BATTN)]:
        S.dma("sp", lambda e, t=t, src=src: e.dma_start(out=t[...], in_=src), "c_ld")
    S.dma("pool", lambda e: e.dma_start(out=gt_sb[...], in_=GT), "c_ld")
    S.group_commit("c_ld", ["consts"])
    S.dma("sp", lambda e: e.dma_start(
        out=x_all[...], in_=X.rearrange("(j p) d -> p j d", p=128)), "x_ld",
        writes=["x_all"])

    es_p1 = ExitStack()
    lg_ps = es_p1.enter_context(nc.psum_tensor("lg_ps", [128, 512], F32))[:, 0:E]
    cs_ps = es_p1.enter_context(nc.psum_tensor("cs_ps", [128, 512], F32))[:, 0:E]
    cnt_ps = es_p1.enter_context(nc.psum_tensor("cnt_ps", [1, 512], F32))[:, 0:E]
    qkv_psA = es_p1.enter_context(nc.psum_tensor("qkv_psA", [128, 512], F32))
    qkv_psB = es_p1.enter_context(nc.psum_tensor("qkv_psB", [128, 512], F32))
    vt_ps = es_p1.enter_context(nc.psum_tensor("vt_ps", [128, 1024], BF16))[:, 0:128]
    base_ps = es_p1.enter_context(nc.psum_tensor("base_ps", [NT, 512], F32))[:, 0:E]
    bc_ps = es_p1.enter_context(nc.psum_tensor("bc_ps", [128, 512], F32))[:, 0:E]

    for j in range(NT):
        xst, xt = (xnstA, ("xnst", 0)) if j % 2 == 0 else (xnstB, ("xnst", 1))
        for cch in range(4):
            S.emit("dve", lambda e, j=j, cch=cch: e.bn_stats(
                stat_sb[:, cch, :], x_all[:, j, cch * 512:(cch + 1) * 512]),
                reads=["x_all"], writes=["stat"])
        S.emit("dve", lambda e: e.bn_aggr(mv_sb[...], stat_sb[...]),
               reads=["stat"], writes=["mv"])
        S.emit("act", lambda e: e.activation(rstd_sb[...], mv_sb[:, 1:2], AF.Ln,
               bias=EPS, scale=1.0), reads=["mv", "consts"], writes=["rstd"])
        S.emit("act", lambda e: e.activation(rstd_sb[...], rstd_sb[...], AF.Exp,
               bias=ZCOL, scale=-0.5), reads=["consts"], writes=["rstd"])
        S.emit("dve", lambda e: e.tensor_scalar(negmr_sb[...], mv_sb[:, 0:1],
               rstd_sb[...], -1.0, ALU.mult, ALU.mult),
               reads=["mv", "rstd"], writes=["negmr"])
        S.emit("dve", lambda e, xst=xst, j=j: e.tensor_scalar(xst[...], x_all[:, j, :],
               rstd_sb[...], negmr_sb[...], ALU.mult, ALU.add),
               reads=["x_all", "rstd", "negmr"], writes=[xt])
        S.dma("sp", lambda e, xst=xst, j=j: e.dma_start(
            out=xnag_in[j * 128:(j + 1) * 128, :], in_=xst[...]), "xn_st",
            reads=[xt], writes=["xnag_in"])
    for fcc in range(FC):
        S.dma("sp", lambda e, fcc=fcc: e.dma_start_transpose(
            xn_fm[:, fcc, :], xnag_in[:, fcc * 128:(fcc + 1) * 128]), "xnfm_ld",
            reads=["xnag_in"])
    S.group_commit("xnfm_ld", ["xn_fm"])

    for j in range(NT):
        for fcc in range(FC):
            S.emit("pe", lambda e, j=j, fcc=fcc: e.matmul(
                lg_ps[...], xn_fm[:, fcc, j * 128:(j + 1) * 128], gt_sb[:, fcc, :],
                start=(fcc == 0), stop=(fcc == FC - 1)),
                reads=(["xn_fm", "consts"] if fcc in (0, FC - 1) else []),
                writes=(["lg_ps"] if fcc in (0, FC - 1) else []),
                inc=(fcc == FC - 1))
        S.emit("dve", lambda e: e.tensor_reduce(r_nmax[...], lg_ps[...], AX.X, ALU.max,
               negate=True), reads=["lg_ps"], writes=["r_nmax"])
        S.emit("act", lambda e: e.activation(r_ex[...], lg_ps[...], AF.Exp,
               bias=r_nmax[...], scale=1.0), reads=["lg_ps", "r_nmax"], writes=["r_ex"])
        S.emit("dve", lambda e: e.tensor_reduce(r_s1[...], r_ex[...], AX.X, ALU.add),
               reads=["r_ex"], writes=["r_s1"])
        S.emit("dve", lambda e: e.reciprocal(r_s1[...], r_s1[...]), writes=["r_s1"])
        S.emit("dve", lambda e: e.tensor_scalar(r_probs[...], r_ex[...], r_s1[...],
               None, ALU.mult), reads=["r_ex", "r_s1"], writes=["r_probs"])
        S.emit("dve", lambda e: e.max(r_m8[...], r_probs[...]),
               reads=["r_probs"], writes=["r_m8"])
        S.emit("dve", lambda e: e.max_index(r_idx8[...], r_m8[...], r_probs[...]),
               reads=["r_m8", "r_probs"], writes=["r_idx8"])
        S.emit("dve", lambda e, j=j: e.tensor_copy(e4f[:, j, :], r_idx8[:, 0:4]),
               reads=["r_idx8"], writes=["e4f"])
        S.emit("dve", lambda e: e.tensor_copy(r_t1[...], r_m8[:, 3:4]),
               reads=["r_m8"], writes=["r_t1"])
        S.emit("dve", lambda e: e.tensor_scalar(r_sel[...], r_probs[...], r_t1[...],
               None, ALU.is_ge), reads=["r_probs", "r_t1"], writes=["r_sel"])
        S.emit("dve", lambda e: e.tensor_copy(r_selb[...], r_sel[...]),
               reads=["r_sel"], writes=["r_selb"])
        S.emit("dve", lambda e: e.tensor_reduce(r_t1[...], r_m8[:, 0:4], AX.X, ALU.add),
               reads=["r_m8"], writes=["r_t1"])
        S.emit("dve", lambda e: e.reciprocal(r_t1[...], r_t1[...]), writes=["r_t1"])
        S.emit("dve", lambda e, j=j: e.scalar_tensor_tensor(w_all[:, j, :], r_probs[...],
               r_t1[...], r_sel[...], ALU.mult, ALU.mult),
               reads=["r_probs", "r_t1", "r_sel"], writes=["w_all"])
        S.emit("pe", lambda e: e.matmul(cs_ps[...], ltri_sb[...], r_selb[...],
               start=True, stop=True), reads=["consts", "r_selb"], writes=["cs_ps"])
        S.emit("pe", lambda e: e.matmul(cnt_ps[...], onesb_sb[...], r_selb[...],
               start=True, stop=True), reads=["consts", "r_selb"], writes=["cnt_ps"])
        S.emit("dve", lambda e, j=j: e.tensor_copy(excl_all[:, j, :], cs_ps[...]),
               reads=["cs_ps"], writes=["excl_all"])
        S.emit("dve", lambda e, j=j: e.tensor_copy(cntstage[:, j, :], cnt_ps[...]),
               reads=["cnt_ps"], writes=["cntstage"])
    S.dma("pool", lambda e: e.dma_start(out=cnt_in.rearrange("(o j) e -> o j e", o=1),
          in_=cntstage[...]), "cnt_st",
          reads=["cntstage"], writes=["cnt_in"])
    S.coll(lambda e: e.collective_compute("AllGather", ALU.bypass,
           replica_groups=[list(range(NC))], ins=[cnt_in.opt()], outs=[cnt_out.opt()]),
           "cc_cnt", reads=["cnt_in"], writes=["cnt_out"])
    S.coll(lambda e: e.collective_compute("AllGather", ALU.bypass,
           replica_groups=[list(range(NC))], ins=[xnag_in.opt()], outs=[xnag_out.opt()]),
           "cc_xn", reads=["xnag_in"], writes=["xnag_out"])

    if upto <= 1:
        return _finish()

    es_r1.close()
    es_r2 = ExitStack()
    Q_all = sb("Q_all", [128, NH, TL], BF16, stack=es_r2, side="right")
    K_own = sb("K_own", [128, TL], BF16, stack=es_r2, side="right")
    V_own = sb("V_own", [128, NT, HD], BF16, stack=es_r2, side="right")
    wbufA = sb("wbufA", [128, FC, 128], BF16, stack=es_r2, side="right")
    wbufB = sb("wbufB", [128, FC, 128], BF16, stack=es_r2, side="right")
    K_sb = sb("K_sb", [128, NC, TL], BF16, stack=es_r2, side="right")
    V_sb = sb("V_sb", [128, NC, NT * HD], BF16, stack=es_r2, side="right")
    E0 = sb("E0", [128, 1024], BF16, stack=es_r2, side="right")
    E1 = sb("E1", [128, 1024], BF16, stack=es_r2, side="right")
    E2 = sb("E2", [128, 1024], BF16, stack=es_r2, side="right")
    Ed = sb("Ed", [128, 1024], BF16, stack=es_r2, side="right")
    rbc_sb = sb("rbc_sb", [128, 1024], F32, stack=es_r2, side="right")
    es_r2b = ExitStack()
    V_fm_stage = sb("V_fm_stage", [128, TL], BF16, stack=es_r2b, side="right")

    for oc in range(OCQ):
        wb, wt = (wbufA, ("wq", 0)) if oc % 2 == 0 else (wbufB, ("wq", 1))
        S.dma("pool", lambda e, oc=oc, wb=wb: e.dma_start(out=wb[...], in_=WTA[oc]),
              "wq_ld" + str(oc % 2), writes=[wt])
        for th in range(2):
            par = (oc * 2 + th) % 2
            ps = qkv_psA if par == 0 else qkv_psB
            pt = ("qkv_ps", par)
            for fcc in range(FC):
                S.emit("pe", lambda e, th=th, fcc=fcc, ps=ps, wb=wb: e.matmul(
                    ps[...], wb[:, fcc, :], xn_fm[:, fcc, th * 512:(th + 1) * 512],
                    start=(fcc == 0), stop=(fcc == FC - 1)),
                    reads=([wt, "xn_fm"] if fcc in (0, FC - 1) else []),
                    writes=([pt] if fcc in (0, FC - 1) else []),
                    inc=(fcc == FC - 1))
            if oc < NH:
                dsts, dtag = (lambda oc=oc, th=th: Q_all[:, oc, th * 512:(th + 1) * 512]), "Q_all"
            elif oc == NH:
                dsts, dtag = (lambda oc=oc, th=th: K_own[:, th * 512:(th + 1) * 512]), "K_own"
            else:
                dsts, dtag = (lambda oc=oc, th=th: V_fm_stage[:, th * 512:(th + 1) * 512]), "V_fm"
            S.emit("dve", lambda e, dsts=dsts, ps=ps, oc=oc: e.tensor_scalar(
                dsts(), ps[...], battn_sb[:, oc:oc + 1], None, ALU.add),
                reads=[pt, "consts"], writes=[dtag])
    for j in range(NT):
        S.emit("pe", lambda e, j=j: e.transpose(
            vt_ps[...], V_fm_stage[:, j * 128:(j + 1) * 128], ident_sb[...]),
            reads=["V_fm", "consts"], writes=["vt_ps"])
        S.emit("dve", lambda e, j=j: e.tensor_copy(V_own[:, j, :], vt_ps[...]),
               reads=["vt_ps"], writes=["V_own"])
    S.dma("pool", lambda e: e.dma_start(out=kag_in, in_=K_own[...]), "kv_st",
          reads=["K_own"], writes=["kag_in"])
    S.dma("pool", lambda e: e.dma_start(
        out=vag_in, in_=V_own[...].rearrange("p j d -> p (j d)")), "kv_st",
        reads=["V_own"], writes=["vag_in"])
    S.coll(lambda e: e.collective_compute("AllGather", ALU.bypass,
           replica_groups=[list(range(NC))], ins=[kag_in.opt()], outs=[kag_out.opt()]),
           "cc_k", reads=["kag_in"], writes=["kag_out"])
    S.coll(lambda e: e.collective_compute("AllGather", ALU.bypass,
           replica_groups=[list(range(NC))], ins=[vag_in.opt()], outs=[vag_out.opt()]),
           "cc_v", reads=["vag_in"], writes=["vag_out"])
    for r in range(NC):
        S.dma("sp", lambda e, r=r: e.dma_start(
            out=K_sb[:, r, :], in_=kag_out[r * 128:(r + 1) * 128, :]), "ksb_ld",
            reads=["kag_out"])
        S.dma("sp", lambda e, r=r: e.dma_start(
            out=V_sb[:, r, :], in_=vag_out.rearrange("(r p) f -> r p f", p=128)[r]),
            "vsb_ld", reads=["vag_out"])
    S.group_commit("ksb_ld", ["K_sb"])
    S.group_commit("vsb_ld", ["V_sb"])

    if upto <= 2:
        return _finish()

    S.dma("pool", lambda e: e.dma_start(out=cnts_sb[...], in_=cnt_out), "cnts_ld",
          reads=["cnt_out"], writes=["cnts_sb"])
    S.emit("pe", lambda e: e.matmul(base_ps[...], psm_sb[...], cnts_sb[...],
           start=True, stop=True), reads=["consts", "cnts_sb"], writes=["base_ps"])
    S.emit("dve", lambda e: e.tensor_copy(base_sb[...], base_ps[...]),
           reads=["base_ps"], writes=["base_sb"])
    for j in range(NT):
        S.emit("pe", lambda e, j=j: e.matmul(bc_ps[...], ejm_sb[:, j * 128:(j + 1) * 128],
               base_sb[...], start=True, stop=True),
               reads=["consts", "base_sb"], writes=["bc_ps"])
        S.emit("dve", lambda e, j=j: e.tensor_copy(bc_sb[:, j, :], bc_ps[...]),
               reads=["bc_ps"], writes=["bc_sb"])
    def rank_unit0(j):
        S.emit("dve", lambda e, j=j: e.tensor_tensor(rank_all[:, j, :],
               excl_all[:, j, :], bc_sb[:, j, :], ALU.add),
               reads=["excl_all", "bc_sb"], writes=["rank_all"])
        S.emit("dve", lambda e, j=j: e.tensor_scalar(r_kept[...], rank_all[:, j, :],
               1023.5, None, ALU.is_le), reads=["rank_all"], writes=["r_kept"])
        S.emit("dve", lambda e, j=j: e.scalar_tensor_tensor(r_brank[...],
               r_kept[...], -1.0e6, rank_all[:, j, :], ALU.mult, ALU.add),
               reads=["r_kept", "rank_all"], writes=["r_brank"])
        S.emit("dve", lambda e: e.tensor_scalar(r_brank[...], r_brank[...], 1.0e6,
               None, ALU.add), writes=["r_brank"])
        S.emit("dve", lambda e, j=j: e.tensor_tensor(r_wk[...], w_all[:, j, :],
               r_kept[...], ALU.mult), reads=["w_all", "r_kept"],
               writes=["r_wk"])

    def rank_unitk(j, k):
        S.emit("dve", lambda e, j=j, k=k: e.tensor_scalar(r_oh[...], iota_sb[...],
               e4f[:, j, k:k + 1], None, ALU.is_equal),
               reads=["consts", "e4f"], writes=["r_oh"])
        S.emit("dve", lambda e: e.tensor_tensor(r_ex[...], r_oh[...], r_wk[...],
               ALU.mult), reads=["r_oh", "r_wk"], writes=["r_ex"])
        S.emit("dve", lambda e, j=j, k=k: e.tensor_reduce(w4k[:, j, k:k + 1],
               r_ex[...], AX.X, ALU.add), reads=["r_ex"], writes=["w4k"])
        S.emit("dve", lambda e: e.tensor_tensor(r_ex[...], r_oh[...], r_brank[...],
               ALU.mult), reads=["r_oh", "r_brank"], writes=["r_ex"])
        S.emit("dve", lambda e: e.tensor_reduce(r_t1[...], r_ex[...], AX.X, ALU.add),
               reads=["r_ex"], writes=["r_t1"])
        S.emit("dve", lambda e, j=j, k=k: e.scalar_tensor_tensor(r_t1[...],
               e4f[:, j, k:k + 1], 1024.0, r_t1[...], ALU.mult, ALU.add),
               reads=["e4f"], writes=["r_t1"])
        S.emit("dve", lambda e: e.tensor_scalar(r_t1[...], r_t1[...], 0.25,
               None, ALU.add), writes=["r_t1"])
        S.emit("dve", lambda e, j=j, k=k: e.tensor_copy(idxs[:, j, k:k + 1],
               r_t1[...]), reads=["r_t1"], writes=["idxs"])
        S.emit("dve", lambda e: e.tensor_tensor(r_t2[...], r_oh[...],
               vals_sb[...], ALU.mult), reads=["r_oh", "consts"],
               writes=["r_t2x"])
        S.emit("dve", lambda e: e.tensor_reduce(r_t3[...], r_t2[...], AX.X,
               ALU.add), reads=["r_t2x"], writes=["r_t3"])
        S.emit("dve", lambda e, j=j, k=k: e.tensor_tensor(
               idxs2[:, j, k:k + 1], r_t3[...], r_t1[...], ALU.add),
               reads=["r_t3", "r_t1"], writes=["idxs2"])

    rank_units = []
    for j in range(NT):
        rank_units.append(("t", j, 0))
        for k in range(4):
            rank_units.append(("k", j, k))

    if debug:
        S.dma("sp", lambda e: e.dma_start(out=dbg["d_cnt"], in_=cnt_out), "dbg",
              reads=["cnt_out"])
        S.dma("sp", lambda e: e.dma_start(
            out=dbg["d_q"], in_=Q_all[...].rearrange("p h t -> p (h t)")), "dbg",
            reads=["Q_all"])
        S.dma("sp", lambda e: e.dma_start(out=dbg["d_k"], in_=K_own[...]), "dbg",
              reads=["K_own"])

    if upto <= 3:
        return _finish()

    es_p1.close()
    es_r2b.close()
    S.barrier("attn_in", ("pe", "act", "dve"))

    es_p4 = ExitStack()
    s_psA = es_p4.enter_context(nc.psum_tensor("s_psA", [128, 1024], F32))
    s_psB = es_p4.enter_context(nc.psum_tensor("s_psB", [128, 1024], F32))
    ctx_ps = es_p4.enter_context(nc.psum_tensor("ctx_ps", [128, 1024], F32))
    den_ps = es_p4.enter_context(nc.psum_tensor("den_ps", [128, 1024], F32))

    def colof(j, i):
        return 4 * j * (j + 1) + i

    def emit_scores_exp(j, hg, i, ntile):
        diag = (i == ntile)
        par = i % 2
        sp_ = s_psA if par == 0 else s_psB
        spt = ("s_ps", par)
        Ebuf = (E0, E1, E2)[i % 3]
        Et = ("E", i % 3)
        if diag:
            kap = K_own[:, j * 128:(j + 1) * 128]
            vap = V_own[:, j, :]
            ktags = ["K_own", "V_own"]
            biasap = DIAGB
        else:
            R, J = i % 8, i // 8
            kap = K_sb[:, R, J * 128:(J + 1) * 128]
            vap = V_sb[:, R, J * HD:(J + 1) * HD]
            ktags = ["K_sb", "V_sb"]
            biasap = cf_sb[:, colof(j, i):colof(j, i) + 1]
        S.emit("pe", lambda e, kap=kap, j=j, hg=hg, sp_=sp_: e.matmul(
            sp_[:, 0:512], kap,
            Q_all[:, hg * 8:hg * 8 + 4, j * 128:(j + 1) * 128],
            start=True, stop=True),
            reads=ktags + ["Q_all"], writes=[spt], inc=False)
        S.emit("pe", lambda e, kap=kap, j=j, hg=hg, sp_=sp_: e.matmul(
            sp_[:, 512:1024], kap,
            Q_all[:, hg * 8 + 4:hg * 8 + 8, j * 128:(j + 1) * 128],
            start=True, stop=True),
            reads=ktags + ["Q_all"], writes=[spt])
        S.emit("act", lambda e, sp_=sp_, Ebuf=Ebuf, biasap=biasap: e.activation(
            Ebuf[...], sp_[...], AF.Exp, bias=biasap, scale=SCALE),
            reads=[spt, "consts"], writes=[Et])
        srcb, stag = Ebuf, Et
        if diag:
            S.emit("dve", lambda e, Ebuf=Ebuf: e.tensor_tensor(
                Ed[...], Ebuf[...], tri_sb[...], ALU.mult),
                reads=[Et, "consts"], writes=["Ed"])
            srcb, stag = Ed, "Ed"
        return (srcb, stag, vap, ktags, i == 0, diag)

    def emit_den_ctx(v):
        srcb, stag, vap, ktags, first, last = v
        S.emit("pe", lambda e, srcb=srcb, first=first, last=last: e.matmul(
            den_ps[:, 0:512], onesm_sb[...], srcb[:, 0:512],
            start=first, stop=last),
            reads=["consts", stag], writes=["den_ps"], inc=False)
        S.emit("pe", lambda e, srcb=srcb, first=first, last=last: e.matmul(
            den_ps[:, 512:1024], onesm_sb[...], srcb[:, 512:1024],
            start=first, stop=last),
            reads=["consts", stag],
            writes=(["den_ps"] if last else []), inc=last)
        S.emit("pe", lambda e, vap=vap, srcb=srcb, first=first, last=last: e.matmul(
            ctx_ps[:, 0:512], vap, srcb[:, 0:512], start=first, stop=last),
            reads=ktags + [stag], writes=["ctx_ps"], inc=False)
        S.emit("pe", lambda e, vap=vap, srcb=srcb, first=first, last=last: e.matmul(
            ctx_ps[:, 512:1024], vap, srcb[:, 512:1024], start=first, stop=last),
            reads=ktags + [stag],
            writes=(["ctx_ps"] if last else []), inc=True)

    for j in range(NT):
        for hg in range(2):
            ntile = 8 * j + 8
            pending = []
            for i in range(ntile + 1):
                pending.append(emit_scores_exp(j, hg, i, ntile))
                if len(pending) > 2:
                    emit_den_ctx(pending.pop(0))
            for v in pending:
                emit_den_ctx(v)
            S.emit("dve", lambda e: e.reciprocal(rbc_sb[...], den_ps[...]),
                   reads=["den_ps"], writes=["rbc"])
            S.emit("dve", lambda e, j=j, hg=hg: e.tensor_tensor(
                ctx_all[:, hg * 8:(hg + 1) * 8, j * 128:(j + 1) * 128],
                ctx_ps[...].rearrange("p (h q) -> p h q", h=8),
                rbc_sb[...].rearrange("p (h q) -> p h q", h=8), ALU.mult),
                reads=["ctx_ps", "rbc"], writes=["ctx_all"])
            if j * 2 + hg >= 1:
                for _ in range(6):
                    if rank_units:
                        u = rank_units.pop(0)
                        if u[0] == "t":
                            rank_unit0(u[1])
                        else:
                            rank_unitk(u[1], u[2])

    S.dma("pool", lambda e: e.dma_start(out=wpA[:, 0:FC, :], in_=WTP[0]),
          "proj_wl0", writes=[("wp", 0)])
    S.dma("pool", lambda e: e.dma_start(out=wpB[:, 0:FC, :], in_=WTP[1]),
          "proj_wl1", writes=[("wp", 1)])
    S.dma("pool", lambda e: e.dma_start(out=sgA[...], in_=SWG[0]),
          "swg_ld0", writes=[("sg", 0)])
    S.dma("pool", lambda e: e.dma_start(out=suA[...], in_=SWU[0]),
          "swu_ld0", writes=[("su", 0)])
    S.dma("pool", lambda e: e.dma_start(out=sgB[...], in_=SWG[1]),
          "swg_ld1", writes=[("sg", 1)])
    S.dma("pool", lambda e: e.dma_start(out=suB[...], in_=SWU[1]),
          "swu_ld1", writes=[("su", 1)])

    while rank_units:
        u = rank_units.pop(0)
        if u[0] == "t":
            rank_unit0(u[1])
        else:
            rank_unitk(u[1], u[2])

    if debug:
        S.dma("sp", lambda e: e.dma_start(
            out=dbg["d_ctx"], in_=ctx_all[...].rearrange("p h t -> p (h t)")), "dbg",
            reads=["ctx_all"])

    if upto <= 4:
        return _finish()

    es_p4.close()
    es_r2.close()
    S.barrier("ffn_in", ("pe", "act", "dve", "sp"))

    es_p5 = ExitStack()
    g_psA = es_p5.enter_context(nc.psum_tensor("g_psA", [128, 512], F32))
    g_psB = es_p5.enter_context(nc.psum_tensor("g_psB", [128, 512], F32))
    u_psA = es_p5.enter_context(nc.psum_tensor("u_psA", [128, 512], F32))
    u_psB = es_p5.enter_context(nc.psum_tensor("u_psB", [128, 512], F32))
    tp_psA = es_p5.enter_context(nc.psum_tensor("tp_psA", [128, 1024], BF16))[:, 0:128]
    tp_psB = es_p5.enter_context(nc.psum_tensor("tp_psB", [128, 1024], BF16))[:, 0:128]

    es_r3a = ExitStack()
    stg_all = sb("stg_all", [128, NT, D], BF16, stack=es_r3a, side="right")
    es_r3c = ExitStack()
    bproj_sb = sb("bproj_sb", [1, D], F32, stack=es_r3c, side="right")
    bias_bc = sb("bias_bc", [128, D], F32, stack=es_r3c, side="right")

    S.dma("sp", lambda e: e.dma_start(out=bproj_sb[...], in_=BPROJ), "bp_ld",
          writes=["bproj"])
    for osl in range(4):
        ps, pt = (g_psA, ("g_ps", 0)) if osl % 2 == 0 else (g_psB, ("g_ps", 1))
        S.emit("pe", lambda e, ps=ps, osl=osl: e.matmul(
            ps[...], onesf_sb[...], bproj_sb[:, osl * 512:(osl + 1) * 512],
            start=True, stop=True), reads=["consts", "bproj"], writes=[pt])
        S.emit("act", lambda e, ps=ps, osl=osl: e.activation(
            bias_bc[:, osl * 512:(osl + 1) * 512], ps[...], AF.Copy, bias=0.0,
            scale=1.0), reads=[pt], writes=["bias_bc"])

    def fm_phase(name, noc, nic, slab_fn, rhs_fn, rtags, nth=2, prefetched=0):
        post_q = []

        def flush_post():
            fmsb, ft, oc, th = post_q.pop(0)
            for tt in range(4):
                tp, tpt = (tp_psA, ("tp", 0)) if tt % 2 == 0 else (tp_psB, ("tp", 1))
                ttg = th * 4 + tt
                S.emit("pe", lambda e, tp=tp, fmsb=fmsb, tt=tt: e.transpose(
                    tp[...], fmsb[:, tt * 128:(tt + 1) * 128], ident_sb[...]),
                    reads=[ft, "consts"], writes=[tpt])
                S.emit("act", lambda e, tp=tp, oc=oc, ttg=ttg: e.activation(
                    stg_all[:, ttg, oc * 128:(oc + 1) * 128], tp[...], AF.Copy,
                    bias=0.0, scale=1.0),
                    reads=[tpt], writes=[("stg", ttg, oc)])

        for oc in range(noc):
            wb, wt = (wpA, ("wp", 0)) if oc % 2 == 0 else (wpB, ("wp", 1))
            if oc >= prefetched:
                S.dma("pool", lambda e, oc=oc, wb=wb: slab_fn(e, wb, oc),
                      name + "_wl" + str(oc % 2), writes=[wt])
            for th in range(nth):
                par = (oc * nth + th) % 2
                ps = g_psA if par == 0 else g_psB
                pt = ("g_ps", par)
                for ic in range(nic):
                    S.emit("pe", lambda e, ps=ps, wb=wb, ic=ic, th=th: e.matmul(
                        ps[...], wb[:, ic, :], rhs_fn(ic, th),
                        start=(ic == 0), stop=(ic == nic - 1)),
                        reads=([wt] + rtags if ic in (0, nic - 1) else []),
                        writes=([pt] if ic in (0, nic - 1) else []),
                        inc=(ic == nic - 1))
                fmsb, ft = (fmsbA, ("fmsb", 0)) if par == 0 else (fmsbB, ("fmsb", 1))
                S.emit("dve", lambda e, fmsb=fmsb, ps=ps: e.tensor_copy(
                    fmsb[...], ps[...]), reads=[pt], writes=[ft])
                post_q.append((fmsb, ft, oc, th))
                if len(post_q) > 1:
                    flush_post()
        while post_q:
            flush_post()

    fm_phase("proj", OC, FC,
             lambda e, wb, oc: e.dma_start(out=wb[:, 0:FC, :], in_=WTP[oc]),
             lambda ic, th: ctx_all[:, ic, th * 512:(th + 1) * 512],
             ["ctx_all"], prefetched=2)
    for ttg in range(NT):
        if sub <= 3:
            break
        S.dma("pool", lambda e, ttg=ttg: e.dma_start(
            out=acc_dram[ttg * 128:(ttg + 1) * 128, :], in_=stg_all[:, ttg, :],
            max_dma_last_dim=2048),
            "acc_proj", reads=[("stg", ttg, oc2) for oc2 in range(OC)],
            writes=[("acc", ttg)])
        if sub <= 4:
            continue
        S.dma("pool", lambda e, ttg=ttg: e.dma_start(
            out=acc_dram[ttg * 128:(ttg + 1) * 128, :],
            in_=X[ttg * 128:(ttg + 1) * 128, :], accum_op=ALU.add,
            max_dma_last_dim=4096),
            "acc_x", reads=[("acc", ttg)], writes=[("acc", ttg)])
        S.dma("pool", lambda e, ttg=ttg: e.dma_start(
            out=acc_dram[ttg * 128:(ttg + 1) * 128, :], in_=bias_bc[...],
            accum_op=ALU.add, max_dma_last_dim=4096),
            "acc_b", reads=[("acc", ttg), "bias_bc"], writes=[("acc", ttg)])

    S.emit("pool", lambda e: e.memset(zero_u32[...], 0), writes=["zero_u32"])
    S.dma("pool", lambda e: e.dma_start(
        out=buf_in.rearrange("(p k) one -> p k one", p=128), in_=zero_u32[...]
        .rearrange("p (k one) -> p k one", one=1)),
        "bufz", reads=["zero_u32"], writes=["buf_in"])
    bnd = {}

    def mkreg(e):
        if "r" not in bnd:
            bnd["r"] = e.to_reg(E * CAP - 1)
            bnd["r2"] = e.to_reg(T - 1)
            bnd["r3"] = e.to_reg(2 * NC * CAP - 1)
    sc_tags = []
    for j in range(NT):
        for k in range(4):
            tg = ("bsc", j, k)
            sc_tags.append(tg)

            def scat(e, j=j, k=k):
                mkreg(e)
                return e.indirect_dma_start(
                    out=buf_in,
                    out_offset=bass.IndirectOffsetOnAxis(ap=idxs[:, j, k:k + 1], axis=0),
                    in_=rowid_sb[:, j:j + 1], in_offset=None,
                    bounds_check=bnd["r"], oob_is_err=False)
            S.dma("pool", scat, "buf_sc", reads=["idxs", "consts", "buf_in"],
                  writes=[tg])
    S.coll(lambda e: e.collective_compute("AllReduce", ALU.add,
           replica_groups=[list(range(NC))], ins=[buf_in.opt()], outs=[buf_out.opt()]),
           "cc_buf", reads=["buf_in"] + sc_tags, writes=["buf_out"])

    for g in range(EL * 8):
        def gmb(e, g=g):
            mkreg(e)
            return e.indirect_dma_start(
                out=mybuf[:, g:g + 1], out_offset=None,
                in_=buf_out,
                in_offset=bass.IndirectOffsetOnAxis(ap=bufidx_sb[:, g:g + 1], axis=0),
                bounds_check=bnd["r"], oob_is_err=False)
        S.dma("pool", gmb, "mybuf_g", reads=["buf_out", "consts"])
    S.group_commit("mybuf_g", ["mybuf"])
    if debug:
        S.dma("sp", lambda e: e.dma_start(
            out=dbg["d_w"], in_=w_all[...].rearrange("p j e -> p (j e)")), "dbg",
            reads=["w_all"])
        S.dma("sp", lambda e: e.dma_start(
            out=dbg["d_rank"], in_=rank_all[...].rearrange("p j e -> p (j e)")), "dbg",
            reads=["rank_all"])
        S.dma("sp", lambda e: e.dma_start(
            out=dbg["d_bc"], in_=bc_sb[...].rearrange("p j e -> p (j e)")), "dbg",
            reads=["bc_sb"])
        S.dma("sp", lambda e: e.dma_start(
            out=dbg["d_excl"], in_=excl_all[...].rearrange("p j e -> p (j e)")), "dbg",
            reads=["excl_all"])
        S.dma("sp", lambda e: e.dma_start(
            out=dbg["d_idx"], in_=idxs[...].rearrange("p j k -> p (j k)")), "dbg",
            reads=["idxs"])
        S.dma("sp", lambda e: e.dma_start(
            out=dbg["d_idx2"], in_=idxs2[...].rearrange("p j k -> p (j k)")), "dbg",
            reads=["idxs2"])
        S.dma("sp", lambda e: e.dma_start(
            out=dbg["d_w4"], in_=w4k[...].rearrange("p j k -> p (j k)")), "dbg",
            reads=["w4k"])
        S.dma("sp", lambda e: e.dma_start(out=dbg["d_buf"], in_=buf_out), "dbg",
              reads=["buf_out"])
    for g in range(EL * 8):
        xg, xt = (xgA, ("xg", 0)) if g % 2 == 0 else (xgB, ("xg", 1))

        def gxg(e, g=g, xg=xg):
            mkreg(e)
            return e.indirect_dma_start(
                out=xg[...], out_offset=None,
                in_=xnag_out,
                in_offset=bass.IndirectOffsetOnAxis(ap=mybuf[:, g:g + 1], axis=0),
                bounds_check=bnd["r2"], oob_is_err=False)
        S.dma("pool", gxg, "xg_g" + str(g % 2), reads=["mybuf", "xnag_out"],
              writes=[xt])
        S.dma("pool", lambda e, g=g, xg=xg: e.dma_start(
            out=xg_dram[g * 128:(g + 1) * 128, :], in_=xg[...]),
            "xg_st", reads=[xt], writes=["xg_dram"])



    es_ctx.close()
    if upto <= 5:
        return _finish()

    es_r3b = ExitStack()
    mid_sb = sb("mid_sb", [128, ICS, TL], BF16, stack=es_r3b, side="right")

    for ic in range(ICS):
        gb, gbt = (sgA, ("sg", 0)) if ic % 2 == 0 else (sgB, ("sg", 1))
        ub, ubt = (suA, ("su", 0)) if ic % 2 == 0 else (suB, ("su", 1))
        if ic >= 2:
            S.dma("pool", lambda e, ic=ic, gb=gb: e.dma_start(out=gb[...], in_=SWG[ic]),
                  "swg_ld" + str(ic % 2), writes=[gbt])
            S.dma("pool", lambda e, ic=ic, ub=ub: e.dma_start(out=ub[...], in_=SWU[ic]),
                  "swu_ld" + str(ic % 2), writes=[ubt])
        for th in range(2):
            gps, gpt = (g_psA, ("g_ps", 0)) if th == 0 else (g_psB, ("g_ps", 1))
            ups, upt = (u_psA, ("u_ps", 0)) if th == 0 else (u_psB, ("u_ps", 1))
            for fcc in range(FC):
                S.emit("pe", lambda e, gps=gps, gb=gb, fcc=fcc, th=th: e.matmul(
                    gps[...], gb[:, fcc, :], xn_fm[:, fcc, th * 512:(th + 1) * 512],
                    start=(fcc == 0), stop=(fcc == FC - 1)),
                    reads=([gbt, "xn_fm"] if fcc in (0, FC - 1) else []),
                    writes=([("g_ps", th)] if fcc in (0, FC - 1) else []),
                    inc=(fcc == FC - 1))
            for fcc in range(FC):
                S.emit("pe", lambda e, ups=ups, ub=ub, fcc=fcc, th=th: e.matmul(
                    ups[...], ub[:, fcc, :], xn_fm[:, fcc, th * 512:(th + 1) * 512],
                    start=(fcc == 0), stop=(fcc == FC - 1)),
                    reads=([ubt, "xn_fm"] if fcc in (0, FC - 1) else []),
                    writes=([("u_ps", th)] if fcc in (0, FC - 1) else []),
                    inc=(fcc == FC - 1))
            S.emit("act", lambda e, gps=gps: e.activation(sil_sb[...], gps[...],
                   AF.Silu, bias=ZCOL, scale=1.0),
                   reads=[gpt, "consts"], writes=["sil"])
            S.emit("dve", lambda e, ic=ic, th=th, ups=ups: e.tensor_tensor(
                mid_sb[:, ic, th * 512:(th + 1) * 512], sil_sb[...], ups[...],
                ALU.mult), reads=["sil", upt], writes=[("mid", ic)])

    fm_phase("shd", OC, ICS,
             lambda e, wb, oc: e.dma_start(out=wb[...], in_=SWD[oc]),
             lambda ic, th: mid_sb[:, ic, th * 512:(th + 1) * 512],
             [("mid", icx) for icx in range(ICS)])
    for ttg in range(NT):
        S.dma("pool", lambda e, ttg=ttg: e.dma_start(
            out=acc_dram[ttg * 128:(ttg + 1) * 128, :], in_=stg_all[:, ttg, :],
            accum_op=ALU.add, max_dma_last_dim=2048),
            "acc_sh", reads=[("stg", ttg, oc2) for oc2 in range(OC)] + [("acc", ttg)],
            writes=[("acc", ttg)])

    if upto <= 6:
        return _finish()

    es_r3b.close()
    es_r3c.close()
    S.barrier("exp_in")

    es_r4 = ExitStack()
    xgfmA = sb("xgfmA", [128, FC, 512], BF16, stack=es_r4, side="right")
    xgfmB = sb("xgfmB", [128, FC, 512], BF16, stack=es_r4, side="right")
    emid = sb("emid", [128, ICE, 512], BF16, stack=es_r4, side="right")
    wgA_ = sb("wgA_", [128, FC, 128], BF16, stack=es_r4, side="right")
    wgB_ = sb("wgB_", [128, FC, 128], BF16, stack=es_r4, side="right")
    wuA_ = sb("wuA_", [128, FC, 128], BF16, stack=es_r4, side="right")
    wuB_ = sb("wuB_", [128, FC, 128], BF16, stack=es_r4, side="right")
    wdA_ = sb("wdA_", [128, ICE, 128], BF16, stack=es_r4, side="right")
    wdB_ = sb("wdB_", [128, ICE, 128], BF16, stack=es_r4, side="right")

    if debug:
        S.dma("sp", lambda e: e.dma_start(out=dbg["d_xg"], in_=xg_dram), "dbg",
              reads=["xg_dram"])

    for el in range(EL):
        for sc in range(2):
            ch = el * 2 + sc
            xgfm = xgfmA if ch % 2 == 0 else xgfmB
            xft = ("xgfm", ch % 2)
            row0 = el * CAP + sc * 512
            for fcc in range(FC):
                S.dma("sp", lambda e, fcc=fcc, row0=row0, xgfm=xgfm: e.dma_start_transpose(
                    xgfm[:, fcc, :], xg_dram[row0:row0 + 512, fcc * 128:(fcc + 1) * 128]),
                    "xgfm_ld" + str(ch % 2), reads=["xg_dram"],
                    writes=([xft] if fcc == 0 else []))
            S.group_commit("xgfm_ld" + str(ch % 2), [xft])
            for ic in range(ICE):
                par = ic % 2
                gb, gbt = (wgA_, ("ewg", 0)) if par == 0 else (wgB_, ("ewg", 1))
                ub, ubt = (wuA_, ("ewu", 0)) if par == 0 else (wuB_, ("ewu", 1))
                S.dma("pool", lambda e, el=el, ic=ic, gb=gb: e.dma_start(
                    out=gb[...], in_=WG[el, ic]), "ewg_ld" + str(par), writes=[gbt])
                S.dma("pool", lambda e, el=el, ic=ic, ub=ub: e.dma_start(
                    out=ub[...], in_=WU[el, ic]), "ewu_ld" + str(par), writes=[ubt])
                gps, gpt = (g_psA, ("g_ps", 0)) if par == 0 else (g_psB, ("g_ps", 1))
                ups, upt = (u_psA, ("u_ps", 0)) if par == 0 else (u_psB, ("u_ps", 1))
                for fcc in range(FC):
                    S.emit("pe", lambda e, gps=gps, gb=gb, fcc=fcc, xgfm=xgfm: e.matmul(
                        gps[...], gb[:, fcc, :], xgfm[:, fcc, :],
                        start=(fcc == 0), stop=(fcc == FC - 1)),
                        reads=([gbt, xft] if fcc in (0, FC - 1) else []),
                        writes=([gpt] if fcc in (0, FC - 1) else []),
                        inc=(fcc == FC - 1))
                for fcc in range(FC):
                    S.emit("pe", lambda e, ups=ups, ub=ub, fcc=fcc, xgfm=xgfm: e.matmul(
                        ups[...], ub[:, fcc, :], xgfm[:, fcc, :],
                        start=(fcc == 0), stop=(fcc == FC - 1)),
                        reads=([ubt, xft] if fcc in (0, FC - 1) else []),
                        writes=([upt] if fcc in (0, FC - 1) else []),
                        inc=(fcc == FC - 1))
                S.emit("act", lambda e, gps=gps: e.activation(sil_sb[...], gps[...],
                       AF.Silu, bias=ZCOL, scale=1.0),
                       reads=[gpt, "consts"], writes=["sil"])
                S.emit("dve", lambda e, ic=ic, ups=ups: e.tensor_tensor(
                    emid[:, ic, :], sil_sb[...], ups[...], ALU.mult),
                    reads=["sil", upt], writes=[("emid", ic)])
            epost = []

            def eflush():
                fmsb, ft, oc = epost.pop(0)
                for tt in range(4):
                    tp, tpt = (tp_psA, ("tp", 0)) if tt % 2 == 0 else (tp_psB, ("tp", 1))
                    S.emit("pe", lambda e, tp=tp, fmsb=fmsb, tt=tt: e.transpose(
                        tp[...], fmsb[:, tt * 128:(tt + 1) * 128], ident_sb[...]),
                        reads=[ft, "consts"], writes=[tpt])
                    S.emit("act", lambda e, tp=tp, oc=oc, tt=tt: e.activation(
                        stg_all[:, tt, oc * 128:(oc + 1) * 128], tp[...], AF.Copy,
                        bias=0.0, scale=1.0),
                        reads=[tpt], writes=[("stg", tt, oc)])

            for oc in range(OC):
                par = oc % 2
                db, dbt = (wdA_, ("ewd", 0)) if par == 0 else (wdB_, ("ewd", 1))
                S.dma("pool", lambda e, el=el, oc=oc, db=db: e.dma_start(
                    out=db[...], in_=WD[el, oc]), "ewd_ld" + str(par), writes=[dbt])
                fps, fpt = (g_psA, ("g_ps", 0)) if par == 0 else (g_psB, ("g_ps", 1))
                for ic in range(ICE):
                    S.emit("pe", lambda e, fps=fps, db=db, ic=ic: e.matmul(
                        fps[...], db[:, ic, :], emid[:, ic, :],
                        start=(ic == 0), stop=(ic == ICE - 1)),
                        reads=([dbt] + [("emid", icx) for icx in range(ICE)]
                               if ic in (0, ICE - 1) else []),
                        writes=([fpt] if ic in (0, ICE - 1) else []),
                        inc=(ic == ICE - 1))
                fmsb, ft = (fmsbA, ("fmsb", 0)) if par == 0 else (fmsbB, ("fmsb", 1))
                S.emit("dve", lambda e, fmsb=fmsb, fps=fps: e.tensor_copy(
                    fmsb[...], fps[...]), reads=[fpt], writes=[ft])
                epost.append((fmsb, ft, oc))
                if len(epost) > 1:
                    eflush()
            while epost:
                eflush()
            for tt in range(4):
                S.dma("pool", lambda e, tt=tt, row0=row0: e.dma_start(
                    out=fout_in[row0 + tt * 128:row0 + (tt + 1) * 128, :],
                    in_=stg_all[:, tt, :]), "fout_st",
                    reads=[("stg", tt, oc2) for oc2 in range(OC)],
                    writes=[("fout_in", el)])
        S.coll(lambda e, el=el: e.collective_compute(
            "AllGather", ALU.bypass, replica_groups=[list(range(NC))],
            ins=[fout_in[el * CAP:(el + 1) * CAP, :].opt()],
            outs=[fout_ag2[el].opt()]),
            "cc_fout", reads=[("fout_in", el)], writes=[("fout_ag", el)])

    if debug:
        S.dma("sp", lambda e: e.dma_start(out=dbg["d_fout"], in_=fout_in), "dbg",
              reads=[("fout_in", 0), ("fout_in", 1)])
        S.dma("sp", lambda e: e.dma_start(
            out=dbg["d_fag"], in_=fout_ag2.rearrange("a r d -> (a r) d")), "dbg",
            reads=[("fout_ag", 0), ("fout_ag", 1)])
        S.dma("sp", lambda e: e.dma_start(out=dbg["d_acc"], in_=acc_dram), "dbg",
              reads=[("acc", ttg) for ttg in range(NT)])

    if upto <= 7:
        return _finish()

    es_r4.close()
    es_r3a.close()
    S.barrier("asm_in")

    es_r5 = ExitStack()
    moe_g = [[sb(f"moe_g{k}_{p}", [128, D], BF16, stack=es_r5, side="right")
              for p in range(2)] for k in range(4)]
    acc_sbA = sb("acc_sbA", [128, D], F32, stack=es_r5, side="right")
    acc_sbB = sb("acc_sbB", [128, D], F32, stack=es_r5, side="right")
    out_sbA = sb("out_sbA", [128, D], F32, stack=es_r5, side="right")
    out_sbB = sb("out_sbB", [128, D], F32, stack=es_r5, side="right")

    fout_flat = fout_ag2.rearrange("a r d -> (a r) d")
    for k in range(4):
        for p in range(2):
            S.emit("dve", lambda e, k=k, p=p: e.memset(moe_g[k][p][...], 0.0),
                   writes=[("moe", k, p)])

    for j in range(NT):
        p = j % 2
        acc, at = (acc_sbA, ("acc_sb", 0)) if p == 0 else (acc_sbB, ("acc_sb", 1))
        out, ot = (out_sbA, ("out_sb", 0)) if p == 0 else (out_sbB, ("out_sb", 1))
        S.dma("sp", lambda e, j=j, acc=acc: e.dma_start(
            out=acc[...], in_=acc_dram[j * 128:(j + 1) * 128, :]),
            "acc_ld" + str(p), reads=[("acc", j)], writes=[at])
        for k in range(4):
            def gmo(e, j=j, k=k, p=p):
                mkreg(e)
                return e.indirect_dma_start(
                    out=moe_g[k][p][...], out_offset=None,
                    in_=fout_flat,
                    in_offset=bass.IndirectOffsetOnAxis(ap=idxs2[:, j, k:k + 1], axis=0),
                    bounds_check=bnd["r3"], oob_is_err=False)
            S.dma("pool", gmo, "moe_g" + str(k) + str(p),
                  reads=[("fout_ag", 0), ("fout_ag", 1), "idxs2"],
                  writes=[("moe", k, p)])
        S.emit("dve", lambda e, j=j, acc=acc, out=out, p=p: e.scalar_tensor_tensor(
            out[...], moe_g[0][p][...], w4k[:, j, 0:1], acc[...],
            ALU.mult, ALU.add),
            reads=[("moe", 0, p), "w4k", at], writes=[ot])
        for k in range(1, 4):
            S.emit("dve", lambda e, j=j, k=k, out=out, p=p: e.scalar_tensor_tensor(
                out[...], moe_g[k][p][...], w4k[:, j, k:k + 1], out[...],
                ALU.mult, ALU.add),
                reads=[("moe", k, p), "w4k"], writes=[ot])
        S.dma("sp", lambda e, j=j, out=out: e.dma_start(
            out=Y[j * 128:(j + 1) * 128, :], in_=out[...]), "y_st",
            reads=[ot], writes=[("y", j)])
    S.final_wait("sp", "y_st")
    if debug:
        S.final_wait("sp", "dbg")

    S.verify()
    with nc.Block() as block:
        S.replay(block)
    es_r5.close()
    es_p5.close()
    es.close()
    return nc, dbg


def _host_prep(c, inputs):
    x = np.asarray(inputs["L0_attn_hidden_states"]).reshape(T, D)
    caw = np.asarray(inputs["L0_attn_c_attn_w"])
    cab = np.asarray(inputs["L0_attn_c_attn_b"])
    cpw = np.asarray(inputs["L0_attn_c_proj_w"])
    cpb = np.asarray(inputs["L0_attn_c_proj_b"])
    gw = np.asarray(inputs["L0_ffn_gate_w"])
    wg = np.asarray(inputs["L0_ffn_w_gate"])
    wu = np.asarray(inputs["L0_ffn_w_up"])
    wd = np.asarray(inputs["L0_ffn_w_down"])
    swg = np.asarray(inputs["L0_ffn_sw_gate"])
    swu = np.asarray(inputs["L0_ffn_sw_up"])
    swd = np.asarray(inputs["L0_ffn_sw_down"])

    blocks = np.arange(NT) * NC + c
    m = {}
    m["x"] = np.ascontiguousarray(x.reshape(64, 128, D)[blocks].reshape(TL, D))
    m["wta"] = np.ascontiguousarray(caw.reshape(OCQ, 128, FC, 128).transpose(0, 3, 2, 1))
    m["wtp"] = np.ascontiguousarray(cpw.reshape(OC, 128, FC, 128).transpose(0, 3, 2, 1))
    m["gt"] = np.ascontiguousarray(gw.reshape(E, FC, 128).transpose(2, 1, 0))
    m["battn"] = np.ascontiguousarray(cab.reshape(OCQ, 128).T)
    m["bproj"] = np.ascontiguousarray(cpb.reshape(1, D))
    m["swg"] = np.ascontiguousarray(swg.reshape(FC, 128, ICS, 128).transpose(2, 1, 0, 3))
    m["swu"] = np.ascontiguousarray(swu.reshape(FC, 128, ICS, 128).transpose(2, 1, 0, 3))
    m["swd"] = np.ascontiguousarray(swd.reshape(ICS, 128, OC, 128).transpose(2, 1, 0, 3))
    m["wg"] = np.ascontiguousarray(
        wg[2 * c:2 * c + 2].reshape(EL, FC, 128, ICE, 128).transpose(0, 3, 2, 1, 4))
    m["wu"] = np.ascontiguousarray(
        wu[2 * c:2 * c + 2].reshape(EL, FC, 128, ICE, 128).transpose(0, 3, 2, 1, 4))
    m["wd"] = np.ascontiguousarray(
        wd[2 * c:2 * c + 2].reshape(EL, ICE, 128, OC, 128).transpose(0, 3, 2, 1, 4))

    tri = np.triu(np.ones((128, 128), np.float32))
    m["tri"] = np.ascontiguousarray(np.tile(tri, (1, 8))).astype(ml_dtypes.bfloat16)
    m["ltri"] = np.ascontiguousarray(
        np.tril(np.ones((128, 128), np.float32), -1).T).astype(ml_dtypes.bfloat16)
    m["ident"] = np.eye(128, dtype=ml_dtypes.bfloat16)
    m["onesb"] = np.ones((128, 1), ml_dtypes.bfloat16)
    m["onesm"] = np.ones((128, 128), ml_dtypes.bfloat16)
    m["onesf"] = np.ones((1, 128), np.float32)

    cfv = np.zeros((128, NMAIN + 3), np.float32)
    for j in range(NT):
        for i in range(8 * j + 8):
            cfv[:, 4 * j * (j + 1) + i] = -12.0 if i < 8 * j + c else -112.0
    cfv[:, NMAIN] = -12.0
    cfv[:, NMAIN + 1] = 1e-5
    cfv[:, NMAIN + 2] = 0.0
    m["cf"] = cfv

    psm = np.zeros((64, NT), np.float32)
    for im in range(64):
        R, jj = im // NT, im % NT
        g = NC * jj + R
        for j in range(NT):
            if g < NC * j + c:
                psm[im, j] = 1.0
    m["psm"] = psm.astype(ml_dtypes.bfloat16)

    ejm = np.zeros((NT, NT * 128), np.float32)
    for j in range(NT):
        ejm[j, j * 128:(j + 1) * 128] = 1.0
    m["ejm"] = ejm
    m["iota16"] = np.tile(np.arange(E, dtype=np.float32), (128, 1))
    ev = np.arange(E)
    v16 = (ev % 2) * 8192 + (ev // 2) * 1024 - ev * 1024
    m["vals16"] = np.tile(v16.astype(np.float32), (128, 1))
    rid = np.zeros((128, NT), np.uint32)
    for j in range(NT):
        rid[:, j] = c * TL + j * 128 + np.arange(128)
    m["rowid"] = rid
    bix = np.zeros((128, EL * 8), np.uint32)
    for g in range(EL * 8):
        bix[:, g] = 2048 * c + g * 128 + np.arange(128)
    m["bufidx"] = bix
    return m


_CACHE = {}


def kernel(**inputs):
    if "nc" not in _CACHE:
        _CACHE["nc"] = build(debug=False)[0]
    nc = _CACHE["nc"]
    in_maps = [_host_prep(c, inputs) for c in range(NC)]
    res = run_bass_kernel_spmd(nc, in_maps, core_ids=list(range(NC)))
    out = np.zeros((1, T, D), np.float32)
    for c in range(NC):
        y = res.results[c]["y"]
        blocks = np.arange(NT) * NC + c
        out[0].reshape(64, 128, D)[blocks] = y.reshape(NT, 128, D)
    return out


# version 07
# speedup vs baseline: 1.0544x; 1.0544x over previous
import numpy as np
import ml_dtypes
from contextlib import ExitStack
import concourse.bass as bass
import concourse.mybir as mybir
from concourse.bass_utils import run_bass_kernel_spmd

F32 = mybir.dt.float32
BF16 = mybir.dt.bfloat16
U32 = mybir.dt.uint32
AF = mybir.ActivationFunctionType
ALU = mybir.AluOpType
AX = mybir.AxisListType

T = 8192
D = 2048
NC = 8
TL = 1024
NT = 8
FC = 16
NH = 16
HD = 128
OCQ = 18
E = 16
EL = 2
CAP = 1024
ICE = 11
ICS = 22
OC = 16
SCALE = 1.0 / float(np.sqrt(128.0))
NMAIN = 288

ENGINES = ("pe", "act", "dve", "pool", "sp")


class Sched:

    def __init__(self, nc, es):
        self.nc = nc
        self.es = es
        self.cmds = {e: [] for e in ENGINES}
        self.semh = {}
        self.semcnt = {}
        self.last_write = {}
        self.readers = {}
        self.waited = {e: {} for e in ENGINES}
        self.esem = {e: self.sem("S_" + e) for e in ENGINES}

    def sem(self, name):
        if name not in self.semh:
            self.semh[name] = self.es.enter_context(self.nc.semaphore(name))
            self.semcnt[name] = 0
        return name

    def _wait(self, eng, semname, val):
        if val <= 0 or self.waited[eng].get(semname, 0) >= val:
            return
        self.waited[eng][semname] = val
        self.cmds[eng].append(("wait", semname, val))

    def _dep_waits(self, eng, reads, writes):
        for tag in reads:
            lw = self.last_write.get(tag)
            if lw is not None:
                self._wait(eng, lw[0], lw[1])
        for tag in writes:
            lw = self.last_write.get(tag)
            if lw is not None:
                self._wait(eng, lw[0], lw[1])
            for (s, v) in self.readers.get(tag, ()):
                self._wait(eng, s, v)

    def _record(self, semname, val, reads, writes):
        for tag in writes:
            self.last_write[tag] = (semname, val)
            self.readers[tag] = []
        for tag in reads:
            self.readers.setdefault(tag, []).append((semname, val))

    def emit(self, eng, fn, reads=(), writes=(), inc=True):
        self._dep_waits(eng, reads, writes)
        if inc:
            s = self.esem[eng]
            self.semcnt[s] += 1
            self.cmds[eng].append(("emit", fn, s, 1))
            self._record(s, self.semcnt[s], reads, writes)
        else:
            self.cmds[eng].append(("emit", fn, None, 0))

    def dma(self, eng, fn, sem, reads=(), writes=()):
        self.sem(sem)
        self._dep_waits(eng, reads, writes)
        self.semcnt[sem] += 16
        self.cmds[eng].append(("emit", fn, sem, 16))
        self._record(sem, self.semcnt[sem], reads, writes)

    def coll(self, fn, sem, reads=(), writes=()):
        self.sem(sem)
        self._dep_waits("pool", reads, writes)
        self.semcnt[sem] += 1
        self.cmds["pool"].append(("emit", fn, sem, 1))
        self._record(sem, self.semcnt[sem], reads, writes)

    def group_commit(self, sem, tags):
        tot = self.semcnt[sem]
        for tag in tags:
            self.last_write[tag] = (sem, tot)
            self.readers[tag] = []

    def barrier(self, name, engines=ENGINES):
        s = self.sem("B_" + name)
        for e in engines:
            self.semcnt[s] += 1
            self.cmds[e].append(("emit", None, s, 1))
        tot = self.semcnt[s]
        for e in engines:
            self._wait(e, s, tot)

    def final_wait(self, eng, sem):
        self.cmds[eng].append(("wait", sem, self.semcnt[sem]))

    def verify(self):
        pos = {e: 0 for e in ENGINES}
        cnt = {s: 0 for s in self.semh}
        progress = True
        while progress:
            progress = False
            for e in ENGINES:
                while pos[e] < len(self.cmds[e]):
                    c = self.cmds[e][pos[e]]
                    if c[0] == "wait":
                        if cnt[c[1]] < c[2]:
                            break
                    elif c[2] is not None:
                        cnt[c[2]] += c[3]
                    pos[e] += 1
                    progress = True
        stuck = {}
        for e in ENGINES:
            if pos[e] < len(self.cmds[e]):
                c = self.cmds[e][pos[e]]
                stuck[e] = f"cmd {pos[e]}/{len(self.cmds[e])}: wait {c[1]}>={c[2]} (at {cnt.get(c[1])})"
        if stuck:
            raise RuntimeError(f"DEADLOCK in schedule: {stuck}")

    def replay(self, block):
        proxies = {"pe": block.tensor, "act": block.scalar, "dve": block.vector,
                   "pool": block.gpsimd, "sp": block.sync}

        def mk(cmds):
            def body(eng):
                for c in cmds:
                    if c[0] == "wait":
                        eng.wait_ge(self.semh[c[1]], c[2])
                    else:
                        _, fn, s, amt = c
                        if fn is None:
                            eng.nop().then_inc(self.semh[s], amt)
                        else:
                            inst = fn(eng)
                            if s is not None:
                                inst.then_inc(self.semh[s], amt)
            return body
        for e in ENGINES:
            proxies[e](mk(self.cmds[e]))


def build(debug=False, upto=99, sub=99):
    nc = bass.Bass("TRN2", target_bir_lowering=False, debug=False)
    es = ExitStack()

    X = nc.dram_tensor("x", [TL, D], F32, kind="ExternalInput").ap()
    WTA = nc.dram_tensor("wta", [OCQ, 128, FC, 128], F32, kind="ExternalInput").ap()
    WTP = nc.dram_tensor("wtp", [OC, 128, FC, 128], F32, kind="ExternalInput").ap()
    GT = nc.dram_tensor("gt", [128, FC, E], F32, kind="ExternalInput").ap()
    BATTN = nc.dram_tensor("battn", [128, OCQ], F32, kind="ExternalInput").ap()
    BPROJ = nc.dram_tensor("bproj", [1, D], F32, kind="ExternalInput").ap()
    SWG = nc.dram_tensor("swg", [ICS, 128, FC, 128], F32, kind="ExternalInput").ap()
    SWU = nc.dram_tensor("swu", [ICS, 128, FC, 128], F32, kind="ExternalInput").ap()
    SWD = nc.dram_tensor("swd", [OC, 128, ICS, 128], F32, kind="ExternalInput").ap()
    WG = nc.dram_tensor("wg", [EL, ICE, 128, FC, 128], F32, kind="ExternalInput").ap()
    WU = nc.dram_tensor("wu", [EL, ICE, 128, FC, 128], F32, kind="ExternalInput").ap()
    WD = nc.dram_tensor("wd", [EL, OC, 128, ICE, 128], F32, kind="ExternalInput").ap()
    TRI = nc.dram_tensor("tri", [128, 8 * 128], BF16, kind="ExternalInput").ap()
    LTRI = nc.dram_tensor("ltri", [128, 128], BF16, kind="ExternalInput").ap()
    IDENT = nc.dram_tensor("ident", [128, 128], BF16, kind="ExternalInput").ap()
    ONESB = nc.dram_tensor("onesb", [128, 1], BF16, kind="ExternalInput").ap()
    ONESM = nc.dram_tensor("onesm", [128, 128], BF16, kind="ExternalInput").ap()
    ONESF = nc.dram_tensor("onesf", [1, 128], F32, kind="ExternalInput").ap()
    CF = nc.dram_tensor("cf", [128, NMAIN + 3], F32, kind="ExternalInput").ap()
    PSM = nc.dram_tensor("psm", [64, NT], BF16, kind="ExternalInput").ap()
    EJM = nc.dram_tensor("ejm", [NT, NT * 128], F32, kind="ExternalInput").ap()
    IOTA16 = nc.dram_tensor("iota16", [128, E], F32, kind="ExternalInput").ap()
    VALS16 = nc.dram_tensor("vals16", [128, E], F32, kind="ExternalInput").ap()
    ROWID = nc.dram_tensor("rowid", [128, NT], U32, kind="ExternalInput").ap()
    BUFIDX = nc.dram_tensor("bufidx", [128, EL * 8], U32, kind="ExternalInput").ap()
    Y = nc.dram_tensor("y", [TL, D], F32, kind="ExternalOutput").ap()

    xnag_in = nc.dram_tensor("xnag_in", [TL, D], BF16).ap()
    xnag_out = nc.dram_tensor("xnag_out", [T, D], BF16).ap()
    kag_in = nc.dram_tensor("kag_in", [128, TL], BF16).ap()
    kag_out = nc.dram_tensor("kag_out", [NC * 128, TL], BF16).ap()
    vag_in = nc.dram_tensor("vag_in", [128, NT * HD], BF16).ap()
    vag_out = nc.dram_tensor("vag_out", [NC * 128, NT * HD], BF16).ap()
    cnt_in = nc.dram_tensor("cnt_in", [NT, E], F32).ap()
    cnt_out = nc.dram_tensor("cnt_out", [64, E], F32).ap()
    buf_in = nc.dram_tensor("buf_in", [E * CAP, 1], U32).ap()
    buf_out = nc.dram_tensor("buf_out", [E * CAP, 1], U32).ap()
    acc_dram = nc.dram_tensor("acc_dram", [TL, D], F32).ap()
    xg_dram = nc.dram_tensor("xg_dram", [EL * CAP, D], BF16).ap()
    fout_in = nc.dram_tensor("fout_in", [EL * CAP, D], BF16).ap()
    fout_ag2 = nc.dram_tensor("fout_ag2", [2, NC * CAP, D], BF16).ap()

    dbg = {}
    if debug:
        for nm, shp, dt in [
            ("d_cnt", [64, E], F32), ("d_w", [128, NT * E], F32),
            ("d_rank", [128, NT * E], F32), ("d_idx", [128, NT * 4], U32),
            ("d_w4", [128, NT * 4], F32), ("d_buf", [E * CAP, 1], U32),
            ("d_q", [128, NH * TL], BF16), ("d_k", [128, TL], BF16),
            ("d_ctx", [128, NH * TL], BF16), ("d_acc", [TL, D], F32),
            ("d_xg", [EL * CAP, D], BF16), ("d_fout", [EL * CAP, D], BF16),
            ("d_idx2", [128, NT * 4], U32), ("d_fag", [2 * NC * CAP, D], BF16),
            ("d_bc", [128, NT * E], F32), ("d_excl", [128, NT * E], F32),
        ]:
            dbg[nm] = nc.dram_tensor(nm, shp, dt, kind="ExternalOutput").ap()

    S = Sched(nc, es)

    def _finish():
        S.verify()
        with nc.Block() as block:
            S.replay(block)
        return nc, dbg

    def sb(name, shape, dt, stack=es, side=None):
        kw = {} if side is None else {"side": side}
        return stack.enter_context(nc.sbuf_tensor(name, shape, dt, **kw))

    tri_sb = sb("tri_sb", [128, 8 * 128], BF16)
    ltri_sb = sb("ltri_sb", [128, 128], BF16)
    ident_sb = sb("ident_sb", [128, 128], BF16)
    onesb_sb = sb("onesb_sb", [128, 1], BF16)
    onesm_sb = sb("onesm_sb", [128, 128], BF16)
    onesf_sb = sb("onesf_sb", [1, 128], F32)
    cf_sb = sb("cf_sb", [128, NMAIN + 3], F32)
    psm_sb = sb("psm_sb", [64, NT], BF16)
    ejm_sb = sb("ejm_sb", [NT, NT * 128], F32)
    iota_sb = sb("iota_sb", [128, E], F32)
    vals_sb = sb("vals_sb", [128, E], F32)
    rowid_sb = sb("rowid_sb", [128, NT], U32)
    bufidx_sb = sb("bufidx_sb", [128, EL * 8], U32)
    gt_sb = sb("gt_sb", [128, FC, E], BF16)
    battn_sb = sb("battn_sb", [128, OCQ], F32)
    xn_fm = sb("xn_fm", [128, FC, TL], BF16)
    w_all = sb("w_all", [128, NT, E], F32)
    excl_all = sb("excl_all", [128, NT, E], F32)
    rank_all = sb("rank_all", [128, NT, E], F32)
    e4f = sb("e4f", [128, NT, 4], F32)
    w4k = sb("w4k", [128, NT, 4], F32)
    idxs = sb("idxs", [128, NT, 4], U32)
    idxs2 = sb("idxs2", [128, NT, 4], U32)
    r_t2 = sb("r_t2", [128, E], F32)
    r_t3 = sb("r_t3", [128, 1], F32)
    cntstage = sb("cntstage", [1, NT, E], F32)
    cnts_sb = sb("cnts_sb", [64, E], BF16)
    base_sb = sb("base_sb", [NT, E], F32)
    bc_sb = sb("bc_sb", [128, NT, E], F32)
    stat_sb = sb("stat_sb", [128, 4, 6], F32)
    mv_sb = sb("mv_sb", [128, 2], F32)
    rstd_sb = sb("rstd_sb", [128, 1], F32)
    negmr_sb = sb("negmr_sb", [128, 1], F32)
    r_nmax = sb("r_nmax", [128, 1], F32)
    r_ex = sb("r_ex", [128, E], F32)
    r_s1 = sb("r_s1", [128, 1], F32)
    r_probs = sb("r_probs", [128, E], F32)
    r_m8 = sb("r_m8", [128, 8], F32)
    r_idx8 = sb("r_idx8", [128, 8], U32)
    r_sel = sb("r_sel", [128, E], F32)
    r_selb = sb("r_selb", [128, E], BF16)
    r_kept = sb("r_kept", [128, E], F32)
    r_brank = sb("r_brank", [128, E], F32)
    r_oh = sb("r_oh", [128, E], F32)
    r_wk = sb("r_wk", [128, E], F32)
    r_t1 = sb("r_t1", [128, 1], F32)
    zero_u32 = sb("zero_u32", [128, 128], U32)
    fmsbA = sb("fmsbA", [128, 512], BF16)
    fmsbB = sb("fmsbB", [128, 512], BF16)
    sil_sb = sb("sil_sb", [128, 512], BF16)
    wpA = sb("wpA", [128, ICS, 128], BF16)
    wpB = sb("wpB", [128, ICS, 128], BF16)
    sgA = sb("sgA", [128, FC, 128], BF16)
    sgB = sb("sgB", [128, FC, 128], BF16)
    suA = sb("suA", [128, FC, 128], BF16)
    suB = sb("suB", [128, FC, 128], BF16)
    mybuf = sb("mybuf", [128, EL * 8], U32)
    xgA = sb("xgA", [128, D], BF16)
    xgB = sb("xgB", [128, D], BF16)
    es_ctx = ExitStack()
    ctx_all = sb("ctx_all", [128, NH, TL], BF16, stack=es_ctx)

    es_r1 = ExitStack()
    x_all = sb("x_all", [128, NT, D], F32, stack=es_r1, side="right")
    xnstA = sb("xnstA", [128, D], BF16, stack=es_r1, side="right")
    xnstB = sb("xnstB", [128, D], BF16, stack=es_r1, side="right")

    EPS = cf_sb[:, NMAIN + 1:NMAIN + 2]
    ZCOL = cf_sb[:, NMAIN + 2:NMAIN + 3]
    DIAGB = cf_sb[:, NMAIN:NMAIN + 1]

    for t, src in [(tri_sb, TRI), (ltri_sb, LTRI), (ident_sb, IDENT),
                   (onesb_sb, ONESB), (onesm_sb, ONESM), (onesf_sb, ONESF), (cf_sb, CF),
                   (psm_sb, PSM), (ejm_sb, EJM), (iota_sb, IOTA16), (vals_sb, VALS16),
                   (rowid_sb, ROWID), (bufidx_sb, BUFIDX), (battn_sb, BATTN)]:
        S.dma("sp", lambda e, t=t, src=src: e.dma_start(out=t[...], in_=src), "c_ld")
    S.dma("pool", lambda e: e.dma_start(out=gt_sb[...], in_=GT), "c_ld")
    S.group_commit("c_ld", ["consts"])
    S.dma("sp", lambda e: e.dma_start(
        out=x_all[...], in_=X.rearrange("(j p) d -> p j d", p=128)), "x_ld",
        writes=["x_all"])

    es_p1 = ExitStack()
    lg_ps = es_p1.enter_context(nc.psum_tensor("lg_ps", [128, 512], F32))[:, 0:E]
    cs_ps = es_p1.enter_context(nc.psum_tensor("cs_ps", [128, 512], F32))[:, 0:E]
    cnt_ps = es_p1.enter_context(nc.psum_tensor("cnt_ps", [1, 512], F32))[:, 0:E]
    qkv_psA = es_p1.enter_context(nc.psum_tensor("qkv_psA", [128, 512], F32))
    qkv_psB = es_p1.enter_context(nc.psum_tensor("qkv_psB", [128, 512], F32))
    vt_ps = es_p1.enter_context(nc.psum_tensor("vt_ps", [128, 1024], BF16))[:, 0:128]
    base_ps = es_p1.enter_context(nc.psum_tensor("base_ps", [NT, 512], F32))[:, 0:E]
    bc_ps = es_p1.enter_context(nc.psum_tensor("bc_ps", [128, 512], F32))[:, 0:E]

    for j in range(NT):
        xst, xt = (xnstA, ("xnst", 0)) if j % 2 == 0 else (xnstB, ("xnst", 1))
        for cch in range(4):
            S.emit("dve", lambda e, j=j, cch=cch: e.bn_stats(
                stat_sb[:, cch, :], x_all[:, j, cch * 512:(cch + 1) * 512]),
                reads=["x_all"], writes=["stat"])
        S.emit("dve", lambda e: e.bn_aggr(mv_sb[...], stat_sb[...]),
               reads=["stat"], writes=["mv"])
        S.emit("act", lambda e: e.activation(rstd_sb[...], mv_sb[:, 1:2], AF.Ln,
               bias=EPS, scale=1.0), reads=["mv", "consts"], writes=["rstd"])
        S.emit("act", lambda e: e.activation(rstd_sb[...], rstd_sb[...], AF.Exp,
               bias=ZCOL, scale=-0.5), reads=["consts"], writes=["rstd"])
        S.emit("dve", lambda e: e.tensor_scalar(negmr_sb[...], mv_sb[:, 0:1],
               rstd_sb[...], -1.0, ALU.mult, ALU.mult),
               reads=["mv", "rstd"], writes=["negmr"])
        S.emit("dve", lambda e, xst=xst, j=j: e.tensor_scalar(xst[...], x_all[:, j, :],
               rstd_sb[...], negmr_sb[...], ALU.mult, ALU.add),
               reads=["x_all", "rstd", "negmr"], writes=[xt])
        S.dma("sp", lambda e, xst=xst, j=j: e.dma_start(
            out=xnag_in[j * 128:(j + 1) * 128, :], in_=xst[...]), "xn_st",
            reads=[xt], writes=["xnag_in"])
    for fcc in range(FC):
        S.dma("sp", lambda e, fcc=fcc: e.dma_start_transpose(
            xn_fm[:, fcc, :], xnag_in[:, fcc * 128:(fcc + 1) * 128]), "xnfm_ld",
            reads=["xnag_in"])
    S.group_commit("xnfm_ld", ["xn_fm"])

    for j in range(NT):
        for fcc in range(FC):
            S.emit("pe", lambda e, j=j, fcc=fcc: e.matmul(
                lg_ps[...], xn_fm[:, fcc, j * 128:(j + 1) * 128], gt_sb[:, fcc, :],
                start=(fcc == 0), stop=(fcc == FC - 1)),
                reads=(["xn_fm", "consts"] if fcc in (0, FC - 1) else []),
                writes=(["lg_ps"] if fcc in (0, FC - 1) else []),
                inc=(fcc == FC - 1))
        S.emit("dve", lambda e: e.tensor_reduce(r_nmax[...], lg_ps[...], AX.X, ALU.max,
               negate=True), reads=["lg_ps"], writes=["r_nmax"])
        S.emit("act", lambda e: e.activation(r_ex[...], lg_ps[...], AF.Exp,
               bias=r_nmax[...], scale=1.0), reads=["lg_ps", "r_nmax"], writes=["r_ex"])
        S.emit("dve", lambda e: e.tensor_reduce(r_s1[...], r_ex[...], AX.X, ALU.add),
               reads=["r_ex"], writes=["r_s1"])
        S.emit("dve", lambda e: e.reciprocal(r_s1[...], r_s1[...]), writes=["r_s1"])
        S.emit("dve", lambda e: e.tensor_scalar(r_probs[...], r_ex[...], r_s1[...],
               None, ALU.mult), reads=["r_ex", "r_s1"], writes=["r_probs"])
        S.emit("dve", lambda e: e.max(r_m8[...], r_probs[...]),
               reads=["r_probs"], writes=["r_m8"])
        S.emit("dve", lambda e: e.max_index(r_idx8[...], r_m8[...], r_probs[...]),
               reads=["r_m8", "r_probs"], writes=["r_idx8"])
        S.emit("dve", lambda e, j=j: e.tensor_copy(e4f[:, j, :], r_idx8[:, 0:4]),
               reads=["r_idx8"], writes=["e4f"])
        S.emit("dve", lambda e: e.tensor_copy(r_t1[...], r_m8[:, 3:4]),
               reads=["r_m8"], writes=["r_t1"])
        S.emit("dve", lambda e: e.tensor_scalar(r_sel[...], r_probs[...], r_t1[...],
               None, ALU.is_ge), reads=["r_probs", "r_t1"], writes=["r_sel"])
        S.emit("dve", lambda e: e.tensor_copy(r_selb[...], r_sel[...]),
               reads=["r_sel"], writes=["r_selb"])
        S.emit("dve", lambda e: e.tensor_reduce(r_t1[...], r_m8[:, 0:4], AX.X, ALU.add),
               reads=["r_m8"], writes=["r_t1"])
        S.emit("dve", lambda e: e.reciprocal(r_t1[...], r_t1[...]), writes=["r_t1"])
        S.emit("dve", lambda e, j=j: e.scalar_tensor_tensor(w_all[:, j, :], r_probs[...],
               r_t1[...], r_sel[...], ALU.mult, ALU.mult),
               reads=["r_probs", "r_t1", "r_sel"], writes=["w_all"])
        S.emit("pe", lambda e: e.matmul(cs_ps[...], ltri_sb[...], r_selb[...],
               start=True, stop=True), reads=["consts", "r_selb"], writes=["cs_ps"])
        S.emit("pe", lambda e: e.matmul(cnt_ps[...], onesb_sb[...], r_selb[...],
               start=True, stop=True), reads=["consts", "r_selb"], writes=["cnt_ps"])
        S.emit("dve", lambda e, j=j: e.tensor_copy(excl_all[:, j, :], cs_ps[...]),
               reads=["cs_ps"], writes=["excl_all"])
        S.emit("dve", lambda e, j=j: e.tensor_copy(cntstage[:, j, :], cnt_ps[...]),
               reads=["cnt_ps"], writes=["cntstage"])
    S.dma("pool", lambda e: e.dma_start(out=cnt_in.rearrange("(o j) e -> o j e", o=1),
          in_=cntstage[...]), "cnt_st",
          reads=["cntstage"], writes=["cnt_in"])
    S.coll(lambda e: e.collective_compute("AllGather", ALU.bypass,
           replica_groups=[list(range(NC))], ins=[cnt_in.opt()], outs=[cnt_out.opt()]),
           "cc_cnt", reads=["cnt_in"], writes=["cnt_out"])
    S.coll(lambda e: e.collective_compute("AllGather", ALU.bypass,
           replica_groups=[list(range(NC))], ins=[xnag_in.opt()], outs=[xnag_out.opt()]),
           "cc_xn", reads=["xnag_in"], writes=["xnag_out"])

    if upto <= 1:
        return _finish()

    es_r1.close()
    es_r2 = ExitStack()
    Q_all = sb("Q_all", [128, NH, TL], BF16, stack=es_r2, side="right")
    K_own = sb("K_own", [128, TL], BF16, stack=es_r2, side="right")
    V_own = sb("V_own", [128, NT, HD], BF16, stack=es_r2, side="right")
    wbufA = sb("wbufA", [128, FC, 128], BF16, stack=es_r2, side="right")
    wbufB = sb("wbufB", [128, FC, 128], BF16, stack=es_r2, side="right")
    K_sb = sb("K_sb", [128, NC, TL], BF16, stack=es_r2, side="right")
    V_sb = sb("V_sb", [128, NC, NT * HD], BF16, stack=es_r2, side="right")
    E0 = sb("E0", [128, 1024], BF16, stack=es_r2, side="right")
    E1 = sb("E1", [128, 1024], BF16, stack=es_r2, side="right")
    E2 = sb("E2", [128, 1024], BF16, stack=es_r2, side="right")
    Ed = sb("Ed", [128, 1024], BF16, stack=es_r2, side="right")
    rbc_sb = sb("rbc_sb", [128, 1024], F32, stack=es_r2, side="right")
    es_r2b = ExitStack()
    V_fm_stage = sb("V_fm_stage", [128, TL], BF16, stack=es_r2b, side="right")

    for oc in range(OCQ):
        wb, wt = (wbufA, ("wq", 0)) if oc % 2 == 0 else (wbufB, ("wq", 1))
        S.dma("pool", lambda e, oc=oc, wb=wb: e.dma_start(out=wb[...], in_=WTA[oc]),
              "wq_ld" + str(oc % 2), writes=[wt])
        for th in range(2):
            par = (oc * 2 + th) % 2
            ps = qkv_psA if par == 0 else qkv_psB
            pt = ("qkv_ps", par)
            for fcc in range(FC):
                S.emit("pe", lambda e, th=th, fcc=fcc, ps=ps, wb=wb: e.matmul(
                    ps[...], wb[:, fcc, :], xn_fm[:, fcc, th * 512:(th + 1) * 512],
                    start=(fcc == 0), stop=(fcc == FC - 1)),
                    reads=([wt, "xn_fm"] if fcc in (0, FC - 1) else []),
                    writes=([pt] if fcc in (0, FC - 1) else []),
                    inc=(fcc == FC - 1))
            if oc < NH:
                dsts, dtag = (lambda oc=oc, th=th: Q_all[:, oc, th * 512:(th + 1) * 512]), "Q_all"
            elif oc == NH:
                dsts, dtag = (lambda oc=oc, th=th: K_own[:, th * 512:(th + 1) * 512]), "K_own"
            else:
                dsts, dtag = (lambda oc=oc, th=th: V_fm_stage[:, th * 512:(th + 1) * 512]), "V_fm"
            S.emit("dve", lambda e, dsts=dsts, ps=ps, oc=oc: e.tensor_scalar(
                dsts(), ps[...], battn_sb[:, oc:oc + 1], None, ALU.add),
                reads=[pt, "consts"], writes=[dtag])
    for j in range(NT):
        S.emit("pe", lambda e, j=j: e.transpose(
            vt_ps[...], V_fm_stage[:, j * 128:(j + 1) * 128], ident_sb[...]),
            reads=["V_fm", "consts"], writes=["vt_ps"])
        S.emit("dve", lambda e, j=j: e.tensor_copy(V_own[:, j, :], vt_ps[...]),
               reads=["vt_ps"], writes=["V_own"])
    S.dma("pool", lambda e: e.dma_start(out=kag_in, in_=K_own[...]), "kv_st",
          reads=["K_own"], writes=["kag_in"])
    S.dma("pool", lambda e: e.dma_start(
        out=vag_in, in_=V_own[...].rearrange("p j d -> p (j d)")), "kv_st",
        reads=["V_own"], writes=["vag_in"])
    S.coll(lambda e: e.collective_compute("AllGather", ALU.bypass,
           replica_groups=[list(range(NC))], ins=[kag_in.opt()], outs=[kag_out.opt()]),
           "cc_k", reads=["kag_in"], writes=["kag_out"])
    S.coll(lambda e: e.collective_compute("AllGather", ALU.bypass,
           replica_groups=[list(range(NC))], ins=[vag_in.opt()], outs=[vag_out.opt()]),
           "cc_v", reads=["vag_in"], writes=["vag_out"])
    for r in range(NC):
        S.dma("sp", lambda e, r=r: e.dma_start(
            out=K_sb[:, r, :], in_=kag_out[r * 128:(r + 1) * 128, :]), "ksb_ld",
            reads=["kag_out"])
        S.dma("sp", lambda e, r=r: e.dma_start(
            out=V_sb[:, r, :], in_=vag_out.rearrange("(r p) f -> r p f", p=128)[r]),
            "vsb_ld", reads=["vag_out"])
    S.group_commit("ksb_ld", ["K_sb"])
    S.group_commit("vsb_ld", ["V_sb"])

    if upto <= 2:
        return _finish()

    S.dma("pool", lambda e: e.dma_start(out=cnts_sb[...], in_=cnt_out), "cnts_ld",
          reads=["cnt_out"], writes=["cnts_sb"])
    S.emit("pe", lambda e: e.matmul(base_ps[...], psm_sb[...], cnts_sb[...],
           start=True, stop=True), reads=["consts", "cnts_sb"], writes=["base_ps"])
    S.emit("dve", lambda e: e.tensor_copy(base_sb[...], base_ps[...]),
           reads=["base_ps"], writes=["base_sb"])
    for j in range(NT):
        S.emit("pe", lambda e, j=j: e.matmul(bc_ps[...], ejm_sb[:, j * 128:(j + 1) * 128],
               base_sb[...], start=True, stop=True),
               reads=["consts", "base_sb"], writes=["bc_ps"])
        S.emit("dve", lambda e, j=j: e.tensor_copy(bc_sb[:, j, :], bc_ps[...]),
               reads=["bc_ps"], writes=["bc_sb"])
    def rank_unit0(j):
        S.emit("dve", lambda e, j=j: e.tensor_tensor(rank_all[:, j, :],
               excl_all[:, j, :], bc_sb[:, j, :], ALU.add),
               reads=["excl_all", "bc_sb"], writes=["rank_all"])
        S.emit("dve", lambda e, j=j: e.tensor_scalar(r_kept[...], rank_all[:, j, :],
               1023.5, None, ALU.is_le), reads=["rank_all"], writes=["r_kept"])
        S.emit("dve", lambda e, j=j: e.scalar_tensor_tensor(r_brank[...],
               r_kept[...], -1.0e6, rank_all[:, j, :], ALU.mult, ALU.add),
               reads=["r_kept", "rank_all"], writes=["r_brank"])
        S.emit("dve", lambda e: e.tensor_scalar(r_brank[...], r_brank[...], 1.0e6,
               None, ALU.add), writes=["r_brank"])
        S.emit("dve", lambda e, j=j: e.tensor_tensor(r_wk[...], w_all[:, j, :],
               r_kept[...], ALU.mult), reads=["w_all", "r_kept"],
               writes=["r_wk"])

    def rank_unitk(j, k):
        S.emit("dve", lambda e, j=j, k=k: e.tensor_scalar(r_oh[...], iota_sb[...],
               e4f[:, j, k:k + 1], None, ALU.is_equal),
               reads=["consts", "e4f"], writes=["r_oh"])
        S.emit("dve", lambda e: e.tensor_tensor(r_ex[...], r_oh[...], r_wk[...],
               ALU.mult), reads=["r_oh", "r_wk"], writes=["r_ex"])
        S.emit("dve", lambda e, j=j, k=k: e.tensor_reduce(w4k[:, j, k:k + 1],
               r_ex[...], AX.X, ALU.add), reads=["r_ex"], writes=["w4k"])
        S.emit("dve", lambda e: e.tensor_tensor(r_ex[...], r_oh[...], r_brank[...],
               ALU.mult), reads=["r_oh", "r_brank"], writes=["r_ex"])
        S.emit("dve", lambda e: e.tensor_reduce(r_t1[...], r_ex[...], AX.X, ALU.add),
               reads=["r_ex"], writes=["r_t1"])
        S.emit("dve", lambda e, j=j, k=k: e.scalar_tensor_tensor(r_t1[...],
               e4f[:, j, k:k + 1], 1024.0, r_t1[...], ALU.mult, ALU.add),
               reads=["e4f"], writes=["r_t1"])
        S.emit("dve", lambda e: e.tensor_scalar(r_t1[...], r_t1[...], 0.25,
               None, ALU.add), writes=["r_t1"])
        S.emit("dve", lambda e, j=j, k=k: e.tensor_copy(idxs[:, j, k:k + 1],
               r_t1[...]), reads=["r_t1"], writes=["idxs"])
        S.emit("dve", lambda e: e.tensor_tensor(r_t2[...], r_oh[...],
               vals_sb[...], ALU.mult), reads=["r_oh", "consts"],
               writes=["r_t2x"])
        S.emit("dve", lambda e: e.tensor_reduce(r_t3[...], r_t2[...], AX.X,
               ALU.add), reads=["r_t2x"], writes=["r_t3"])
        S.emit("dve", lambda e, j=j, k=k: e.tensor_tensor(
               idxs2[:, j, k:k + 1], r_t3[...], r_t1[...], ALU.add),
               reads=["r_t3", "r_t1"], writes=["idxs2"])

    rank_units = []
    for j in range(NT):
        rank_units.append(("t", j, 0))
        for k in range(4):
            rank_units.append(("k", j, k))

    if debug:
        S.dma("sp", lambda e: e.dma_start(out=dbg["d_cnt"], in_=cnt_out), "dbg",
              reads=["cnt_out"])
        S.dma("sp", lambda e: e.dma_start(
            out=dbg["d_q"], in_=Q_all[...].rearrange("p h t -> p (h t)")), "dbg",
            reads=["Q_all"])
        S.dma("sp", lambda e: e.dma_start(out=dbg["d_k"], in_=K_own[...]), "dbg",
              reads=["K_own"])

    if upto <= 3:
        return _finish()

    es_p1.close()
    es_r2b.close()
    S.barrier("attn_in", ("pe", "act", "dve"))

    es_p4 = ExitStack()
    s_psA = es_p4.enter_context(nc.psum_tensor("s_psA", [128, 1024], F32))
    s_psB = es_p4.enter_context(nc.psum_tensor("s_psB", [128, 1024], F32))
    ctx_ps = es_p4.enter_context(nc.psum_tensor("ctx_ps", [128, 1024], F32))
    den_ps = es_p4.enter_context(nc.psum_tensor("den_ps", [128, 1024], F32))

    def colof(j, i):
        return 4 * j * (j + 1) + i

    def emit_scores_exp(j, hg, i, ntile):
        diag = (i == ntile)
        par = i % 2
        sp_ = s_psA if par == 0 else s_psB
        spt = ("s_ps", par)
        Ebuf = (E0, E1, E2)[i % 3]
        Et = ("E", i % 3)
        if diag:
            kap = K_own[:, j * 128:(j + 1) * 128]
            vap = V_own[:, j, :]
            ktags = ["K_own", "V_own"]
            biasap = DIAGB
        else:
            R, J = i % 8, i // 8
            kap = K_sb[:, R, J * 128:(J + 1) * 128]
            vap = V_sb[:, R, J * HD:(J + 1) * HD]
            ktags = ["K_sb", "V_sb"]
            biasap = cf_sb[:, colof(j, i):colof(j, i) + 1]
        S.emit("pe", lambda e, kap=kap, j=j, hg=hg, sp_=sp_: e.matmul(
            sp_[:, 0:512], kap,
            Q_all[:, hg * 8:hg * 8 + 4, j * 128:(j + 1) * 128],
            start=True, stop=True),
            reads=ktags + ["Q_all"], writes=[spt], inc=False)
        S.emit("pe", lambda e, kap=kap, j=j, hg=hg, sp_=sp_: e.matmul(
            sp_[:, 512:1024], kap,
            Q_all[:, hg * 8 + 4:hg * 8 + 8, j * 128:(j + 1) * 128],
            start=True, stop=True),
            reads=ktags + ["Q_all"], writes=[spt])
        S.emit("act", lambda e, sp_=sp_, Ebuf=Ebuf, biasap=biasap: e.activation(
            Ebuf[...], sp_[...], AF.Exp, bias=biasap, scale=SCALE),
            reads=[spt, "consts"], writes=[Et])
        srcb, stag = Ebuf, Et
        if diag:
            S.emit("dve", lambda e, Ebuf=Ebuf: e.tensor_tensor(
                Ed[...], Ebuf[...], tri_sb[...], ALU.mult),
                reads=[Et, "consts"], writes=["Ed"])
            srcb, stag = Ed, "Ed"
        return (srcb, stag, vap, ktags, i == 0, diag)

    def emit_den_ctx(v):
        srcb, stag, vap, ktags, first, last = v
        S.emit("pe", lambda e, srcb=srcb, first=first, last=last: e.matmul(
            den_ps[:, 0:512], onesm_sb[...], srcb[:, 0:512],
            start=first, stop=last),
            reads=["consts", stag], writes=["den_ps"], inc=False)
        S.emit("pe", lambda e, srcb=srcb, first=first, last=last: e.matmul(
            den_ps[:, 512:1024], onesm_sb[...], srcb[:, 512:1024],
            start=first, stop=last),
            reads=["consts", stag],
            writes=(["den_ps"] if last else []), inc=last)
        S.emit("pe", lambda e, vap=vap, srcb=srcb, first=first, last=last: e.matmul(
            ctx_ps[:, 0:512], vap, srcb[:, 0:512], start=first, stop=last),
            reads=ktags + [stag], writes=["ctx_ps"], inc=False)
        S.emit("pe", lambda e, vap=vap, srcb=srcb, first=first, last=last: e.matmul(
            ctx_ps[:, 512:1024], vap, srcb[:, 512:1024], start=first, stop=last),
            reads=ktags + [stag],
            writes=(["ctx_ps"] if last else []), inc=True)

    for j in range(NT):
        for hg in range(2):
            ntile = 8 * j + 8
            pending = []
            for i in range(ntile + 1):
                pending.append(emit_scores_exp(j, hg, i, ntile))
                if len(pending) > 2:
                    emit_den_ctx(pending.pop(0))
            for v in pending:
                emit_den_ctx(v)
            S.emit("dve", lambda e: e.reciprocal(rbc_sb[...], den_ps[...]),
                   reads=["den_ps"], writes=["rbc"])
            S.emit("dve", lambda e, j=j, hg=hg: e.tensor_tensor(
                ctx_all[:, hg * 8:(hg + 1) * 8, j * 128:(j + 1) * 128],
                ctx_ps[...].rearrange("p (h q) -> p h q", h=8),
                rbc_sb[...].rearrange("p (h q) -> p h q", h=8), ALU.mult),
                reads=["ctx_ps", "rbc"], writes=["ctx_all"])
            if j * 2 + hg >= 1:
                for _ in range(6):
                    if rank_units:
                        u = rank_units.pop(0)
                        if u[0] == "t":
                            rank_unit0(u[1])
                        else:
                            rank_unitk(u[1], u[2])

    S.dma("pool", lambda e: e.dma_start(out=wpA[:, 0:FC, :], in_=WTP[0]),
          "proj_wl0", writes=[("wp", 0)])
    S.dma("pool", lambda e: e.dma_start(out=wpB[:, 0:FC, :], in_=WTP[1]),
          "proj_wl1", writes=[("wp", 1)])
    S.dma("pool", lambda e: e.dma_start(out=sgA[...], in_=SWG[0]),
          "swg_ld0", writes=[("sg", 0)])
    S.dma("pool", lambda e: e.dma_start(out=suA[...], in_=SWU[0]),
          "swu_ld0", writes=[("su", 0)])
    S.dma("pool", lambda e: e.dma_start(out=sgB[...], in_=SWG[1]),
          "swg_ld1", writes=[("sg", 1)])
    S.dma("pool", lambda e: e.dma_start(out=suB[...], in_=SWU[1]),
          "swu_ld1", writes=[("su", 1)])

    while rank_units:
        u = rank_units.pop(0)
        if u[0] == "t":
            rank_unit0(u[1])
        else:
            rank_unitk(u[1], u[2])

    S.emit("pool", lambda e: e.memset(zero_u32[...], 0), writes=["zero_u32"])
    S.dma("pool", lambda e: e.dma_start(
        out=buf_in.rearrange("(p k) one -> p k one", p=128), in_=zero_u32[...]
        .rearrange("p (k one) -> p k one", one=1)),
        "bufz", reads=["zero_u32"], writes=["buf_in"])
    bnd = {}

    def mkreg(e):
        if "r" not in bnd:
            bnd["r"] = e.to_reg(E * CAP - 1)
            bnd["r2"] = e.to_reg(T - 1)
            bnd["r3"] = e.to_reg(2 * NC * CAP - 1)
    sc_tags = []
    for j in range(NT):
        for k in range(4):
            tg = ("bsc", j, k)
            sc_tags.append(tg)

            def scat(e, j=j, k=k):
                mkreg(e)
                return e.indirect_dma_start(
                    out=buf_in,
                    out_offset=bass.IndirectOffsetOnAxis(ap=idxs[:, j, k:k + 1], axis=0),
                    in_=rowid_sb[:, j:j + 1], in_offset=None,
                    bounds_check=bnd["r"], oob_is_err=False)
            S.dma("pool", scat, "buf_sc", reads=["idxs", "consts", "buf_in"],
                  writes=[tg])
    S.coll(lambda e: e.collective_compute("AllReduce", ALU.add,
           replica_groups=[list(range(NC))], ins=[buf_in.opt()], outs=[buf_out.opt()]),
           "cc_buf", reads=["buf_in"] + sc_tags, writes=["buf_out"])

    for g in range(EL * 8):
        def gmb(e, g=g):
            mkreg(e)
            return e.indirect_dma_start(
                out=mybuf[:, g:g + 1], out_offset=None,
                in_=buf_out,
                in_offset=bass.IndirectOffsetOnAxis(ap=bufidx_sb[:, g:g + 1], axis=0),
                bounds_check=bnd["r"], oob_is_err=False)
        S.dma("pool", gmb, "mybuf_g", reads=["buf_out", "consts"])
    S.group_commit("mybuf_g", ["mybuf"])
    if debug:
        S.dma("sp", lambda e: e.dma_start(
            out=dbg["d_w"], in_=w_all[...].rearrange("p j e -> p (j e)")), "dbg",
            reads=["w_all"])
        S.dma("sp", lambda e: e.dma_start(
            out=dbg["d_rank"], in_=rank_all[...].rearrange("p j e -> p (j e)")), "dbg",
            reads=["rank_all"])
        S.dma("sp", lambda e: e.dma_start(
            out=dbg["d_bc"], in_=bc_sb[...].rearrange("p j e -> p (j e)")), "dbg",
            reads=["bc_sb"])
        S.dma("sp", lambda e: e.dma_start(
            out=dbg["d_excl"], in_=excl_all[...].rearrange("p j e -> p (j e)")), "dbg",
            reads=["excl_all"])
        S.dma("sp", lambda e: e.dma_start(
            out=dbg["d_idx"], in_=idxs[...].rearrange("p j k -> p (j k)")), "dbg",
            reads=["idxs"])
        S.dma("sp", lambda e: e.dma_start(
            out=dbg["d_idx2"], in_=idxs2[...].rearrange("p j k -> p (j k)")), "dbg",
            reads=["idxs2"])
        S.dma("sp", lambda e: e.dma_start(
            out=dbg["d_w4"], in_=w4k[...].rearrange("p j k -> p (j k)")), "dbg",
            reads=["w4k"])
        S.dma("sp", lambda e: e.dma_start(out=dbg["d_buf"], in_=buf_out), "dbg",
              reads=["buf_out"])
    for g in range(EL * 8):
        xg, xt = (xgA, ("xg", 0)) if g % 2 == 0 else (xgB, ("xg", 1))

        def gxg(e, g=g, xg=xg):
            mkreg(e)
            return e.indirect_dma_start(
                out=xg[...], out_offset=None,
                in_=xnag_out,
                in_offset=bass.IndirectOffsetOnAxis(ap=mybuf[:, g:g + 1], axis=0),
                bounds_check=bnd["r2"], oob_is_err=False)
        S.dma("pool", gxg, "xg_g" + str(g % 2), reads=["mybuf", "xnag_out"],
              writes=[xt])
        S.dma("pool", lambda e, g=g, xg=xg: e.dma_start(
            out=xg_dram[g * 128:(g + 1) * 128, :], in_=xg[...]),
            "xg_st", reads=[xt], writes=["xg_dram"])




    if debug:
        S.dma("sp", lambda e: e.dma_start(
            out=dbg["d_ctx"], in_=ctx_all[...].rearrange("p h t -> p (h t)")), "dbg",
            reads=["ctx_all"])

    if upto <= 4:
        return _finish()

    es_p4.close()
    es_r2.close()
    S.barrier("ffn_in", ("pe", "act", "dve", "sp"))

    es_p5 = ExitStack()
    g_psA = es_p5.enter_context(nc.psum_tensor("g_psA", [128, 512], F32))
    g_psB = es_p5.enter_context(nc.psum_tensor("g_psB", [128, 512], F32))
    u_psA = es_p5.enter_context(nc.psum_tensor("u_psA", [128, 512], F32))
    u_psB = es_p5.enter_context(nc.psum_tensor("u_psB", [128, 512], F32))
    tp_psA = es_p5.enter_context(nc.psum_tensor("tp_psA", [128, 1024], BF16))[:, 0:128]
    tp_psB = es_p5.enter_context(nc.psum_tensor("tp_psB", [128, 1024], BF16))[:, 0:128]

    es_r3a = ExitStack()
    stg_all = sb("stg_all", [128, NT, D], BF16, stack=es_r3a, side="right")
    es_r3c = ExitStack()
    bproj_sb = sb("bproj_sb", [1, D], F32, stack=es_r3c, side="right")
    bias_bc = sb("bias_bc", [128, D], F32, stack=es_r3c, side="right")

    S.dma("sp", lambda e: e.dma_start(out=bproj_sb[...], in_=BPROJ), "bp_ld",
          writes=["bproj"])
    for osl in range(4):
        ps, pt = (g_psA, ("g_ps", 0)) if osl % 2 == 0 else (g_psB, ("g_ps", 1))
        S.emit("pe", lambda e, ps=ps, osl=osl: e.matmul(
            ps[...], onesf_sb[...], bproj_sb[:, osl * 512:(osl + 1) * 512],
            start=True, stop=True), reads=["consts", "bproj"], writes=[pt])
        S.emit("act", lambda e, ps=ps, osl=osl: e.activation(
            bias_bc[:, osl * 512:(osl + 1) * 512], ps[...], AF.Copy, bias=0.0,
            scale=1.0), reads=[pt], writes=["bias_bc"])

    def fm_phase(name, noc, nic, slab_fn, rhs_fn, rtags, nth=2, prefetched=0):
        post_q = []

        def flush_post():
            fmsb, ft, oc, th = post_q.pop(0)
            for tt in range(4):
                tp, tpt = (tp_psA, ("tp", 0)) if tt % 2 == 0 else (tp_psB, ("tp", 1))
                ttg = th * 4 + tt
                S.emit("pe", lambda e, tp=tp, fmsb=fmsb, tt=tt: e.transpose(
                    tp[...], fmsb[:, tt * 128:(tt + 1) * 128], ident_sb[...]),
                    reads=[ft, "consts"], writes=[tpt])
                S.emit("act", lambda e, tp=tp, oc=oc, ttg=ttg: e.activation(
                    stg_all[:, ttg, oc * 128:(oc + 1) * 128], tp[...], AF.Copy,
                    bias=0.0, scale=1.0),
                    reads=[tpt], writes=[("stg", ttg, oc)])

        for oc in range(noc):
            wb, wt = (wpA, ("wp", 0)) if oc % 2 == 0 else (wpB, ("wp", 1))
            if oc >= prefetched:
                S.dma("pool", lambda e, oc=oc, wb=wb: slab_fn(e, wb, oc),
                      name + "_wl" + str(oc % 2), writes=[wt])
            for th in range(nth):
                par = (oc * nth + th) % 2
                ps = g_psA if par == 0 else g_psB
                pt = ("g_ps", par)
                for ic in range(nic):
                    S.emit("pe", lambda e, ps=ps, wb=wb, ic=ic, th=th: e.matmul(
                        ps[...], wb[:, ic, :], rhs_fn(ic, th),
                        start=(ic == 0), stop=(ic == nic - 1)),
                        reads=([wt] + rtags if ic in (0, nic - 1) else []),
                        writes=([pt] if ic in (0, nic - 1) else []),
                        inc=(ic == nic - 1))
                fmsb, ft = (fmsbA, ("fmsb", 0)) if par == 0 else (fmsbB, ("fmsb", 1))
                S.emit("dve", lambda e, fmsb=fmsb, ps=ps: e.tensor_copy(
                    fmsb[...], ps[...]), reads=[pt], writes=[ft])
                post_q.append((fmsb, ft, oc, th))
                if len(post_q) > 1:
                    flush_post()
        while post_q:
            flush_post()

    fm_phase("proj", OC, FC,
             lambda e, wb, oc: e.dma_start(out=wb[:, 0:FC, :], in_=WTP[oc]),
             lambda ic, th: ctx_all[:, ic, th * 512:(th + 1) * 512],
             ["ctx_all"], prefetched=2)
    for ttg in range(NT):
        if sub <= 3:
            break
        S.dma("pool", lambda e, ttg=ttg: e.dma_start(
            out=acc_dram[ttg * 128:(ttg + 1) * 128, :], in_=stg_all[:, ttg, :],
            max_dma_last_dim=2048),
            "acc_proj", reads=[("stg", ttg, oc2) for oc2 in range(OC)],
            writes=[("acc", ttg)])
        if sub <= 4:
            continue
        S.dma("pool", lambda e, ttg=ttg: e.dma_start(
            out=acc_dram[ttg * 128:(ttg + 1) * 128, :],
            in_=X[ttg * 128:(ttg + 1) * 128, :], accum_op=ALU.add,
            max_dma_last_dim=4096),
            "acc_x", reads=[("acc", ttg)], writes=[("acc", ttg)])
        S.dma("pool", lambda e, ttg=ttg: e.dma_start(
            out=acc_dram[ttg * 128:(ttg + 1) * 128, :], in_=bias_bc[...],
            accum_op=ALU.add, max_dma_last_dim=4096),
            "acc_b", reads=[("acc", ttg), "bias_bc"], writes=[("acc", ttg)])

    es_ctx.close()
    if upto <= 5:
        return _finish()

    es_r3b = ExitStack()
    mid_sb = sb("mid_sb", [128, ICS, TL], BF16, stack=es_r3b, side="right")

    for ic in range(ICS):
        gb, gbt = (sgA, ("sg", 0)) if ic % 2 == 0 else (sgB, ("sg", 1))
        ub, ubt = (suA, ("su", 0)) if ic % 2 == 0 else (suB, ("su", 1))
        if ic >= 2:
            S.dma("pool", lambda e, ic=ic, gb=gb: e.dma_start(out=gb[...], in_=SWG[ic]),
                  "swg_ld" + str(ic % 2), writes=[gbt])
            S.dma("pool", lambda e, ic=ic, ub=ub: e.dma_start(out=ub[...], in_=SWU[ic]),
                  "swu_ld" + str(ic % 2), writes=[ubt])
        for th in range(2):
            gps, gpt = (g_psA, ("g_ps", 0)) if th == 0 else (g_psB, ("g_ps", 1))
            ups, upt = (u_psA, ("u_ps", 0)) if th == 0 else (u_psB, ("u_ps", 1))
            for fcc in range(FC):
                S.emit("pe", lambda e, gps=gps, gb=gb, fcc=fcc, th=th: e.matmul(
                    gps[...], gb[:, fcc, :], xn_fm[:, fcc, th * 512:(th + 1) * 512],
                    start=(fcc == 0), stop=(fcc == FC - 1)),
                    reads=([gbt, "xn_fm"] if fcc in (0, FC - 1) else []),
                    writes=([("g_ps", th)] if fcc in (0, FC - 1) else []),
                    inc=(fcc == FC - 1))
            for fcc in range(FC):
                S.emit("pe", lambda e, ups=ups, ub=ub, fcc=fcc, th=th: e.matmul(
                    ups[...], ub[:, fcc, :], xn_fm[:, fcc, th * 512:(th + 1) * 512],
                    start=(fcc == 0), stop=(fcc == FC - 1)),
                    reads=([ubt, "xn_fm"] if fcc in (0, FC - 1) else []),
                    writes=([("u_ps", th)] if fcc in (0, FC - 1) else []),
                    inc=(fcc == FC - 1))
            S.emit("act", lambda e, gps=gps: e.activation(sil_sb[...], gps[...],
                   AF.Silu, bias=ZCOL, scale=1.0),
                   reads=[gpt, "consts"], writes=["sil"])
            S.emit("dve", lambda e, ic=ic, th=th, ups=ups: e.tensor_tensor(
                mid_sb[:, ic, th * 512:(th + 1) * 512], sil_sb[...], ups[...],
                ALU.mult), reads=["sil", upt], writes=[("mid", ic)])

    fm_phase("shd", OC, ICS,
             lambda e, wb, oc: e.dma_start(out=wb[...], in_=SWD[oc]),
             lambda ic, th: mid_sb[:, ic, th * 512:(th + 1) * 512],
             [("mid", icx) for icx in range(ICS)])
    for ttg in range(NT):
        S.dma("pool", lambda e, ttg=ttg: e.dma_start(
            out=acc_dram[ttg * 128:(ttg + 1) * 128, :], in_=stg_all[:, ttg, :],
            accum_op=ALU.add, max_dma_last_dim=2048),
            "acc_sh", reads=[("stg", ttg, oc2) for oc2 in range(OC)] + [("acc", ttg)],
            writes=[("acc", ttg)])

    if upto <= 6:
        return _finish()

    es_r3b.close()
    es_r3c.close()
    S.barrier("exp_in")

    es_r4 = ExitStack()
    xgfmA = sb("xgfmA", [128, FC, 512], BF16, stack=es_r4, side="right")
    xgfmB = sb("xgfmB", [128, FC, 512], BF16, stack=es_r4, side="right")
    emid = sb("emid", [128, ICE, 512], BF16, stack=es_r4, side="right")
    wgA_ = sb("wgA_", [128, FC, 128], BF16, stack=es_r4, side="right")
    wgB_ = sb("wgB_", [128, FC, 128], BF16, stack=es_r4, side="right")
    wuA_ = sb("wuA_", [128, FC, 128], BF16, stack=es_r4, side="right")
    wuB_ = sb("wuB_", [128, FC, 128], BF16, stack=es_r4, side="right")
    wdA_ = sb("wdA_", [128, ICE, 128], BF16, stack=es_r4, side="right")
    wdB_ = sb("wdB_", [128, ICE, 128], BF16, stack=es_r4, side="right")

    if debug:
        S.dma("sp", lambda e: e.dma_start(out=dbg["d_xg"], in_=xg_dram), "dbg",
              reads=["xg_dram"])

    for el in range(EL):
        for sc in range(2):
            ch = el * 2 + sc
            xgfm = xgfmA if ch % 2 == 0 else xgfmB
            xft = ("xgfm", ch % 2)
            row0 = el * CAP + sc * 512
            for fcc in range(FC):
                S.dma("sp", lambda e, fcc=fcc, row0=row0, xgfm=xgfm: e.dma_start_transpose(
                    xgfm[:, fcc, :], xg_dram[row0:row0 + 512, fcc * 128:(fcc + 1) * 128]),
                    "xgfm_ld" + str(ch % 2), reads=["xg_dram"],
                    writes=([xft] if fcc == 0 else []))
            S.group_commit("xgfm_ld" + str(ch % 2), [xft])
            for ic in range(ICE):
                par = ic % 2
                gb, gbt = (wgA_, ("ewg", 0)) if par == 0 else (wgB_, ("ewg", 1))
                ub, ubt = (wuA_, ("ewu", 0)) if par == 0 else (wuB_, ("ewu", 1))
                S.dma("pool", lambda e, el=el, ic=ic, gb=gb: e.dma_start(
                    out=gb[...], in_=WG[el, ic]), "ewg_ld" + str(par), writes=[gbt])
                S.dma("pool", lambda e, el=el, ic=ic, ub=ub: e.dma_start(
                    out=ub[...], in_=WU[el, ic]), "ewu_ld" + str(par), writes=[ubt])
                gps, gpt = (g_psA, ("g_ps", 0)) if par == 0 else (g_psB, ("g_ps", 1))
                ups, upt = (u_psA, ("u_ps", 0)) if par == 0 else (u_psB, ("u_ps", 1))
                for fcc in range(FC):
                    S.emit("pe", lambda e, gps=gps, gb=gb, fcc=fcc, xgfm=xgfm: e.matmul(
                        gps[...], gb[:, fcc, :], xgfm[:, fcc, :],
                        start=(fcc == 0), stop=(fcc == FC - 1)),
                        reads=([gbt, xft] if fcc in (0, FC - 1) else []),
                        writes=([gpt] if fcc in (0, FC - 1) else []),
                        inc=(fcc == FC - 1))
                for fcc in range(FC):
                    S.emit("pe", lambda e, ups=ups, ub=ub, fcc=fcc, xgfm=xgfm: e.matmul(
                        ups[...], ub[:, fcc, :], xgfm[:, fcc, :],
                        start=(fcc == 0), stop=(fcc == FC - 1)),
                        reads=([ubt, xft] if fcc in (0, FC - 1) else []),
                        writes=([upt] if fcc in (0, FC - 1) else []),
                        inc=(fcc == FC - 1))
                S.emit("act", lambda e, gps=gps: e.activation(sil_sb[...], gps[...],
                       AF.Silu, bias=ZCOL, scale=1.0),
                       reads=[gpt, "consts"], writes=["sil"])
                S.emit("dve", lambda e, ic=ic, ups=ups: e.tensor_tensor(
                    emid[:, ic, :], sil_sb[...], ups[...], ALU.mult),
                    reads=["sil", upt], writes=[("emid", ic)])
            epost = []

            def eflush():
                fmsb, ft, oc = epost.pop(0)
                for tt in range(4):
                    tp, tpt = (tp_psA, ("tp", 0)) if tt % 2 == 0 else (tp_psB, ("tp", 1))
                    S.emit("pe", lambda e, tp=tp, fmsb=fmsb, tt=tt: e.transpose(
                        tp[...], fmsb[:, tt * 128:(tt + 1) * 128], ident_sb[...]),
                        reads=[ft, "consts"], writes=[tpt])
                    S.emit("act", lambda e, tp=tp, oc=oc, tt=tt: e.activation(
                        stg_all[:, tt, oc * 128:(oc + 1) * 128], tp[...], AF.Copy,
                        bias=0.0, scale=1.0),
                        reads=[tpt], writes=[("stg", tt, oc)])

            for oc in range(OC):
                par = oc % 2
                db, dbt = (wdA_, ("ewd", 0)) if par == 0 else (wdB_, ("ewd", 1))
                S.dma("pool", lambda e, el=el, oc=oc, db=db: e.dma_start(
                    out=db[...], in_=WD[el, oc]), "ewd_ld" + str(par), writes=[dbt])
                fps, fpt = (g_psA, ("g_ps", 0)) if par == 0 else (g_psB, ("g_ps", 1))
                for ic in range(ICE):
                    S.emit("pe", lambda e, fps=fps, db=db, ic=ic: e.matmul(
                        fps[...], db[:, ic, :], emid[:, ic, :],
                        start=(ic == 0), stop=(ic == ICE - 1)),
                        reads=([dbt] + [("emid", icx) for icx in range(ICE)]
                               if ic in (0, ICE - 1) else []),
                        writes=([fpt] if ic in (0, ICE - 1) else []),
                        inc=(ic == ICE - 1))
                fmsb, ft = (fmsbA, ("fmsb", 0)) if par == 0 else (fmsbB, ("fmsb", 1))
                S.emit("dve", lambda e, fmsb=fmsb, fps=fps: e.tensor_copy(
                    fmsb[...], fps[...]), reads=[fpt], writes=[ft])
                epost.append((fmsb, ft, oc))
                if len(epost) > 1:
                    eflush()
            while epost:
                eflush()
            for tt in range(4):
                S.dma("pool", lambda e, tt=tt, row0=row0: e.dma_start(
                    out=fout_in[row0 + tt * 128:row0 + (tt + 1) * 128, :],
                    in_=stg_all[:, tt, :]), "fout_st",
                    reads=[("stg", tt, oc2) for oc2 in range(OC)],
                    writes=[("fout_in", el)])
        S.coll(lambda e, el=el: e.collective_compute(
            "AllGather", ALU.bypass, replica_groups=[list(range(NC))],
            ins=[fout_in[el * CAP:(el + 1) * CAP, :].opt()],
            outs=[fout_ag2[el].opt()]),
            "cc_fout", reads=[("fout_in", el)], writes=[("fout_ag", el)])

    if debug:
        S.dma("sp", lambda e: e.dma_start(out=dbg["d_fout"], in_=fout_in), "dbg",
              reads=[("fout_in", 0), ("fout_in", 1)])
        S.dma("sp", lambda e: e.dma_start(
            out=dbg["d_fag"], in_=fout_ag2.rearrange("a r d -> (a r) d")), "dbg",
            reads=[("fout_ag", 0), ("fout_ag", 1)])
        S.dma("sp", lambda e: e.dma_start(out=dbg["d_acc"], in_=acc_dram), "dbg",
              reads=[("acc", ttg) for ttg in range(NT)])

    if upto <= 7:
        return _finish()

    es_r4.close()
    es_r3a.close()
    S.barrier("asm_in")

    es_r5 = ExitStack()
    moe_g = [[sb(f"moe_g{k}_{p}", [128, D], BF16, stack=es_r5, side="right")
              for p in range(2)] for k in range(4)]
    acc_sbA = sb("acc_sbA", [128, D], F32, stack=es_r5, side="right")
    acc_sbB = sb("acc_sbB", [128, D], F32, stack=es_r5, side="right")
    out_sbA = sb("out_sbA", [128, D], F32, stack=es_r5, side="right")
    out_sbB = sb("out_sbB", [128, D], F32, stack=es_r5, side="right")

    fout_flat = fout_ag2.rearrange("a r d -> (a r) d")
    for k in range(4):
        for p in range(2):
            S.emit("dve", lambda e, k=k, p=p: e.memset(moe_g[k][p][...], 0.0),
                   writes=[("moe", k, p)])

    for j in range(NT):
        p = j % 2
        acc, at = (acc_sbA, ("acc_sb", 0)) if p == 0 else (acc_sbB, ("acc_sb", 1))
        out, ot = (out_sbA, ("out_sb", 0)) if p == 0 else (out_sbB, ("out_sb", 1))
        S.dma("sp", lambda e, j=j, acc=acc: e.dma_start(
            out=acc[...], in_=acc_dram[j * 128:(j + 1) * 128, :]),
            "acc_ld" + str(p), reads=[("acc", j)], writes=[at])
        for k in range(4):
            def gmo(e, j=j, k=k, p=p):
                mkreg(e)
                return e.indirect_dma_start(
                    out=moe_g[k][p][...], out_offset=None,
                    in_=fout_flat,
                    in_offset=bass.IndirectOffsetOnAxis(ap=idxs2[:, j, k:k + 1], axis=0),
                    bounds_check=bnd["r3"], oob_is_err=False)
            S.dma("pool", gmo, "moe_g" + str(k) + str(p),
                  reads=[("fout_ag", 0), ("fout_ag", 1), "idxs2"],
                  writes=[("moe", k, p)])
        S.emit("dve", lambda e, j=j, acc=acc, out=out, p=p: e.scalar_tensor_tensor(
            out[...], moe_g[0][p][...], w4k[:, j, 0:1], acc[...],
            ALU.mult, ALU.add),
            reads=[("moe", 0, p), "w4k", at], writes=[ot])
        for k in range(1, 4):
            S.emit("dve", lambda e, j=j, k=k, out=out, p=p: e.scalar_tensor_tensor(
                out[...], moe_g[k][p][...], w4k[:, j, k:k + 1], out[...],
                ALU.mult, ALU.add),
                reads=[("moe", k, p), "w4k"], writes=[ot])
        S.dma("sp", lambda e, j=j, out=out: e.dma_start(
            out=Y[j * 128:(j + 1) * 128, :], in_=out[...]), "y_st",
            reads=[ot], writes=[("y", j)])
    S.final_wait("sp", "y_st")
    if debug:
        S.final_wait("sp", "dbg")

    S.verify()
    with nc.Block() as block:
        S.replay(block)
    es_r5.close()
    es_p5.close()
    es.close()
    return nc, dbg


def _host_prep(c, inputs):
    x = np.asarray(inputs["L0_attn_hidden_states"]).reshape(T, D)
    caw = np.asarray(inputs["L0_attn_c_attn_w"])
    cab = np.asarray(inputs["L0_attn_c_attn_b"])
    cpw = np.asarray(inputs["L0_attn_c_proj_w"])
    cpb = np.asarray(inputs["L0_attn_c_proj_b"])
    gw = np.asarray(inputs["L0_ffn_gate_w"])
    wg = np.asarray(inputs["L0_ffn_w_gate"])
    wu = np.asarray(inputs["L0_ffn_w_up"])
    wd = np.asarray(inputs["L0_ffn_w_down"])
    swg = np.asarray(inputs["L0_ffn_sw_gate"])
    swu = np.asarray(inputs["L0_ffn_sw_up"])
    swd = np.asarray(inputs["L0_ffn_sw_down"])

    blocks = np.arange(NT) * NC + c
    m = {}
    m["x"] = np.ascontiguousarray(x.reshape(64, 128, D)[blocks].reshape(TL, D))
    m["wta"] = np.ascontiguousarray(caw.reshape(OCQ, 128, FC, 128).transpose(0, 3, 2, 1))
    m["wtp"] = np.ascontiguousarray(cpw.reshape(OC, 128, FC, 128).transpose(0, 3, 2, 1))
    m["gt"] = np.ascontiguousarray(gw.reshape(E, FC, 128).transpose(2, 1, 0))
    m["battn"] = np.ascontiguousarray(cab.reshape(OCQ, 128).T)
    m["bproj"] = np.ascontiguousarray(cpb.reshape(1, D))
    m["swg"] = np.ascontiguousarray(swg.reshape(FC, 128, ICS, 128).transpose(2, 1, 0, 3))
    m["swu"] = np.ascontiguousarray(swu.reshape(FC, 128, ICS, 128).transpose(2, 1, 0, 3))
    m["swd"] = np.ascontiguousarray(swd.reshape(ICS, 128, OC, 128).transpose(2, 1, 0, 3))
    m["wg"] = np.ascontiguousarray(
        wg[2 * c:2 * c + 2].reshape(EL, FC, 128, ICE, 128).transpose(0, 3, 2, 1, 4))
    m["wu"] = np.ascontiguousarray(
        wu[2 * c:2 * c + 2].reshape(EL, FC, 128, ICE, 128).transpose(0, 3, 2, 1, 4))
    m["wd"] = np.ascontiguousarray(
        wd[2 * c:2 * c + 2].reshape(EL, ICE, 128, OC, 128).transpose(0, 3, 2, 1, 4))

    tri = np.triu(np.ones((128, 128), np.float32))
    m["tri"] = np.ascontiguousarray(np.tile(tri, (1, 8))).astype(ml_dtypes.bfloat16)
    m["ltri"] = np.ascontiguousarray(
        np.tril(np.ones((128, 128), np.float32), -1).T).astype(ml_dtypes.bfloat16)
    m["ident"] = np.eye(128, dtype=ml_dtypes.bfloat16)
    m["onesb"] = np.ones((128, 1), ml_dtypes.bfloat16)
    m["onesm"] = np.ones((128, 128), ml_dtypes.bfloat16)
    m["onesf"] = np.ones((1, 128), np.float32)

    cfv = np.zeros((128, NMAIN + 3), np.float32)
    for j in range(NT):
        for i in range(8 * j + 8):
            cfv[:, 4 * j * (j + 1) + i] = -12.0 if i < 8 * j + c else -112.0
    cfv[:, NMAIN] = -12.0
    cfv[:, NMAIN + 1] = 1e-5
    cfv[:, NMAIN + 2] = 0.0
    m["cf"] = cfv

    psm = np.zeros((64, NT), np.float32)
    for im in range(64):
        R, jj = im // NT, im % NT
        g = NC * jj + R
        for j in range(NT):
            if g < NC * j + c:
                psm[im, j] = 1.0
    m["psm"] = psm.astype(ml_dtypes.bfloat16)

    ejm = np.zeros((NT, NT * 128), np.float32)
    for j in range(NT):
        ejm[j, j * 128:(j + 1) * 128] = 1.0
    m["ejm"] = ejm
    m["iota16"] = np.tile(np.arange(E, dtype=np.float32), (128, 1))
    ev = np.arange(E)
    v16 = (ev % 2) * 8192 + (ev // 2) * 1024 - ev * 1024
    m["vals16"] = np.tile(v16.astype(np.float32), (128, 1))
    rid = np.zeros((128, NT), np.uint32)
    for j in range(NT):
        rid[:, j] = c * TL + j * 128 + np.arange(128)
    m["rowid"] = rid
    bix = np.zeros((128, EL * 8), np.uint32)
    for g in range(EL * 8):
        bix[:, g] = 2048 * c + g * 128 + np.arange(128)
    m["bufidx"] = bix
    return m


_CACHE = {}


def kernel(**inputs):
    if "nc" not in _CACHE:
        _CACHE["nc"] = build(debug=False)[0]
    nc = _CACHE["nc"]
    in_maps = [_host_prep(c, inputs) for c in range(NC)]
    res = run_bass_kernel_spmd(nc, in_maps, core_ids=list(range(NC)))
    out = np.zeros((1, T, D), np.float32)
    for c in range(NC):
        y = res.results[c]["y"]
        blocks = np.arange(NT) * NC + c
        out[0].reshape(64, 128, D)[blocks] = y.reshape(NT, 128, D)
    return out


# version 08
# speedup vs baseline: 1.0643x; 1.0093x over previous
import numpy as np
import ml_dtypes
from contextlib import ExitStack
import concourse.bass as bass
import concourse.mybir as mybir
from concourse.bass_utils import run_bass_kernel_spmd

F32 = mybir.dt.float32
BF16 = mybir.dt.bfloat16
U32 = mybir.dt.uint32
AF = mybir.ActivationFunctionType
ALU = mybir.AluOpType
AX = mybir.AxisListType

T = 8192
D = 2048
NC = 8
TL = 1024
NT = 8
FC = 16
NH = 16
HD = 128
OCQ = 18
E = 16
EL = 2
CAP = 1024
ICE = 11
ICS = 22
OC = 16
SCALE = 1.0 / float(np.sqrt(128.0))
NMAIN = 288

ENGINES = ("pe", "act", "dve", "pool", "sp")


class Sched:

    def __init__(self, nc, es):
        self.nc = nc
        self.es = es
        self.cmds = {e: [] for e in ENGINES}
        self.semh = {}
        self.semcnt = {}
        self.last_write = {}
        self.readers = {}
        self.waited = {e: {} for e in ENGINES}
        self.esem = {e: self.sem("S_" + e) for e in ENGINES}

    def sem(self, name):
        if name not in self.semh:
            self.semh[name] = self.es.enter_context(self.nc.semaphore(name))
            self.semcnt[name] = 0
        return name

    def _wait(self, eng, semname, val):
        if val <= 0 or self.waited[eng].get(semname, 0) >= val:
            return
        self.waited[eng][semname] = val
        self.cmds[eng].append(("wait", semname, val))

    def _dep_waits(self, eng, reads, writes):
        for tag in reads:
            lw = self.last_write.get(tag)
            if lw is not None:
                self._wait(eng, lw[0], lw[1])
        for tag in writes:
            lw = self.last_write.get(tag)
            if lw is not None:
                self._wait(eng, lw[0], lw[1])
            for (s, v) in self.readers.get(tag, ()):
                self._wait(eng, s, v)

    def _record(self, semname, val, reads, writes):
        for tag in writes:
            self.last_write[tag] = (semname, val)
            self.readers[tag] = []
        for tag in reads:
            self.readers.setdefault(tag, []).append((semname, val))

    def emit(self, eng, fn, reads=(), writes=(), inc=True):
        self._dep_waits(eng, reads, writes)
        if inc:
            s = self.esem[eng]
            self.semcnt[s] += 1
            self.cmds[eng].append(("emit", fn, s, 1))
            self._record(s, self.semcnt[s], reads, writes)
        else:
            self.cmds[eng].append(("emit", fn, None, 0))

    def dma(self, eng, fn, sem, reads=(), writes=()):
        self.sem(sem)
        self._dep_waits(eng, reads, writes)
        self.semcnt[sem] += 16
        self.cmds[eng].append(("emit", fn, sem, 16))
        self._record(sem, self.semcnt[sem], reads, writes)

    def coll(self, fn, sem, reads=(), writes=()):
        self.sem(sem)
        self._dep_waits("pool", reads, writes)
        self.semcnt[sem] += 1
        self.cmds["pool"].append(("emit", fn, sem, 1))
        self._record(sem, self.semcnt[sem], reads, writes)

    def group_commit(self, sem, tags):
        tot = self.semcnt[sem]
        for tag in tags:
            self.last_write[tag] = (sem, tot)
            self.readers[tag] = []

    def barrier(self, name, engines=ENGINES):
        s = self.sem("B_" + name)
        for e in engines:
            self.semcnt[s] += 1
            self.cmds[e].append(("emit", None, s, 1))
        tot = self.semcnt[s]
        for e in engines:
            self._wait(e, s, tot)

    def final_wait(self, eng, sem):
        self.cmds[eng].append(("wait", sem, self.semcnt[sem]))

    def verify(self):
        pos = {e: 0 for e in ENGINES}
        cnt = {s: 0 for s in self.semh}
        progress = True
        while progress:
            progress = False
            for e in ENGINES:
                while pos[e] < len(self.cmds[e]):
                    c = self.cmds[e][pos[e]]
                    if c[0] == "wait":
                        if cnt[c[1]] < c[2]:
                            break
                    elif c[2] is not None:
                        cnt[c[2]] += c[3]
                    pos[e] += 1
                    progress = True
        stuck = {}
        for e in ENGINES:
            if pos[e] < len(self.cmds[e]):
                c = self.cmds[e][pos[e]]
                stuck[e] = f"cmd {pos[e]}/{len(self.cmds[e])}: wait {c[1]}>={c[2]} (at {cnt.get(c[1])})"
        if stuck:
            raise RuntimeError(f"DEADLOCK in schedule: {stuck}")

    def replay(self, block):
        proxies = {"pe": block.tensor, "act": block.scalar, "dve": block.vector,
                   "pool": block.gpsimd, "sp": block.sync}

        def mk(cmds):
            def body(eng):
                for c in cmds:
                    if c[0] == "wait":
                        eng.wait_ge(self.semh[c[1]], c[2])
                    else:
                        _, fn, s, amt = c
                        if fn is None:
                            eng.nop().then_inc(self.semh[s], amt)
                        else:
                            inst = fn(eng)
                            if s is not None:
                                inst.then_inc(self.semh[s], amt)
            return body
        for e in ENGINES:
            proxies[e](mk(self.cmds[e]))


def build(debug=False, upto=99, sub=99):
    nc = bass.Bass("TRN2", target_bir_lowering=False, debug=False)
    es = ExitStack()

    X = nc.dram_tensor("x", [TL, D], F32, kind="ExternalInput").ap()
    WTA = nc.dram_tensor("wta", [OCQ, 128, FC, 128], F32, kind="ExternalInput").ap()
    WTP = nc.dram_tensor("wtp", [OC, 128, FC, 128], F32, kind="ExternalInput").ap()
    GT = nc.dram_tensor("gt", [128, FC, E], F32, kind="ExternalInput").ap()
    BATTN = nc.dram_tensor("battn", [128, OCQ], F32, kind="ExternalInput").ap()
    BPROJ = nc.dram_tensor("bproj", [1, D], F32, kind="ExternalInput").ap()
    SWG = nc.dram_tensor("swg", [ICS, 128, FC, 128], F32, kind="ExternalInput").ap()
    SWU = nc.dram_tensor("swu", [ICS, 128, FC, 128], F32, kind="ExternalInput").ap()
    SWD = nc.dram_tensor("swd", [OC, 128, ICS, 128], F32, kind="ExternalInput").ap()
    WG = nc.dram_tensor("wg", [EL, ICE, 128, FC, 128], F32, kind="ExternalInput").ap()
    WU = nc.dram_tensor("wu", [EL, ICE, 128, FC, 128], F32, kind="ExternalInput").ap()
    WD = nc.dram_tensor("wd", [EL, OC, 128, ICE, 128], F32, kind="ExternalInput").ap()
    TRI = nc.dram_tensor("tri", [128, 8 * 128], BF16, kind="ExternalInput").ap()
    LTRI = nc.dram_tensor("ltri", [128, 128], BF16, kind="ExternalInput").ap()
    IDENT = nc.dram_tensor("ident", [128, 128], BF16, kind="ExternalInput").ap()
    ONESB = nc.dram_tensor("onesb", [128, 1], BF16, kind="ExternalInput").ap()
    ONESM = nc.dram_tensor("onesm", [128, 128], BF16, kind="ExternalInput").ap()
    ONESF = nc.dram_tensor("onesf", [1, 128], F32, kind="ExternalInput").ap()
    CF = nc.dram_tensor("cf", [128, NMAIN + 3], F32, kind="ExternalInput").ap()
    PSM = nc.dram_tensor("psm", [64, NT], BF16, kind="ExternalInput").ap()
    EJM = nc.dram_tensor("ejm", [NT, NT * 128], F32, kind="ExternalInput").ap()
    IOTA16 = nc.dram_tensor("iota16", [128, E], F32, kind="ExternalInput").ap()
    VALS16 = nc.dram_tensor("vals16", [128, E], F32, kind="ExternalInput").ap()
    ROWID = nc.dram_tensor("rowid", [128, NT], U32, kind="ExternalInput").ap()
    BUFIDX = nc.dram_tensor("bufidx", [128, EL * 8], U32, kind="ExternalInput").ap()
    Y = nc.dram_tensor("y", [TL, D], F32, kind="ExternalOutput").ap()

    xnag_in = nc.dram_tensor("xnag_in", [TL, D], BF16).ap()
    xnag_out = nc.dram_tensor("xnag_out", [T, D], BF16).ap()
    kag_in = nc.dram_tensor("kag_in", [128, TL], BF16).ap()
    kag_out = nc.dram_tensor("kag_out", [NC * 128, TL], BF16).ap()
    vag_in = nc.dram_tensor("vag_in", [128, NT * HD], BF16).ap()
    vag_out = nc.dram_tensor("vag_out", [NC * 128, NT * HD], BF16).ap()
    cnt_in = nc.dram_tensor("cnt_in", [NT, E], F32).ap()
    cnt_out = nc.dram_tensor("cnt_out", [64, E], F32).ap()
    buf_in = nc.dram_tensor("buf_in", [E * CAP, 1], U32).ap()
    buf_out = nc.dram_tensor("buf_out", [E * CAP, 1], U32).ap()
    acc_dram = nc.dram_tensor("acc_dram", [TL, D], F32).ap()
    xg_dram = nc.dram_tensor("xg_dram", [EL * CAP, D], BF16).ap()
    fout_in = nc.dram_tensor("fout_in", [EL * CAP, D], BF16).ap()
    fout_ag2 = nc.dram_tensor("fout_ag2", [2, NC * CAP, D], BF16).ap()

    dbg = {}
    if debug:
        for nm, shp, dt in [
            ("d_cnt", [64, E], F32), ("d_w", [128, NT * E], F32),
            ("d_rank", [128, NT * E], F32), ("d_idx", [128, NT * 4], U32),
            ("d_w4", [128, NT * 4], F32), ("d_buf", [E * CAP, 1], U32),
            ("d_q", [128, NH * TL], BF16), ("d_k", [128, TL], BF16),
            ("d_ctx", [128, NH * TL], BF16), ("d_acc", [TL, D], F32),
            ("d_xg", [EL * CAP, D], BF16), ("d_fout", [EL * CAP, D], BF16),
            ("d_idx2", [128, NT * 4], U32), ("d_fag", [2 * NC * CAP, D], BF16),
            ("d_bc", [128, NT * E], F32), ("d_excl", [128, NT * E], F32),
        ]:
            dbg[nm] = nc.dram_tensor(nm, shp, dt, kind="ExternalOutput").ap()

    S = Sched(nc, es)

    def _finish():
        S.verify()
        with nc.Block() as block:
            S.replay(block)
        return nc, dbg

    def sb(name, shape, dt, stack=es, side=None):
        kw = {} if side is None else {"side": side}
        return stack.enter_context(nc.sbuf_tensor(name, shape, dt, **kw))

    tri_sb = sb("tri_sb", [128, 8 * 128], BF16)
    ltri_sb = sb("ltri_sb", [128, 128], BF16)
    ident_sb = sb("ident_sb", [128, 128], BF16)
    onesb_sb = sb("onesb_sb", [128, 1], BF16)
    onesm_sb = sb("onesm_sb", [128, 128], BF16)
    onesf_sb = sb("onesf_sb", [1, 128], F32)
    cf_sb = sb("cf_sb", [128, NMAIN + 3], F32)
    psm_sb = sb("psm_sb", [64, NT], BF16)
    ejm_sb = sb("ejm_sb", [NT, NT * 128], F32)
    iota_sb = sb("iota_sb", [128, E], F32)
    vals_sb = sb("vals_sb", [128, E], F32)
    rowid_sb = sb("rowid_sb", [128, NT], U32)
    bufidx_sb = sb("bufidx_sb", [128, EL * 8], U32)
    gt_sb = sb("gt_sb", [128, FC, E], BF16)
    battn_sb = sb("battn_sb", [128, OCQ], F32)
    xn_fm = sb("xn_fm", [128, FC, TL], BF16)
    w_all = sb("w_all", [128, NT, E], F32)
    excl_all = sb("excl_all", [128, NT, E], F32)
    rank_all = sb("rank_all", [128, NT, E], F32)
    e4f = sb("e4f", [128, NT, 4], F32)
    w4k = sb("w4k", [128, NT, 4], F32)
    idxs = sb("idxs", [128, NT, 4], U32)
    idxs2 = sb("idxs2", [128, NT, 4], U32)
    r_t2 = sb("r_t2", [128, E], F32)
    r_t3 = sb("r_t3", [128, 1], F32)
    cntstage = sb("cntstage", [1, NT, E], F32)
    cnts_sb = sb("cnts_sb", [64, E], BF16)
    base_sb = sb("base_sb", [NT, E], F32)
    bc_sb = sb("bc_sb", [128, NT, E], F32)
    stat_sb = sb("stat_sb", [128, 4, 6], F32)
    mv_sb = sb("mv_sb", [128, 2], F32)
    rstd_sb = sb("rstd_sb", [128, 1], F32)
    negmr_sb = sb("negmr_sb", [128, 1], F32)
    r_nmax = sb("r_nmax", [128, 1], F32)
    r_ex = sb("r_ex", [128, E], F32)
    r_s1 = sb("r_s1", [128, 1], F32)
    r_probs = sb("r_probs", [128, E], F32)
    r_m8 = sb("r_m8", [128, 8], F32)
    r_idx8 = sb("r_idx8", [128, 8], U32)
    r_sel = sb("r_sel", [128, E], F32)
    r_selb = sb("r_selb", [128, E], BF16)
    r_kept = sb("r_kept", [128, E], F32)
    r_brank = sb("r_brank", [128, E], F32)
    r_oh = sb("r_oh", [128, E], F32)
    r_wk = sb("r_wk", [128, E], F32)
    r_t1 = sb("r_t1", [128, 1], F32)
    zero_u32 = sb("zero_u32", [128, 128], U32)
    fmsbA = sb("fmsbA", [128, 512], BF16)
    fmsbB = sb("fmsbB", [128, 512], BF16)
    sil_sb = sb("sil_sb", [128, 512], BF16)
    wpA = sb("wpA", [128, ICS, 128], BF16)
    wpB = sb("wpB", [128, ICS, 128], BF16)
    sgA = sb("sgA", [128, FC, 128], BF16)
    sgB = sb("sgB", [128, FC, 128], BF16)
    suA = sb("suA", [128, FC, 128], BF16)
    suB = sb("suB", [128, FC, 128], BF16)
    mybuf = sb("mybuf", [128, EL * 8], U32)
    xgA = sb("xgA", [128, D], BF16)
    xgB = sb("xgB", [128, D], BF16)
    es_ctx = ExitStack()
    ctx_all = sb("ctx_all", [128, NH, TL], BF16, stack=es_ctx)

    es_r1 = ExitStack()
    x_all = sb("x_all", [128, NT, D], F32, stack=es_r1, side="right")
    xnstA = sb("xnstA", [128, D], BF16, stack=es_r1, side="right")
    xnstB = sb("xnstB", [128, D], BF16, stack=es_r1, side="right")

    EPS = cf_sb[:, NMAIN + 1:NMAIN + 2]
    ZCOL = cf_sb[:, NMAIN + 2:NMAIN + 3]
    DIAGB = cf_sb[:, NMAIN:NMAIN + 1]

    for t, src in [(tri_sb, TRI), (ltri_sb, LTRI), (ident_sb, IDENT),
                   (onesb_sb, ONESB), (onesm_sb, ONESM), (onesf_sb, ONESF), (cf_sb, CF),
                   (psm_sb, PSM), (ejm_sb, EJM), (iota_sb, IOTA16), (vals_sb, VALS16),
                   (rowid_sb, ROWID), (bufidx_sb, BUFIDX), (battn_sb, BATTN)]:
        S.dma("sp", lambda e, t=t, src=src: e.dma_start(out=t[...], in_=src), "c_ld")
    S.dma("pool", lambda e: e.dma_start(out=gt_sb[...], in_=GT), "cg_ld")
    S.group_commit("c_ld", ["consts"])
    S.group_commit("cg_ld", ["consts_g"])
    S.dma("sp", lambda e: e.dma_start(
        out=x_all[...], in_=X.rearrange("(j p) d -> p j d", p=128)), "x_ld",
        writes=["x_all"])

    es_p1 = ExitStack()
    lg_ps = es_p1.enter_context(nc.psum_tensor("lg_ps", [128, 512], F32))[:, 0:E]
    cs_ps = es_p1.enter_context(nc.psum_tensor("cs_ps", [128, 512], F32))[:, 0:E]
    cnt_ps = es_p1.enter_context(nc.psum_tensor("cnt_ps", [1, 512], F32))[:, 0:E]
    qkv_psA = es_p1.enter_context(nc.psum_tensor("qkv_psA", [128, 512], F32))
    qkv_psB = es_p1.enter_context(nc.psum_tensor("qkv_psB", [128, 512], F32))
    vt_ps = es_p1.enter_context(nc.psum_tensor("vt_ps", [128, 1024], BF16))[:, 0:128]
    base_ps = es_p1.enter_context(nc.psum_tensor("base_ps", [NT, 512], F32))[:, 0:E]
    bc_ps = es_p1.enter_context(nc.psum_tensor("bc_ps", [128, 512], F32))[:, 0:E]

    for j in range(NT):
        xst, xt = (xnstA, ("xnst", 0)) if j % 2 == 0 else (xnstB, ("xnst", 1))
        for cch in range(4):
            S.emit("dve", lambda e, j=j, cch=cch: e.bn_stats(
                stat_sb[:, cch, :], x_all[:, j, cch * 512:(cch + 1) * 512]),
                reads=["x_all"], writes=["stat"])
        S.emit("dve", lambda e: e.bn_aggr(mv_sb[...], stat_sb[...]),
               reads=["stat"], writes=["mv"])
        S.emit("act", lambda e: e.activation(rstd_sb[...], mv_sb[:, 1:2], AF.Ln,
               bias=EPS, scale=1.0), reads=["mv", "consts"], writes=["rstd"])
        S.emit("act", lambda e: e.activation(rstd_sb[...], rstd_sb[...], AF.Exp,
               bias=ZCOL, scale=-0.5), reads=["consts"], writes=["rstd"])
        S.emit("dve", lambda e: e.tensor_scalar(negmr_sb[...], mv_sb[:, 0:1],
               rstd_sb[...], -1.0, ALU.mult, ALU.mult),
               reads=["mv", "rstd"], writes=["negmr"])
        S.emit("dve", lambda e, xst=xst, j=j: e.tensor_scalar(xst[...], x_all[:, j, :],
               rstd_sb[...], negmr_sb[...], ALU.mult, ALU.add),
               reads=["x_all", "rstd", "negmr"], writes=[xt])
        S.dma("sp", lambda e, xst=xst, j=j: e.dma_start(
            out=xnag_in[j * 128:(j + 1) * 128, :], in_=xst[...]), "xn_st",
            reads=[xt], writes=["xnag_in"])
    for fcc in range(FC):
        S.dma("sp", lambda e, fcc=fcc: e.dma_start_transpose(
            xn_fm[:, fcc, :], xnag_in[:, fcc * 128:(fcc + 1) * 128]), "xnfm_ld",
            reads=["xnag_in"])
    S.group_commit("xnfm_ld", ["xn_fm"])

    for j in range(NT):
        for fcc in range(FC):
            S.emit("pe", lambda e, j=j, fcc=fcc: e.matmul(
                lg_ps[...], xn_fm[:, fcc, j * 128:(j + 1) * 128], gt_sb[:, fcc, :],
                start=(fcc == 0), stop=(fcc == FC - 1)),
                reads=(["xn_fm", "consts", "consts_g"] if fcc in (0, FC - 1) else []),
                writes=(["lg_ps"] if fcc in (0, FC - 1) else []),
                inc=(fcc == FC - 1))
        S.emit("dve", lambda e: e.tensor_reduce(r_nmax[...], lg_ps[...], AX.X, ALU.max,
               negate=True), reads=["lg_ps"], writes=["r_nmax"])
        S.emit("act", lambda e: e.activation(r_ex[...], lg_ps[...], AF.Exp,
               bias=r_nmax[...], scale=1.0), reads=["lg_ps", "r_nmax"], writes=["r_ex"])
        S.emit("dve", lambda e: e.tensor_reduce(r_s1[...], r_ex[...], AX.X, ALU.add),
               reads=["r_ex"], writes=["r_s1"])
        S.emit("dve", lambda e: e.reciprocal(r_s1[...], r_s1[...]), writes=["r_s1"])
        S.emit("dve", lambda e: e.tensor_scalar(r_probs[...], r_ex[...], r_s1[...],
               None, ALU.mult), reads=["r_ex", "r_s1"], writes=["r_probs"])
        S.emit("dve", lambda e: e.max(r_m8[...], r_probs[...]),
               reads=["r_probs"], writes=["r_m8"])
        S.emit("dve", lambda e: e.max_index(r_idx8[...], r_m8[...], r_probs[...]),
               reads=["r_m8", "r_probs"], writes=["r_idx8"])
        S.emit("dve", lambda e, j=j: e.tensor_copy(e4f[:, j, :], r_idx8[:, 0:4]),
               reads=["r_idx8"], writes=["e4f"])
        S.emit("dve", lambda e: e.tensor_copy(r_t1[...], r_m8[:, 3:4]),
               reads=["r_m8"], writes=["r_t1"])
        S.emit("dve", lambda e: e.tensor_scalar(r_sel[...], r_probs[...], r_t1[...],
               None, ALU.is_ge), reads=["r_probs", "r_t1"], writes=["r_sel"])
        S.emit("dve", lambda e: e.tensor_copy(r_selb[...], r_sel[...]),
               reads=["r_sel"], writes=["r_selb"])
        S.emit("dve", lambda e: e.tensor_reduce(r_t1[...], r_m8[:, 0:4], AX.X, ALU.add),
               reads=["r_m8"], writes=["r_t1"])
        S.emit("dve", lambda e: e.reciprocal(r_t1[...], r_t1[...]), writes=["r_t1"])
        S.emit("dve", lambda e, j=j: e.scalar_tensor_tensor(w_all[:, j, :], r_probs[...],
               r_t1[...], r_sel[...], ALU.mult, ALU.mult),
               reads=["r_probs", "r_t1", "r_sel"], writes=["w_all"])
        S.emit("pe", lambda e: e.matmul(cs_ps[...], ltri_sb[...], r_selb[...],
               start=True, stop=True), reads=["consts", "r_selb"], writes=["cs_ps"])
        S.emit("pe", lambda e: e.matmul(cnt_ps[...], onesb_sb[...], r_selb[...],
               start=True, stop=True), reads=["consts", "r_selb"], writes=["cnt_ps"])
        S.emit("dve", lambda e, j=j: e.tensor_copy(excl_all[:, j, :], cs_ps[...]),
               reads=["cs_ps"], writes=["excl_all"])
        S.emit("dve", lambda e, j=j: e.tensor_copy(cntstage[:, j, :], cnt_ps[...]),
               reads=["cnt_ps"], writes=["cntstage"])
    S.dma("pool", lambda e: e.dma_start(out=cnt_in.rearrange("(o j) e -> o j e", o=1),
          in_=cntstage[...]), "cnt_st",
          reads=["cntstage"], writes=["cnt_in"])
    S.coll(lambda e: e.collective_compute("AllGather", ALU.bypass,
           replica_groups=[list(range(NC))], ins=[cnt_in.opt()], outs=[cnt_out.opt()]),
           "cc_cnt", reads=["cnt_in"], writes=["cnt_out"])
    S.coll(lambda e: e.collective_compute("AllGather", ALU.bypass,
           replica_groups=[list(range(NC))], ins=[xnag_in.opt()], outs=[xnag_out.opt()]),
           "cc_xn", reads=["xnag_in"], writes=["xnag_out"])

    if upto <= 1:
        return _finish()

    es_r1.close()
    es_r2 = ExitStack()
    Q_all = sb("Q_all", [128, NH, TL], BF16, stack=es_r2, side="right")
    K_own = sb("K_own", [128, TL], BF16, stack=es_r2, side="right")
    V_own = sb("V_own", [128, NT, HD], BF16, stack=es_r2, side="right")
    wbufA = sb("wbufA", [128, FC, 128], BF16, stack=es_r2, side="right")
    wbufB = sb("wbufB", [128, FC, 128], BF16, stack=es_r2, side="right")
    K_sb = sb("K_sb", [128, NC, TL], BF16, stack=es_r2, side="right")
    V_sb = sb("V_sb", [128, NC, NT * HD], BF16, stack=es_r2, side="right")
    E0 = sb("E0", [128, 1024], BF16, stack=es_r2, side="right")
    E1 = sb("E1", [128, 1024], BF16, stack=es_r2, side="right")
    E2 = sb("E2", [128, 1024], BF16, stack=es_r2, side="right")
    Ed = sb("Ed", [128, 1024], BF16, stack=es_r2, side="right")
    rbc_sb = sb("rbc_sb", [128, 1024], F32, stack=es_r2, side="right")
    es_r2b = ExitStack()
    V_fm_stage = sb("V_fm_stage", [128, TL], BF16, stack=es_r2b, side="right")

    for oc in range(OCQ):
        wb, wt = (wbufA, ("wq", 0)) if oc % 2 == 0 else (wbufB, ("wq", 1))
        S.dma("pool", lambda e, oc=oc, wb=wb: e.dma_start(out=wb[...], in_=WTA[oc]),
              "wq_ld" + str(oc % 2), writes=[wt])
        for th in range(2):
            par = (oc * 2 + th) % 2
            ps = qkv_psA if par == 0 else qkv_psB
            pt = ("qkv_ps", par)
            for fcc in range(FC):
                S.emit("pe", lambda e, th=th, fcc=fcc, ps=ps, wb=wb: e.matmul(
                    ps[...], wb[:, fcc, :], xn_fm[:, fcc, th * 512:(th + 1) * 512],
                    start=(fcc == 0), stop=(fcc == FC - 1)),
                    reads=([wt, "xn_fm"] if fcc in (0, FC - 1) else []),
                    writes=([pt] if fcc in (0, FC - 1) else []),
                    inc=(fcc == FC - 1))
            if oc < NH:
                dsts, dtag = (lambda oc=oc, th=th: Q_all[:, oc, th * 512:(th + 1) * 512]), "Q_all"
            elif oc == NH:
                dsts, dtag = (lambda oc=oc, th=th: K_own[:, th * 512:(th + 1) * 512]), "K_own"
            else:
                dsts, dtag = (lambda oc=oc, th=th: V_fm_stage[:, th * 512:(th + 1) * 512]), "V_fm"
            S.emit("dve", lambda e, dsts=dsts, ps=ps, oc=oc: e.tensor_scalar(
                dsts(), ps[...], battn_sb[:, oc:oc + 1], None, ALU.add),
                reads=[pt, "consts"], writes=[dtag])
    for j in range(NT):
        S.emit("pe", lambda e, j=j: e.transpose(
            vt_ps[...], V_fm_stage[:, j * 128:(j + 1) * 128], ident_sb[...]),
            reads=["V_fm", "consts"], writes=["vt_ps"])
        S.emit("dve", lambda e, j=j: e.tensor_copy(V_own[:, j, :], vt_ps[...]),
               reads=["vt_ps"], writes=["V_own"])
    S.dma("pool", lambda e: e.dma_start(out=kag_in, in_=K_own[...]), "kv_st",
          reads=["K_own"], writes=["kag_in"])
    S.dma("pool", lambda e: e.dma_start(
        out=vag_in, in_=V_own[...].rearrange("p j d -> p (j d)")), "kv_st",
        reads=["V_own"], writes=["vag_in"])
    S.coll(lambda e: e.collective_compute("AllGather", ALU.bypass,
           replica_groups=[list(range(NC))], ins=[kag_in.opt()], outs=[kag_out.opt()]),
           "cc_k", reads=["kag_in"], writes=["kag_out"])
    S.coll(lambda e: e.collective_compute("AllGather", ALU.bypass,
           replica_groups=[list(range(NC))], ins=[vag_in.opt()], outs=[vag_out.opt()]),
           "cc_v", reads=["vag_in"], writes=["vag_out"])
    for r in range(NC):
        S.dma("sp", lambda e, r=r: e.dma_start(
            out=K_sb[:, r, :], in_=kag_out[r * 128:(r + 1) * 128, :]), "ksb_ld",
            reads=["kag_out"])
        S.dma("sp", lambda e, r=r: e.dma_start(
            out=V_sb[:, r, :], in_=vag_out.rearrange("(r p) f -> r p f", p=128)[r]),
            "vsb_ld", reads=["vag_out"])
    S.group_commit("ksb_ld", ["K_sb"])
    S.group_commit("vsb_ld", ["V_sb"])

    if upto <= 2:
        return _finish()

    S.dma("pool", lambda e: e.dma_start(out=cnts_sb[...], in_=cnt_out), "cnts_ld",
          reads=["cnt_out"], writes=["cnts_sb"])
    S.emit("pe", lambda e: e.matmul(base_ps[...], psm_sb[...], cnts_sb[...],
           start=True, stop=True), reads=["consts", "cnts_sb"], writes=["base_ps"])
    S.emit("dve", lambda e: e.tensor_copy(base_sb[...], base_ps[...]),
           reads=["base_ps"], writes=["base_sb"])
    for j in range(NT):
        S.emit("pe", lambda e, j=j: e.matmul(bc_ps[...], ejm_sb[:, j * 128:(j + 1) * 128],
               base_sb[...], start=True, stop=True),
               reads=["consts", "base_sb"], writes=["bc_ps"])
        S.emit("dve", lambda e, j=j: e.tensor_copy(bc_sb[:, j, :], bc_ps[...]),
               reads=["bc_ps"], writes=["bc_sb"])
    def rank_unit0(j):
        S.emit("dve", lambda e, j=j: e.tensor_tensor(rank_all[:, j, :],
               excl_all[:, j, :], bc_sb[:, j, :], ALU.add),
               reads=["excl_all", "bc_sb"], writes=["rank_all"])
        S.emit("dve", lambda e, j=j: e.tensor_scalar(r_kept[...], rank_all[:, j, :],
               1023.5, None, ALU.is_le), reads=["rank_all"], writes=["r_kept"])
        S.emit("dve", lambda e, j=j: e.scalar_tensor_tensor(r_brank[...],
               r_kept[...], -1.0e6, rank_all[:, j, :], ALU.mult, ALU.add),
               reads=["r_kept", "rank_all"], writes=["r_brank"])
        S.emit("dve", lambda e: e.tensor_scalar(r_brank[...], r_brank[...], 1.0e6,
               None, ALU.add), writes=["r_brank"])
        S.emit("dve", lambda e, j=j: e.tensor_tensor(r_wk[...], w_all[:, j, :],
               r_kept[...], ALU.mult), reads=["w_all", "r_kept"],
               writes=["r_wk"])

    def rank_unitk(j, k):
        S.emit("dve", lambda e, j=j, k=k: e.tensor_scalar(r_oh[...], iota_sb[...],
               e4f[:, j, k:k + 1], None, ALU.is_equal),
               reads=["consts", "e4f"], writes=["r_oh"])
        S.emit("dve", lambda e: e.tensor_tensor(r_ex[...], r_oh[...], r_wk[...],
               ALU.mult), reads=["r_oh", "r_wk"], writes=["r_ex"])
        S.emit("dve", lambda e, j=j, k=k: e.tensor_reduce(w4k[:, j, k:k + 1],
               r_ex[...], AX.X, ALU.add), reads=["r_ex"], writes=["w4k"])
        S.emit("dve", lambda e: e.tensor_tensor(r_ex[...], r_oh[...], r_brank[...],
               ALU.mult), reads=["r_oh", "r_brank"], writes=["r_ex"])
        S.emit("dve", lambda e: e.tensor_reduce(r_t1[...], r_ex[...], AX.X, ALU.add),
               reads=["r_ex"], writes=["r_t1"])
        S.emit("dve", lambda e, j=j, k=k: e.scalar_tensor_tensor(r_t1[...],
               e4f[:, j, k:k + 1], 1024.0, r_t1[...], ALU.mult, ALU.add),
               reads=["e4f"], writes=["r_t1"])
        S.emit("dve", lambda e: e.tensor_scalar(r_t1[...], r_t1[...], 0.25,
               None, ALU.add), writes=["r_t1"])
        S.emit("dve", lambda e, j=j, k=k: e.tensor_copy(idxs[:, j, k:k + 1],
               r_t1[...]), reads=["r_t1"], writes=["idxs"])
        S.emit("dve", lambda e: e.tensor_tensor(r_t2[...], r_oh[...],
               vals_sb[...], ALU.mult), reads=["r_oh", "consts"],
               writes=["r_t2x"])
        S.emit("dve", lambda e: e.tensor_reduce(r_t3[...], r_t2[...], AX.X,
               ALU.add), reads=["r_t2x"], writes=["r_t3"])
        S.emit("dve", lambda e, j=j, k=k: e.tensor_tensor(
               idxs2[:, j, k:k + 1], r_t3[...], r_t1[...], ALU.add),
               reads=["r_t3", "r_t1"], writes=["idxs2"])

    rank_units = []
    for j in range(NT):
        rank_units.append(("t", j, 0))
        for k in range(4):
            rank_units.append(("k", j, k))

    if debug:
        S.dma("sp", lambda e: e.dma_start(out=dbg["d_cnt"], in_=cnt_out), "dbg",
              reads=["cnt_out"])
        S.dma("sp", lambda e: e.dma_start(
            out=dbg["d_q"], in_=Q_all[...].rearrange("p h t -> p (h t)")), "dbg",
            reads=["Q_all"])
        S.dma("sp", lambda e: e.dma_start(out=dbg["d_k"], in_=K_own[...]), "dbg",
              reads=["K_own"])

    if upto <= 3:
        return _finish()

    es_p1.close()
    es_r2b.close()
    S.barrier("attn_in", ("pe", "act", "dve"))

    es_p4 = ExitStack()
    s_psA = es_p4.enter_context(nc.psum_tensor("s_psA", [128, 1024], F32))
    s_psB = es_p4.enter_context(nc.psum_tensor("s_psB", [128, 1024], F32))
    ctx_ps = es_p4.enter_context(nc.psum_tensor("ctx_ps", [128, 1024], F32))
    den_ps = es_p4.enter_context(nc.psum_tensor("den_ps", [128, 1024], F32))

    def colof(j, i):
        return 4 * j * (j + 1) + i

    def emit_scores_exp(j, hg, i, ntile):
        diag = (i == ntile)
        par = i % 2
        sp_ = s_psA if par == 0 else s_psB
        spt = ("s_ps", par)
        Ebuf = (E0, E1, E2)[i % 3]
        Et = ("E", i % 3)
        if diag:
            kap = K_own[:, j * 128:(j + 1) * 128]
            vap = V_own[:, j, :]
            ktags = ["K_own", "V_own"]
            biasap = DIAGB
        else:
            R, J = i % 8, i // 8
            kap = K_sb[:, R, J * 128:(J + 1) * 128]
            vap = V_sb[:, R, J * HD:(J + 1) * HD]
            ktags = ["K_sb", "V_sb"]
            biasap = cf_sb[:, colof(j, i):colof(j, i) + 1]
        S.emit("pe", lambda e, kap=kap, j=j, hg=hg, sp_=sp_: e.matmul(
            sp_[:, 0:512], kap,
            Q_all[:, hg * 8:hg * 8 + 4, j * 128:(j + 1) * 128],
            start=True, stop=True),
            reads=ktags + ["Q_all"], writes=[spt], inc=False)
        S.emit("pe", lambda e, kap=kap, j=j, hg=hg, sp_=sp_: e.matmul(
            sp_[:, 512:1024], kap,
            Q_all[:, hg * 8 + 4:hg * 8 + 8, j * 128:(j + 1) * 128],
            start=True, stop=True),
            reads=ktags + ["Q_all"], writes=[spt])
        S.emit("act", lambda e, sp_=sp_, Ebuf=Ebuf, biasap=biasap: e.activation(
            Ebuf[...], sp_[...], AF.Exp, bias=biasap, scale=SCALE),
            reads=[spt, "consts"], writes=[Et])
        srcb, stag = Ebuf, Et
        if diag:
            S.emit("dve", lambda e, Ebuf=Ebuf: e.tensor_tensor(
                Ed[...], Ebuf[...], tri_sb[...], ALU.mult),
                reads=[Et, "consts"], writes=["Ed"])
            srcb, stag = Ed, "Ed"
        return (srcb, stag, vap, ktags, i == 0, diag)

    def emit_den_ctx(v):
        srcb, stag, vap, ktags, first, last = v
        S.emit("pe", lambda e, srcb=srcb, first=first, last=last: e.matmul(
            den_ps[:, 0:512], onesm_sb[...], srcb[:, 0:512],
            start=first, stop=last),
            reads=["consts", stag], writes=["den_ps"], inc=False)
        S.emit("pe", lambda e, srcb=srcb, first=first, last=last: e.matmul(
            den_ps[:, 512:1024], onesm_sb[...], srcb[:, 512:1024],
            start=first, stop=last),
            reads=["consts", stag],
            writes=(["den_ps"] if last else []), inc=last)
        S.emit("pe", lambda e, vap=vap, srcb=srcb, first=first, last=last: e.matmul(
            ctx_ps[:, 0:512], vap, srcb[:, 0:512], start=first, stop=last),
            reads=ktags + [stag], writes=["ctx_ps"], inc=False)
        S.emit("pe", lambda e, vap=vap, srcb=srcb, first=first, last=last: e.matmul(
            ctx_ps[:, 512:1024], vap, srcb[:, 512:1024], start=first, stop=last),
            reads=ktags + [stag],
            writes=(["ctx_ps"] if last else []), inc=True)

    for j in range(NT):
        for hg in range(2):
            ntile = 8 * j + 8
            pending = []
            for i in range(ntile + 1):
                pending.append(emit_scores_exp(j, hg, i, ntile))
                if len(pending) > 2:
                    emit_den_ctx(pending.pop(0))
            for v in pending:
                emit_den_ctx(v)
            S.emit("dve", lambda e: e.reciprocal(rbc_sb[...], den_ps[...]),
                   reads=["den_ps"], writes=["rbc"])
            S.emit("dve", lambda e, j=j, hg=hg: e.tensor_tensor(
                ctx_all[:, hg * 8:(hg + 1) * 8, j * 128:(j + 1) * 128],
                ctx_ps[...].rearrange("p (h q) -> p h q", h=8),
                rbc_sb[...].rearrange("p (h q) -> p h q", h=8), ALU.mult),
                reads=["ctx_ps", "rbc"], writes=["ctx_all"])
            if j * 2 + hg >= 1:
                for _ in range(6):
                    if rank_units:
                        u = rank_units.pop(0)
                        if u[0] == "t":
                            rank_unit0(u[1])
                        else:
                            rank_unitk(u[1], u[2])

    S.dma("pool", lambda e: e.dma_start(out=wpA[:, 0:FC, :], in_=WTP[0]),
          "proj_wl0", writes=[("wp", 0)])
    S.dma("pool", lambda e: e.dma_start(out=wpB[:, 0:FC, :], in_=WTP[1]),
          "proj_wl1", writes=[("wp", 1)])
    S.dma("pool", lambda e: e.dma_start(out=sgA[...], in_=SWG[0]),
          "swg_ld0", writes=[("sg", 0)])
    S.dma("pool", lambda e: e.dma_start(out=suA[...], in_=SWU[0]),
          "swu_ld0", writes=[("su", 0)])
    S.dma("pool", lambda e: e.dma_start(out=sgB[...], in_=SWG[1]),
          "swg_ld1", writes=[("sg", 1)])
    S.dma("pool", lambda e: e.dma_start(out=suB[...], in_=SWU[1]),
          "swu_ld1", writes=[("su", 1)])

    while rank_units:
        u = rank_units.pop(0)
        if u[0] == "t":
            rank_unit0(u[1])
        else:
            rank_unitk(u[1], u[2])

    S.emit("pool", lambda e: e.memset(zero_u32[...], 0), writes=["zero_u32"])
    S.dma("pool", lambda e: e.dma_start(
        out=buf_in.rearrange("(p k) one -> p k one", p=128), in_=zero_u32[...]
        .rearrange("p (k one) -> p k one", one=1)),
        "bufz", reads=["zero_u32"], writes=["buf_in"])
    bnd = {}

    def mkreg(e):
        if "r" not in bnd:
            bnd["r"] = e.to_reg(E * CAP - 1)
            bnd["r2"] = e.to_reg(T - 1)
            bnd["r3"] = e.to_reg(2 * NC * CAP - 1)
    sc_tags = []
    for j in range(NT):
        for k in range(4):
            tg = ("bsc", j, k)
            sc_tags.append(tg)

            def scat(e, j=j, k=k):
                mkreg(e)
                return e.indirect_dma_start(
                    out=buf_in,
                    out_offset=bass.IndirectOffsetOnAxis(ap=idxs[:, j, k:k + 1], axis=0),
                    in_=rowid_sb[:, j:j + 1], in_offset=None,
                    bounds_check=bnd["r"], oob_is_err=False)
            S.dma("pool", scat, "buf_sc", reads=["idxs", "consts", "buf_in"],
                  writes=[tg])
    S.coll(lambda e: e.collective_compute("AllReduce", ALU.add,
           replica_groups=[list(range(NC))], ins=[buf_in.opt()], outs=[buf_out.opt()]),
           "cc_buf", reads=["buf_in"] + sc_tags, writes=["buf_out"])

    for g in range(EL * 8):
        def gmb(e, g=g):
            mkreg(e)
            return e.indirect_dma_start(
                out=mybuf[:, g:g + 1], out_offset=None,
                in_=buf_out,
                in_offset=bass.IndirectOffsetOnAxis(ap=bufidx_sb[:, g:g + 1], axis=0),
                bounds_check=bnd["r"], oob_is_err=False)
        S.dma("pool", gmb, "mybuf_g", reads=["buf_out", "consts"])
    S.group_commit("mybuf_g", ["mybuf"])
    if debug:
        S.dma("sp", lambda e: e.dma_start(
            out=dbg["d_w"], in_=w_all[...].rearrange("p j e -> p (j e)")), "dbg",
            reads=["w_all"])
        S.dma("sp", lambda e: e.dma_start(
            out=dbg["d_rank"], in_=rank_all[...].rearrange("p j e -> p (j e)")), "dbg",
            reads=["rank_all"])
        S.dma("sp", lambda e: e.dma_start(
            out=dbg["d_bc"], in_=bc_sb[...].rearrange("p j e -> p (j e)")), "dbg",
            reads=["bc_sb"])
        S.dma("sp", lambda e: e.dma_start(
            out=dbg["d_excl"], in_=excl_all[...].rearrange("p j e -> p (j e)")), "dbg",
            reads=["excl_all"])
        S.dma("sp", lambda e: e.dma_start(
            out=dbg["d_idx"], in_=idxs[...].rearrange("p j k -> p (j k)")), "dbg",
            reads=["idxs"])
        S.dma("sp", lambda e: e.dma_start(
            out=dbg["d_idx2"], in_=idxs2[...].rearrange("p j k -> p (j k)")), "dbg",
            reads=["idxs2"])
        S.dma("sp", lambda e: e.dma_start(
            out=dbg["d_w4"], in_=w4k[...].rearrange("p j k -> p (j k)")), "dbg",
            reads=["w4k"])
        S.dma("sp", lambda e: e.dma_start(out=dbg["d_buf"], in_=buf_out), "dbg",
              reads=["buf_out"])
    for g in range(EL * 8):
        xg, xt = (xgA, ("xg", 0)) if g % 2 == 0 else (xgB, ("xg", 1))

        def gxg(e, g=g, xg=xg):
            mkreg(e)
            return e.indirect_dma_start(
                out=xg[...], out_offset=None,
                in_=xnag_out,
                in_offset=bass.IndirectOffsetOnAxis(ap=mybuf[:, g:g + 1], axis=0),
                bounds_check=bnd["r2"], oob_is_err=False)
        S.dma("pool", gxg, "xg_g" + str(g % 2), reads=["mybuf", "xnag_out"],
              writes=[xt])
        S.dma("pool", lambda e, g=g, xg=xg: e.dma_start(
            out=xg_dram[g * 128:(g + 1) * 128, :], in_=xg[...]),
            "xg_st", reads=[xt], writes=["xg_dram"])




    if debug:
        S.dma("sp", lambda e: e.dma_start(
            out=dbg["d_ctx"], in_=ctx_all[...].rearrange("p h t -> p (h t)")), "dbg",
            reads=["ctx_all"])

    if upto <= 4:
        return _finish()

    es_p4.close()
    es_r2.close()
    S.barrier("ffn_in", ("pe", "act", "dve", "sp"))

    es_p5 = ExitStack()
    g_psA = es_p5.enter_context(nc.psum_tensor("g_psA", [128, 512], F32))
    g_psB = es_p5.enter_context(nc.psum_tensor("g_psB", [128, 512], F32))
    u_psA = es_p5.enter_context(nc.psum_tensor("u_psA", [128, 512], F32))
    u_psB = es_p5.enter_context(nc.psum_tensor("u_psB", [128, 512], F32))
    tp_psA = es_p5.enter_context(nc.psum_tensor("tp_psA", [128, 1024], BF16))[:, 0:128]
    tp_psB = es_p5.enter_context(nc.psum_tensor("tp_psB", [128, 1024], BF16))[:, 0:128]

    es_r3a = ExitStack()
    stg_all = sb("stg_all", [128, NT, D], BF16, stack=es_r3a, side="right")
    es_r3c = ExitStack()
    bproj_sb = sb("bproj_sb", [1, D], F32, stack=es_r3c, side="right")
    bias_bc = sb("bias_bc", [128, D], F32, stack=es_r3c, side="right")

    S.dma("sp", lambda e: e.dma_start(out=bproj_sb[...], in_=BPROJ), "bp_ld",
          writes=["bproj"])
    for osl in range(4):
        ps, pt = (g_psA, ("g_ps", 0)) if osl % 2 == 0 else (g_psB, ("g_ps", 1))
        S.emit("pe", lambda e, ps=ps, osl=osl: e.matmul(
            ps[...], onesf_sb[...], bproj_sb[:, osl * 512:(osl + 1) * 512],
            start=True, stop=True), reads=["consts", "bproj"], writes=[pt])
        S.emit("act", lambda e, ps=ps, osl=osl: e.activation(
            bias_bc[:, osl * 512:(osl + 1) * 512], ps[...], AF.Copy, bias=0.0,
            scale=1.0), reads=[pt], writes=["bias_bc"])

    def fm_phase(name, noc, nic, slab_fn, rhs_fn, rtags, nth=2, prefetched=0):
        post_q = []

        def flush_post():
            fmsb, ft, oc, th = post_q.pop(0)
            for tt in range(4):
                tp, tpt = (tp_psA, ("tp", 0)) if tt % 2 == 0 else (tp_psB, ("tp", 1))
                ttg = th * 4 + tt
                S.emit("pe", lambda e, tp=tp, fmsb=fmsb, tt=tt: e.transpose(
                    tp[...], fmsb[:, tt * 128:(tt + 1) * 128], ident_sb[...]),
                    reads=[ft, "consts"], writes=[tpt])
                S.emit("act", lambda e, tp=tp, oc=oc, ttg=ttg: e.activation(
                    stg_all[:, ttg, oc * 128:(oc + 1) * 128], tp[...], AF.Copy,
                    bias=0.0, scale=1.0),
                    reads=[tpt], writes=[("stg", ttg, oc)])

        for oc in range(noc):
            wb, wt = (wpA, ("wp", 0)) if oc % 2 == 0 else (wpB, ("wp", 1))
            if oc >= prefetched:
                S.dma("pool", lambda e, oc=oc, wb=wb: slab_fn(e, wb, oc),
                      name + "_wl" + str(oc % 2), writes=[wt])
            for th in range(nth):
                par = (oc * nth + th) % 2
                ps = g_psA if par == 0 else g_psB
                pt = ("g_ps", par)
                for ic in range(nic):
                    S.emit("pe", lambda e, ps=ps, wb=wb, ic=ic, th=th: e.matmul(
                        ps[...], wb[:, ic, :], rhs_fn(ic, th),
                        start=(ic == 0), stop=(ic == nic - 1)),
                        reads=([wt] + rtags if ic in (0, nic - 1) else []),
                        writes=([pt] if ic in (0, nic - 1) else []),
                        inc=(ic == nic - 1))
                fmsb, ft = (fmsbA, ("fmsb", 0)) if par == 0 else (fmsbB, ("fmsb", 1))
                S.emit("dve", lambda e, fmsb=fmsb, ps=ps: e.tensor_copy(
                    fmsb[...], ps[...]), reads=[pt], writes=[ft])
                post_q.append((fmsb, ft, oc, th))
                if len(post_q) > 1:
                    flush_post()
        while post_q:
            flush_post()

    fm_phase("proj", OC, FC,
             lambda e, wb, oc: e.dma_start(out=wb[:, 0:FC, :], in_=WTP[oc]),
             lambda ic, th: ctx_all[:, ic, th * 512:(th + 1) * 512],
             ["ctx_all"], prefetched=2)
    for ttg in range(NT):
        if sub <= 3:
            break
        S.dma("pool", lambda e, ttg=ttg: e.dma_start(
            out=acc_dram[ttg * 128:(ttg + 1) * 128, :], in_=stg_all[:, ttg, :],
            max_dma_last_dim=2048),
            "acc_proj", reads=[("stg", ttg, oc2) for oc2 in range(OC)],
            writes=[("acc", ttg)])
        if sub <= 4:
            continue
        S.dma("pool", lambda e, ttg=ttg: e.dma_start(
            out=acc_dram[ttg * 128:(ttg + 1) * 128, :],
            in_=X[ttg * 128:(ttg + 1) * 128, :], accum_op=ALU.add,
            max_dma_last_dim=4096),
            "acc_x", reads=[("acc", ttg)], writes=[("acc", ttg)])
        S.dma("pool", lambda e, ttg=ttg: e.dma_start(
            out=acc_dram[ttg * 128:(ttg + 1) * 128, :], in_=bias_bc[...],
            accum_op=ALU.add, max_dma_last_dim=4096),
            "acc_b", reads=[("acc", ttg), "bias_bc"], writes=[("acc", ttg)])

    es_ctx.close()
    if upto <= 5:
        return _finish()

    es_r3b = ExitStack()
    mid_sb = sb("mid_sb", [128, ICS, TL], BF16, stack=es_r3b, side="right")

    for ic in range(ICS):
        gb, gbt = (sgA, ("sg", 0)) if ic % 2 == 0 else (sgB, ("sg", 1))
        ub, ubt = (suA, ("su", 0)) if ic % 2 == 0 else (suB, ("su", 1))
        if ic >= 2:
            S.dma("pool", lambda e, ic=ic, gb=gb: e.dma_start(out=gb[...], in_=SWG[ic]),
                  "swg_ld" + str(ic % 2), writes=[gbt])
            S.dma("pool", lambda e, ic=ic, ub=ub: e.dma_start(out=ub[...], in_=SWU[ic]),
                  "swu_ld" + str(ic % 2), writes=[ubt])
        for th in range(2):
            gps, gpt = (g_psA, ("g_ps", 0)) if th == 0 else (g_psB, ("g_ps", 1))
            ups, upt = (u_psA, ("u_ps", 0)) if th == 0 else (u_psB, ("u_ps", 1))
            for fcc in range(FC):
                S.emit("pe", lambda e, gps=gps, gb=gb, fcc=fcc, th=th: e.matmul(
                    gps[...], gb[:, fcc, :], xn_fm[:, fcc, th * 512:(th + 1) * 512],
                    start=(fcc == 0), stop=(fcc == FC - 1)),
                    reads=([gbt, "xn_fm"] if fcc in (0, FC - 1) else []),
                    writes=([("g_ps", th)] if fcc in (0, FC - 1) else []),
                    inc=(fcc == FC - 1))
            for fcc in range(FC):
                S.emit("pe", lambda e, ups=ups, ub=ub, fcc=fcc, th=th: e.matmul(
                    ups[...], ub[:, fcc, :], xn_fm[:, fcc, th * 512:(th + 1) * 512],
                    start=(fcc == 0), stop=(fcc == FC - 1)),
                    reads=([ubt, "xn_fm"] if fcc in (0, FC - 1) else []),
                    writes=([("u_ps", th)] if fcc in (0, FC - 1) else []),
                    inc=(fcc == FC - 1))
            S.emit("act", lambda e, gps=gps: e.activation(sil_sb[...], gps[...],
                   AF.Silu, bias=ZCOL, scale=1.0),
                   reads=[gpt, "consts"], writes=["sil"])
            S.emit("dve", lambda e, ic=ic, th=th, ups=ups: e.tensor_tensor(
                mid_sb[:, ic, th * 512:(th + 1) * 512], sil_sb[...], ups[...],
                ALU.mult), reads=["sil", upt], writes=[("mid", ic)])

    fm_phase("shd", OC, ICS,
             lambda e, wb, oc: e.dma_start(out=wb[...], in_=SWD[oc]),
             lambda ic, th: mid_sb[:, ic, th * 512:(th + 1) * 512],
             [("mid", icx) for icx in range(ICS)])
    for ttg in range(NT):
        S.dma("pool", lambda e, ttg=ttg: e.dma_start(
            out=acc_dram[ttg * 128:(ttg + 1) * 128, :], in_=stg_all[:, ttg, :],
            accum_op=ALU.add, max_dma_last_dim=2048),
            "acc_sh", reads=[("stg", ttg, oc2) for oc2 in range(OC)] + [("acc", ttg)],
            writes=[("acc", ttg)])

    if upto <= 6:
        return _finish()

    es_r3b.close()
    es_r3c.close()
    S.barrier("exp_in")

    es_r4 = ExitStack()
    xgfmA = sb("xgfmA", [128, FC, 512], BF16, stack=es_r4, side="right")
    xgfmB = sb("xgfmB", [128, FC, 512], BF16, stack=es_r4, side="right")
    emid = sb("emid", [128, ICE, 512], BF16, stack=es_r4, side="right")
    wgA_ = sb("wgA_", [128, FC, 128], BF16, stack=es_r4, side="right")
    wgB_ = sb("wgB_", [128, FC, 128], BF16, stack=es_r4, side="right")
    wuA_ = sb("wuA_", [128, FC, 128], BF16, stack=es_r4, side="right")
    wuB_ = sb("wuB_", [128, FC, 128], BF16, stack=es_r4, side="right")
    wdA_ = sb("wdA_", [128, ICE, 128], BF16, stack=es_r4, side="right")
    wdB_ = sb("wdB_", [128, ICE, 128], BF16, stack=es_r4, side="right")

    if debug:
        S.dma("sp", lambda e: e.dma_start(out=dbg["d_xg"], in_=xg_dram), "dbg",
              reads=["xg_dram"])

    for el in range(EL):
        for sc in range(2):
            ch = el * 2 + sc
            xgfm = xgfmA if ch % 2 == 0 else xgfmB
            xft = ("xgfm", ch % 2)
            row0 = el * CAP + sc * 512
            for fcc in range(FC):
                S.dma("sp", lambda e, fcc=fcc, row0=row0, xgfm=xgfm: e.dma_start_transpose(
                    xgfm[:, fcc, :], xg_dram[row0:row0 + 512, fcc * 128:(fcc + 1) * 128]),
                    "xgfm_ld" + str(ch % 2), reads=["xg_dram"],
                    writes=([xft] if fcc == 0 else []))
            S.group_commit("xgfm_ld" + str(ch % 2), [xft])
            for ic in range(ICE):
                par = ic % 2
                gb, gbt = (wgA_, ("ewg", 0)) if par == 0 else (wgB_, ("ewg", 1))
                ub, ubt = (wuA_, ("ewu", 0)) if par == 0 else (wuB_, ("ewu", 1))
                S.dma("pool", lambda e, el=el, ic=ic, gb=gb: e.dma_start(
                    out=gb[...], in_=WG[el, ic]), "ewg_ld" + str(par), writes=[gbt])
                S.dma("pool", lambda e, el=el, ic=ic, ub=ub: e.dma_start(
                    out=ub[...], in_=WU[el, ic]), "ewu_ld" + str(par), writes=[ubt])
                gps, gpt = (g_psA, ("g_ps", 0)) if par == 0 else (g_psB, ("g_ps", 1))
                ups, upt = (u_psA, ("u_ps", 0)) if par == 0 else (u_psB, ("u_ps", 1))
                for fcc in range(FC):
                    S.emit("pe", lambda e, gps=gps, gb=gb, fcc=fcc, xgfm=xgfm: e.matmul(
                        gps[...], gb[:, fcc, :], xgfm[:, fcc, :],
                        start=(fcc == 0), stop=(fcc == FC - 1)),
                        reads=([gbt, xft] if fcc in (0, FC - 1) else []),
                        writes=([gpt] if fcc in (0, FC - 1) else []),
                        inc=(fcc == FC - 1))
                for fcc in range(FC):
                    S.emit("pe", lambda e, ups=ups, ub=ub, fcc=fcc, xgfm=xgfm: e.matmul(
                        ups[...], ub[:, fcc, :], xgfm[:, fcc, :],
                        start=(fcc == 0), stop=(fcc == FC - 1)),
                        reads=([ubt, xft] if fcc in (0, FC - 1) else []),
                        writes=([upt] if fcc in (0, FC - 1) else []),
                        inc=(fcc == FC - 1))
                S.emit("act", lambda e, gps=gps: e.activation(sil_sb[...], gps[...],
                       AF.Silu, bias=ZCOL, scale=1.0),
                       reads=[gpt, "consts"], writes=["sil"])
                S.emit("dve", lambda e, ic=ic, ups=ups: e.tensor_tensor(
                    emid[:, ic, :], sil_sb[...], ups[...], ALU.mult),
                    reads=["sil", upt], writes=[("emid", ic)])
            epost = []

            def eflush():
                fmsb, ft, oc = epost.pop(0)
                for tt in range(4):
                    tp, tpt = (tp_psA, ("tp", 0)) if tt % 2 == 0 else (tp_psB, ("tp", 1))
                    S.emit("pe", lambda e, tp=tp, fmsb=fmsb, tt=tt: e.transpose(
                        tp[...], fmsb[:, tt * 128:(tt + 1) * 128], ident_sb[...]),
                        reads=[ft, "consts"], writes=[tpt])
                    S.emit("act", lambda e, tp=tp, oc=oc, tt=tt: e.activation(
                        stg_all[:, tt, oc * 128:(oc + 1) * 128], tp[...], AF.Copy,
                        bias=0.0, scale=1.0),
                        reads=[tpt], writes=[("stg", tt, oc)])

            for oc in range(OC):
                par = oc % 2
                db, dbt = (wdA_, ("ewd", 0)) if par == 0 else (wdB_, ("ewd", 1))
                S.dma("pool", lambda e, el=el, oc=oc, db=db: e.dma_start(
                    out=db[...], in_=WD[el, oc]), "ewd_ld" + str(par), writes=[dbt])
                fps, fpt = (g_psA, ("g_ps", 0)) if par == 0 else (g_psB, ("g_ps", 1))
                for ic in range(ICE):
                    S.emit("pe", lambda e, fps=fps, db=db, ic=ic: e.matmul(
                        fps[...], db[:, ic, :], emid[:, ic, :],
                        start=(ic == 0), stop=(ic == ICE - 1)),
                        reads=([dbt] + [("emid", icx) for icx in range(ICE)]
                               if ic in (0, ICE - 1) else []),
                        writes=([fpt] if ic in (0, ICE - 1) else []),
                        inc=(ic == ICE - 1))
                fmsb, ft = (fmsbA, ("fmsb", 0)) if par == 0 else (fmsbB, ("fmsb", 1))
                S.emit("dve", lambda e, fmsb=fmsb, fps=fps: e.tensor_copy(
                    fmsb[...], fps[...]), reads=[fpt], writes=[ft])
                epost.append((fmsb, ft, oc))
                if len(epost) > 1:
                    eflush()
            while epost:
                eflush()
            for tt in range(4):
                S.dma("pool", lambda e, tt=tt, row0=row0: e.dma_start(
                    out=fout_in[row0 + tt * 128:row0 + (tt + 1) * 128, :],
                    in_=stg_all[:, tt, :]), "fout_st",
                    reads=[("stg", tt, oc2) for oc2 in range(OC)],
                    writes=[("fout_in", el)])
        S.coll(lambda e, el=el: e.collective_compute(
            "AllGather", ALU.bypass, replica_groups=[list(range(NC))],
            ins=[fout_in[el * CAP:(el + 1) * CAP, :].opt()],
            outs=[fout_ag2[el].opt()]),
            "cc_fout", reads=[("fout_in", el)], writes=[("fout_ag", el)])

    if debug:
        S.dma("sp", lambda e: e.dma_start(out=dbg["d_fout"], in_=fout_in), "dbg",
              reads=[("fout_in", 0), ("fout_in", 1)])
        S.dma("sp", lambda e: e.dma_start(
            out=dbg["d_fag"], in_=fout_ag2.rearrange("a r d -> (a r) d")), "dbg",
            reads=[("fout_ag", 0), ("fout_ag", 1)])
        S.dma("sp", lambda e: e.dma_start(out=dbg["d_acc"], in_=acc_dram), "dbg",
              reads=[("acc", ttg) for ttg in range(NT)])

    if upto <= 7:
        return _finish()

    es_r4.close()
    es_r3a.close()
    S.barrier("asm_in")

    es_r5 = ExitStack()
    moe_g = [[sb(f"moe_g{k}_{p}", [128, D], BF16, stack=es_r5, side="right")
              for p in range(2)] for k in range(4)]
    acc_sbA = sb("acc_sbA", [128, D], F32, stack=es_r5, side="right")
    acc_sbB = sb("acc_sbB", [128, D], F32, stack=es_r5, side="right")
    out_sbA = sb("out_sbA", [128, D], F32, stack=es_r5, side="right")
    out_sbB = sb("out_sbB", [128, D], F32, stack=es_r5, side="right")

    fout_flat = fout_ag2.rearrange("a r d -> (a r) d")
    for k in range(4):
        for p in range(2):
            S.emit("dve", lambda e, k=k, p=p: e.memset(moe_g[k][p][...], 0.0),
                   writes=[("moe", k, p)])

    for j in range(NT):
        p = j % 2
        acc, at = (acc_sbA, ("acc_sb", 0)) if p == 0 else (acc_sbB, ("acc_sb", 1))
        out, ot = (out_sbA, ("out_sb", 0)) if p == 0 else (out_sbB, ("out_sb", 1))
        S.dma("sp", lambda e, j=j, acc=acc: e.dma_start(
            out=acc[...], in_=acc_dram[j * 128:(j + 1) * 128, :]),
            "acc_ld" + str(p), reads=[("acc", j)], writes=[at])
        for k in range(4):
            def gmo(e, j=j, k=k, p=p):
                mkreg(e)
                return e.indirect_dma_start(
                    out=moe_g[k][p][...], out_offset=None,
                    in_=fout_flat,
                    in_offset=bass.IndirectOffsetOnAxis(ap=idxs2[:, j, k:k + 1], axis=0),
                    bounds_check=bnd["r3"], oob_is_err=False)
            S.dma("pool", gmo, "moe_g" + str(k) + str(p),
                  reads=[("fout_ag", 0), ("fout_ag", 1), "idxs2"],
                  writes=[("moe", k, p)])
        S.emit("dve", lambda e, j=j, acc=acc, out=out, p=p: e.scalar_tensor_tensor(
            out[...], moe_g[0][p][...], w4k[:, j, 0:1], acc[...],
            ALU.mult, ALU.add),
            reads=[("moe", 0, p), "w4k", at], writes=[ot])
        for k in range(1, 4):
            S.emit("dve", lambda e, j=j, k=k, out=out, p=p: e.scalar_tensor_tensor(
                out[...], moe_g[k][p][...], w4k[:, j, k:k + 1], out[...],
                ALU.mult, ALU.add),
                reads=[("moe", k, p), "w4k"], writes=[ot])
        S.dma("sp", lambda e, j=j, out=out: e.dma_start(
            out=Y[j * 128:(j + 1) * 128, :], in_=out[...]), "y_st",
            reads=[ot], writes=[("y", j)])
    S.final_wait("sp", "y_st")
    if debug:
        S.final_wait("sp", "dbg")

    S.verify()
    with nc.Block() as block:
        S.replay(block)
    es_r5.close()
    es_p5.close()
    es.close()
    return nc, dbg


def _host_prep(c, inputs):
    x = np.asarray(inputs["L0_attn_hidden_states"]).reshape(T, D)
    caw = np.asarray(inputs["L0_attn_c_attn_w"])
    cab = np.asarray(inputs["L0_attn_c_attn_b"])
    cpw = np.asarray(inputs["L0_attn_c_proj_w"])
    cpb = np.asarray(inputs["L0_attn_c_proj_b"])
    gw = np.asarray(inputs["L0_ffn_gate_w"])
    wg = np.asarray(inputs["L0_ffn_w_gate"])
    wu = np.asarray(inputs["L0_ffn_w_up"])
    wd = np.asarray(inputs["L0_ffn_w_down"])
    swg = np.asarray(inputs["L0_ffn_sw_gate"])
    swu = np.asarray(inputs["L0_ffn_sw_up"])
    swd = np.asarray(inputs["L0_ffn_sw_down"])

    blocks = np.arange(NT) * NC + c
    m = {}
    m["x"] = np.ascontiguousarray(x.reshape(64, 128, D)[blocks].reshape(TL, D))
    m["wta"] = np.ascontiguousarray(caw.reshape(OCQ, 128, FC, 128).transpose(0, 3, 2, 1))
    m["wtp"] = np.ascontiguousarray(cpw.reshape(OC, 128, FC, 128).transpose(0, 3, 2, 1))
    m["gt"] = np.ascontiguousarray(gw.reshape(E, FC, 128).transpose(2, 1, 0))
    m["battn"] = np.ascontiguousarray(cab.reshape(OCQ, 128).T)
    m["bproj"] = np.ascontiguousarray(cpb.reshape(1, D))
    m["swg"] = np.ascontiguousarray(swg.reshape(FC, 128, ICS, 128).transpose(2, 1, 0, 3))
    m["swu"] = np.ascontiguousarray(swu.reshape(FC, 128, ICS, 128).transpose(2, 1, 0, 3))
    m["swd"] = np.ascontiguousarray(swd.reshape(ICS, 128, OC, 128).transpose(2, 1, 0, 3))
    m["wg"] = np.ascontiguousarray(
        wg[2 * c:2 * c + 2].reshape(EL, FC, 128, ICE, 128).transpose(0, 3, 2, 1, 4))
    m["wu"] = np.ascontiguousarray(
        wu[2 * c:2 * c + 2].reshape(EL, FC, 128, ICE, 128).transpose(0, 3, 2, 1, 4))
    m["wd"] = np.ascontiguousarray(
        wd[2 * c:2 * c + 2].reshape(EL, ICE, 128, OC, 128).transpose(0, 3, 2, 1, 4))

    tri = np.triu(np.ones((128, 128), np.float32))
    m["tri"] = np.ascontiguousarray(np.tile(tri, (1, 8))).astype(ml_dtypes.bfloat16)
    m["ltri"] = np.ascontiguousarray(
        np.tril(np.ones((128, 128), np.float32), -1).T).astype(ml_dtypes.bfloat16)
    m["ident"] = np.eye(128, dtype=ml_dtypes.bfloat16)
    m["onesb"] = np.ones((128, 1), ml_dtypes.bfloat16)
    m["onesm"] = np.ones((128, 128), ml_dtypes.bfloat16)
    m["onesf"] = np.ones((1, 128), np.float32)

    cfv = np.zeros((128, NMAIN + 3), np.float32)
    for j in range(NT):
        for i in range(8 * j + 8):
            cfv[:, 4 * j * (j + 1) + i] = -12.0 if i < 8 * j + c else -112.0
    cfv[:, NMAIN] = -12.0
    cfv[:, NMAIN + 1] = 1e-5
    cfv[:, NMAIN + 2] = 0.0
    m["cf"] = cfv

    psm = np.zeros((64, NT), np.float32)
    for im in range(64):
        R, jj = im // NT, im % NT
        g = NC * jj + R
        for j in range(NT):
            if g < NC * j + c:
                psm[im, j] = 1.0
    m["psm"] = psm.astype(ml_dtypes.bfloat16)

    ejm = np.zeros((NT, NT * 128), np.float32)
    for j in range(NT):
        ejm[j, j * 128:(j + 1) * 128] = 1.0
    m["ejm"] = ejm
    m["iota16"] = np.tile(np.arange(E, dtype=np.float32), (128, 1))
    ev = np.arange(E)
    v16 = (ev % 2) * 8192 + (ev // 2) * 1024 - ev * 1024
    m["vals16"] = np.tile(v16.astype(np.float32), (128, 1))
    rid = np.zeros((128, NT), np.uint32)
    for j in range(NT):
        rid[:, j] = c * TL + j * 128 + np.arange(128)
    m["rowid"] = rid
    bix = np.zeros((128, EL * 8), np.uint32)
    for g in range(EL * 8):
        bix[:, g] = 2048 * c + g * 128 + np.arange(128)
    m["bufidx"] = bix
    return m


_CACHE = {}


def kernel(**inputs):
    if "nc" not in _CACHE:
        _CACHE["nc"] = build(debug=False)[0]
    nc = _CACHE["nc"]
    in_maps = [_host_prep(c, inputs) for c in range(NC)]
    res = run_bass_kernel_spmd(nc, in_maps, core_ids=list(range(NC)))
    out = np.zeros((1, T, D), np.float32)
    for c in range(NC):
        y = res.results[c]["y"]
        blocks = np.arange(NT) * NC + c
        out[0].reshape(64, 128, D)[blocks] = y.reshape(NT, 128, D)
    return out


# version 09
# speedup vs baseline: 1.0682x; 1.0037x over previous
import numpy as np
import ml_dtypes
from contextlib import ExitStack
import concourse.bass as bass
import concourse.mybir as mybir
from concourse.bass_utils import run_bass_kernel_spmd

F32 = mybir.dt.float32
BF16 = mybir.dt.bfloat16
U32 = mybir.dt.uint32
AF = mybir.ActivationFunctionType
ALU = mybir.AluOpType
AX = mybir.AxisListType

T = 8192
D = 2048
NC = 8
TL = 1024
NT = 8
FC = 16
NH = 16
HD = 128
OCQ = 18
E = 16
EL = 2
CAP = 1024
ICE = 11
ICS = 22
OC = 16
SCALE = 1.0 / float(np.sqrt(128.0))
NMAIN = 288

ENGINES = ("pe", "act", "dve", "pool", "sp")


class Sched:

    def __init__(self, nc, es):
        self.nc = nc
        self.es = es
        self.cmds = {e: [] for e in ENGINES}
        self.semh = {}
        self.semcnt = {}
        self.last_write = {}
        self.readers = {}
        self.waited = {e: {} for e in ENGINES}
        self.esem = {e: self.sem("S_" + e) for e in ENGINES}

    def sem(self, name):
        if name not in self.semh:
            self.semh[name] = self.es.enter_context(self.nc.semaphore(name))
            self.semcnt[name] = 0
        return name

    def _wait(self, eng, semname, val):
        if val <= 0 or self.waited[eng].get(semname, 0) >= val:
            return
        self.waited[eng][semname] = val
        self.cmds[eng].append(("wait", semname, val))

    def _dep_waits(self, eng, reads, writes):
        for tag in reads:
            lw = self.last_write.get(tag)
            if lw is not None:
                self._wait(eng, lw[0], lw[1])
        for tag in writes:
            lw = self.last_write.get(tag)
            if lw is not None:
                self._wait(eng, lw[0], lw[1])
            for (s, v) in self.readers.get(tag, ()):
                self._wait(eng, s, v)

    def _record(self, semname, val, reads, writes):
        for tag in writes:
            self.last_write[tag] = (semname, val)
            self.readers[tag] = []
        for tag in reads:
            self.readers.setdefault(tag, []).append((semname, val))

    def emit(self, eng, fn, reads=(), writes=(), inc=True):
        self._dep_waits(eng, reads, writes)
        if inc:
            s = self.esem[eng]
            self.semcnt[s] += 1
            self.cmds[eng].append(("emit", fn, s, 1))
            self._record(s, self.semcnt[s], reads, writes)
        else:
            self.cmds[eng].append(("emit", fn, None, 0))

    def dma(self, eng, fn, sem, reads=(), writes=()):
        self.sem(sem)
        self._dep_waits(eng, reads, writes)
        self.semcnt[sem] += 16
        self.cmds[eng].append(("emit", fn, sem, 16))
        self._record(sem, self.semcnt[sem], reads, writes)

    def coll(self, fn, sem, reads=(), writes=()):
        self.sem(sem)
        self._dep_waits("pool", reads, writes)
        self.semcnt[sem] += 1
        self.cmds["pool"].append(("emit", fn, sem, 1))
        self._record(sem, self.semcnt[sem], reads, writes)

    def group_commit(self, sem, tags):
        tot = self.semcnt[sem]
        for tag in tags:
            self.last_write[tag] = (sem, tot)
            self.readers[tag] = []

    def barrier(self, name, engines=ENGINES):
        s = self.sem("B_" + name)
        for e in engines:
            self.semcnt[s] += 1
            self.cmds[e].append(("emit", None, s, 1))
        tot = self.semcnt[s]
        for e in engines:
            self._wait(e, s, tot)

    def final_wait(self, eng, sem):
        self.cmds[eng].append(("wait", sem, self.semcnt[sem]))

    def verify(self):
        pos = {e: 0 for e in ENGINES}
        cnt = {s: 0 for s in self.semh}
        progress = True
        while progress:
            progress = False
            for e in ENGINES:
                while pos[e] < len(self.cmds[e]):
                    c = self.cmds[e][pos[e]]
                    if c[0] == "wait":
                        if cnt[c[1]] < c[2]:
                            break
                    elif c[2] is not None:
                        cnt[c[2]] += c[3]
                    pos[e] += 1
                    progress = True
        stuck = {}
        for e in ENGINES:
            if pos[e] < len(self.cmds[e]):
                c = self.cmds[e][pos[e]]
                stuck[e] = f"cmd {pos[e]}/{len(self.cmds[e])}: wait {c[1]}>={c[2]} (at {cnt.get(c[1])})"
        if stuck:
            raise RuntimeError(f"DEADLOCK in schedule: {stuck}")

    def replay(self, block):
        proxies = {"pe": block.tensor, "act": block.scalar, "dve": block.vector,
                   "pool": block.gpsimd, "sp": block.sync}

        def mk(cmds):
            def body(eng):
                for c in cmds:
                    if c[0] == "wait":
                        eng.wait_ge(self.semh[c[1]], c[2])
                    else:
                        _, fn, s, amt = c
                        if fn is None:
                            eng.nop().then_inc(self.semh[s], amt)
                        else:
                            inst = fn(eng)
                            if s is not None:
                                inst.then_inc(self.semh[s], amt)
            return body
        for e in ENGINES:
            proxies[e](mk(self.cmds[e]))


def build(debug=False, upto=99, sub=99):
    nc = bass.Bass("TRN2", target_bir_lowering=False, debug=False)
    es = ExitStack()

    X = nc.dram_tensor("x", [TL, D], F32, kind="ExternalInput").ap()
    WTA = nc.dram_tensor("wta", [OCQ, 128, FC, 128], F32, kind="ExternalInput").ap()
    WTP = nc.dram_tensor("wtp", [OC, 128, FC, 128], F32, kind="ExternalInput").ap()
    GT = nc.dram_tensor("gt", [128, FC, E], F32, kind="ExternalInput").ap()
    BATTN = nc.dram_tensor("battn", [128, OCQ], F32, kind="ExternalInput").ap()
    BPROJ = nc.dram_tensor("bproj", [1, D], F32, kind="ExternalInput").ap()
    SWG = nc.dram_tensor("swg", [ICS, 128, FC, 128], F32, kind="ExternalInput").ap()
    SWU = nc.dram_tensor("swu", [ICS, 128, FC, 128], F32, kind="ExternalInput").ap()
    SWD = nc.dram_tensor("swd", [OC, 128, ICS, 128], F32, kind="ExternalInput").ap()
    WG = nc.dram_tensor("wg", [EL, ICE, 128, FC, 128], F32, kind="ExternalInput").ap()
    WU = nc.dram_tensor("wu", [EL, ICE, 128, FC, 128], F32, kind="ExternalInput").ap()
    WD = nc.dram_tensor("wd", [EL, OC, 128, ICE, 128], F32, kind="ExternalInput").ap()
    TRI = nc.dram_tensor("tri", [128, 8 * 128], BF16, kind="ExternalInput").ap()
    LTRI = nc.dram_tensor("ltri", [128, 128], BF16, kind="ExternalInput").ap()
    IDENT = nc.dram_tensor("ident", [128, 128], BF16, kind="ExternalInput").ap()
    ONESB = nc.dram_tensor("onesb", [128, 1], BF16, kind="ExternalInput").ap()
    ONESM = nc.dram_tensor("onesm", [128, 128], BF16, kind="ExternalInput").ap()
    ONESF = nc.dram_tensor("onesf", [1, 128], F32, kind="ExternalInput").ap()
    CF = nc.dram_tensor("cf", [128, NMAIN + 3], F32, kind="ExternalInput").ap()
    PSM = nc.dram_tensor("psm", [64, NT], BF16, kind="ExternalInput").ap()
    EJM = nc.dram_tensor("ejm", [NT, NT * 128], F32, kind="ExternalInput").ap()
    IOTA16 = nc.dram_tensor("iota16", [128, E], F32, kind="ExternalInput").ap()
    VALS16 = nc.dram_tensor("vals16", [128, E], F32, kind="ExternalInput").ap()
    ROWID = nc.dram_tensor("rowid", [128, NT], U32, kind="ExternalInput").ap()
    BUFIDX = nc.dram_tensor("bufidx", [128, EL * 8], U32, kind="ExternalInput").ap()
    Y = nc.dram_tensor("y", [TL, D], F32, kind="ExternalOutput").ap()

    xnag_in = nc.dram_tensor("xnag_in", [TL, D], BF16).ap()
    xnag_out = nc.dram_tensor("xnag_out", [T, D], BF16).ap()
    kag_in = nc.dram_tensor("kag_in", [128, TL], BF16).ap()
    kag_out = nc.dram_tensor("kag_out", [NC * 128, TL], BF16).ap()
    vag_in = nc.dram_tensor("vag_in", [128, NT * HD], BF16).ap()
    vag_out = nc.dram_tensor("vag_out", [NC * 128, NT * HD], BF16).ap()
    cnt_in = nc.dram_tensor("cnt_in", [NT, E], F32).ap()
    cnt_out = nc.dram_tensor("cnt_out", [64, E], F32).ap()
    buf_in = nc.dram_tensor("buf_in", [E * CAP, 1], U32).ap()
    buf_out = nc.dram_tensor("buf_out", [E * CAP, 1], U32).ap()
    acc_dram = nc.dram_tensor("acc_dram", [TL, D], F32).ap()
    xg_dram = nc.dram_tensor("xg_dram", [EL * CAP, D], BF16).ap()
    fout_in = nc.dram_tensor("fout_in", [EL * CAP, D], BF16).ap()
    fout_ag2 = nc.dram_tensor("fout_ag2", [2, NC * CAP, D], BF16).ap()

    dbg = {}
    if debug:
        for nm, shp, dt in [
            ("d_cnt", [64, E], F32), ("d_w", [128, NT * E], F32),
            ("d_rank", [128, NT * E], F32), ("d_idx", [128, NT * 4], U32),
            ("d_w4", [128, NT * 4], F32), ("d_buf", [E * CAP, 1], U32),
            ("d_q", [128, NH * TL], BF16), ("d_k", [128, TL], BF16),
            ("d_ctx", [128, NH * TL], BF16), ("d_acc", [TL, D], F32),
            ("d_xg", [EL * CAP, D], BF16), ("d_fout", [EL * CAP, D], BF16),
            ("d_idx2", [128, NT * 4], U32), ("d_fag", [2 * NC * CAP, D], BF16),
            ("d_bc", [128, NT * E], F32), ("d_excl", [128, NT * E], F32),
        ]:
            dbg[nm] = nc.dram_tensor(nm, shp, dt, kind="ExternalOutput").ap()

    S = Sched(nc, es)

    def _finish():
        S.verify()
        with nc.Block() as block:
            S.replay(block)
        return nc, dbg

    def sb(name, shape, dt, stack=es, side=None):
        kw = {} if side is None else {"side": side}
        return stack.enter_context(nc.sbuf_tensor(name, shape, dt, **kw))

    tri_sb = sb("tri_sb", [128, 8 * 128], BF16)
    ltri_sb = sb("ltri_sb", [128, 128], BF16)
    ident_sb = sb("ident_sb", [128, 128], BF16)
    onesb_sb = sb("onesb_sb", [128, 1], BF16)
    onesm_sb = sb("onesm_sb", [128, 128], BF16)
    onesf_sb = sb("onesf_sb", [1, 128], F32)
    cf_sb = sb("cf_sb", [128, NMAIN + 3], F32)
    psm_sb = sb("psm_sb", [64, NT], BF16)
    ejm_sb = sb("ejm_sb", [NT, NT * 128], F32)
    iota_sb = sb("iota_sb", [128, E], F32)
    vals_sb = sb("vals_sb", [128, E], F32)
    rowid_sb = sb("rowid_sb", [128, NT], U32)
    bufidx_sb = sb("bufidx_sb", [128, EL * 8], U32)
    gt_sb = sb("gt_sb", [128, FC, E], BF16)
    battn_sb = sb("battn_sb", [128, OCQ], F32)
    xn_fm = sb("xn_fm", [128, FC, TL], BF16)
    w_all = sb("w_all", [128, NT, E], F32)
    excl_all = sb("excl_all", [128, NT, E], F32)
    rank_all = sb("rank_all", [128, NT, E], F32)
    e4f = sb("e4f", [128, NT, 4], F32)
    w4k = sb("w4k", [128, NT, 4], F32)
    idxs = sb("idxs", [128, NT, 4], U32)
    idxs2 = sb("idxs2", [128, NT, 4], U32)
    r_t2 = sb("r_t2", [128, E], F32)
    r_t3 = sb("r_t3", [128, 1], F32)
    cntstage = sb("cntstage", [1, NT, E], F32)
    cnts_sb = sb("cnts_sb", [64, E], BF16)
    base_sb = sb("base_sb", [NT, E], F32)
    bc_sb = sb("bc_sb", [128, NT, E], F32)
    stat_sb = sb("stat_sb", [128, 4, 6], F32)
    mv_sb = sb("mv_sb", [128, 2], F32)
    rstd_sb = sb("rstd_sb", [128, 1], F32)
    negmr_sb = sb("negmr_sb", [128, 1], F32)
    r_nmax = sb("r_nmax", [128, 1], F32)
    r_ex = sb("r_ex", [128, E], F32)
    r_s1 = sb("r_s1", [128, 1], F32)
    r_probs = sb("r_probs", [128, E], F32)
    r_m8 = sb("r_m8", [128, 8], F32)
    r_idx8 = sb("r_idx8", [128, 8], U32)
    r_sel = sb("r_sel", [128, E], F32)
    r_selb = sb("r_selb", [128, E], BF16)
    r_kept = sb("r_kept", [128, E], F32)
    r_brank = sb("r_brank", [128, E], F32)
    r_oh = sb("r_oh", [128, E], F32)
    r_wk = sb("r_wk", [128, E], F32)
    r_t1 = sb("r_t1", [128, 1], F32)
    zero_u32 = sb("zero_u32", [128, 128], U32)
    fmsbA = sb("fmsbA", [128, 512], BF16)
    fmsbB = sb("fmsbB", [128, 512], BF16)
    sil_sb = sb("sil_sb", [128, 512], BF16)
    wpA = sb("wpA", [128, ICS, 128], BF16)
    wpB = sb("wpB", [128, ICS, 128], BF16)
    sgA = sb("sgA", [128, FC, 128], BF16)
    sgB = sb("sgB", [128, FC, 128], BF16)
    suA = sb("suA", [128, FC, 128], BF16)
    suB = sb("suB", [128, FC, 128], BF16)
    mybuf = sb("mybuf", [128, EL * 8], U32)
    xgA = sb("xgA", [128, D], BF16)
    xgB = sb("xgB", [128, D], BF16)
    es_ctx = ExitStack()
    ctx_all = sb("ctx_all", [128, NH, TL], BF16, stack=es_ctx)

    es_r1 = ExitStack()
    x_all = sb("x_all", [128, NT, D], F32, stack=es_r1, side="right")
    xnstA = sb("xnstA", [128, D], BF16, stack=es_r1, side="right")
    xnstB = sb("xnstB", [128, D], BF16, stack=es_r1, side="right")

    EPS = cf_sb[:, NMAIN + 1:NMAIN + 2]
    ZCOL = cf_sb[:, NMAIN + 2:NMAIN + 3]
    DIAGB = cf_sb[:, NMAIN:NMAIN + 1]

    for t, src in [(tri_sb, TRI), (ltri_sb, LTRI), (ident_sb, IDENT),
                   (onesb_sb, ONESB), (onesm_sb, ONESM), (onesf_sb, ONESF), (cf_sb, CF),
                   (psm_sb, PSM), (ejm_sb, EJM), (iota_sb, IOTA16), (vals_sb, VALS16),
                   (rowid_sb, ROWID), (bufidx_sb, BUFIDX), (battn_sb, BATTN)]:
        S.dma("sp", lambda e, t=t, src=src: e.dma_start(out=t[...], in_=src), "c_ld")
    S.dma("pool", lambda e: e.dma_start(out=gt_sb[...], in_=GT), "cg_ld")
    S.group_commit("c_ld", ["consts"])
    S.group_commit("cg_ld", ["consts_g"])
    S.dma("sp", lambda e: e.dma_start(
        out=x_all[...], in_=X.rearrange("(j p) d -> p j d", p=128)), "x_ld",
        writes=["x_all"])

    es_p1 = ExitStack()
    lg_ps = es_p1.enter_context(nc.psum_tensor("lg_ps", [128, 512], F32))[:, 0:E]
    cs_ps = es_p1.enter_context(nc.psum_tensor("cs_ps", [128, 512], F32))[:, 0:E]
    cnt_ps = es_p1.enter_context(nc.psum_tensor("cnt_ps", [1, 512], F32))[:, 0:E]
    qkv_psA = es_p1.enter_context(nc.psum_tensor("qkv_psA", [128, 512], F32))
    qkv_psB = es_p1.enter_context(nc.psum_tensor("qkv_psB", [128, 512], F32))
    vt_ps = es_p1.enter_context(nc.psum_tensor("vt_ps", [128, 1024], BF16))[:, 0:128]
    base_ps = es_p1.enter_context(nc.psum_tensor("base_ps", [NT, 512], F32))[:, 0:E]
    bc_ps = es_p1.enter_context(nc.psum_tensor("bc_ps", [128, 512], F32))[:, 0:E]

    for j in range(NT):
        xst, xt = (xnstA, ("xnst", 0)) if j % 2 == 0 else (xnstB, ("xnst", 1))
        for cch in range(4):
            S.emit("dve", lambda e, j=j, cch=cch: e.bn_stats(
                stat_sb[:, cch, :], x_all[:, j, cch * 512:(cch + 1) * 512]),
                reads=["x_all"], writes=["stat"])
        S.emit("dve", lambda e: e.bn_aggr(mv_sb[...], stat_sb[...]),
               reads=["stat"], writes=["mv"])
        S.emit("act", lambda e: e.activation(rstd_sb[...], mv_sb[:, 1:2], AF.Ln,
               bias=EPS, scale=1.0), reads=["mv", "consts"], writes=["rstd"])
        S.emit("act", lambda e: e.activation(rstd_sb[...], rstd_sb[...], AF.Exp,
               bias=ZCOL, scale=-0.5), reads=["consts"], writes=["rstd"])
        S.emit("dve", lambda e: e.tensor_scalar(negmr_sb[...], mv_sb[:, 0:1],
               rstd_sb[...], -1.0, ALU.mult, ALU.mult),
               reads=["mv", "rstd"], writes=["negmr"])
        S.emit("dve", lambda e, xst=xst, j=j: e.tensor_scalar(xst[...], x_all[:, j, :],
               rstd_sb[...], negmr_sb[...], ALU.mult, ALU.add),
               reads=["x_all", "rstd", "negmr"], writes=[xt])
        S.dma("sp", lambda e, xst=xst, j=j: e.dma_start(
            out=xnag_in[j * 128:(j + 1) * 128, :], in_=xst[...]), "xn_st",
            reads=[xt], writes=["xnag_in"])
    for fcc in range(FC):
        S.dma("sp", lambda e, fcc=fcc: e.dma_start_transpose(
            xn_fm[:, fcc, :], xnag_in[:, fcc * 128:(fcc + 1) * 128]), "xnfm_ld",
            reads=["xnag_in"])
    S.group_commit("xnfm_ld", ["xn_fm"])

    for j in range(NT):
        for fcc in range(FC):
            S.emit("pe", lambda e, j=j, fcc=fcc: e.matmul(
                lg_ps[...], xn_fm[:, fcc, j * 128:(j + 1) * 128], gt_sb[:, fcc, :],
                start=(fcc == 0), stop=(fcc == FC - 1)),
                reads=(["xn_fm", "consts", "consts_g"] if fcc in (0, FC - 1) else []),
                writes=(["lg_ps"] if fcc in (0, FC - 1) else []),
                inc=(fcc == FC - 1))
        S.emit("dve", lambda e: e.tensor_reduce(r_nmax[...], lg_ps[...], AX.X, ALU.max,
               negate=True), reads=["lg_ps"], writes=["r_nmax"])
        S.emit("act", lambda e: e.activation(r_ex[...], lg_ps[...], AF.Exp,
               bias=r_nmax[...], scale=1.0), reads=["lg_ps", "r_nmax"], writes=["r_ex"])
        S.emit("dve", lambda e: e.tensor_reduce(r_s1[...], r_ex[...], AX.X, ALU.add),
               reads=["r_ex"], writes=["r_s1"])
        S.emit("dve", lambda e: e.reciprocal(r_s1[...], r_s1[...]), writes=["r_s1"])
        S.emit("dve", lambda e: e.tensor_scalar(r_probs[...], r_ex[...], r_s1[...],
               None, ALU.mult), reads=["r_ex", "r_s1"], writes=["r_probs"])
        S.emit("dve", lambda e: e.max(r_m8[...], r_probs[...]),
               reads=["r_probs"], writes=["r_m8"])
        S.emit("dve", lambda e: e.max_index(r_idx8[...], r_m8[...], r_probs[...]),
               reads=["r_m8", "r_probs"], writes=["r_idx8"])
        S.emit("dve", lambda e, j=j: e.tensor_copy(e4f[:, j, :], r_idx8[:, 0:4]),
               reads=["r_idx8"], writes=["e4f"])
        S.emit("dve", lambda e: e.tensor_copy(r_t1[...], r_m8[:, 3:4]),
               reads=["r_m8"], writes=["r_t1"])
        S.emit("dve", lambda e: e.tensor_scalar(r_sel[...], r_probs[...], r_t1[...],
               None, ALU.is_ge), reads=["r_probs", "r_t1"], writes=["r_sel"])
        S.emit("dve", lambda e: e.tensor_copy(r_selb[...], r_sel[...]),
               reads=["r_sel"], writes=["r_selb"])
        S.emit("dve", lambda e: e.tensor_reduce(r_t1[...], r_m8[:, 0:4], AX.X, ALU.add),
               reads=["r_m8"], writes=["r_t1"])
        S.emit("dve", lambda e: e.reciprocal(r_t1[...], r_t1[...]), writes=["r_t1"])
        S.emit("dve", lambda e, j=j: e.scalar_tensor_tensor(w_all[:, j, :], r_probs[...],
               r_t1[...], r_sel[...], ALU.mult, ALU.mult),
               reads=["r_probs", "r_t1", "r_sel"], writes=["w_all"])
        S.emit("pe", lambda e: e.matmul(cs_ps[...], ltri_sb[...], r_selb[...],
               start=True, stop=True), reads=["consts", "r_selb"], writes=["cs_ps"])
        S.emit("pe", lambda e: e.matmul(cnt_ps[...], onesb_sb[...], r_selb[...],
               start=True, stop=True), reads=["consts", "r_selb"], writes=["cnt_ps"])
        S.emit("dve", lambda e, j=j: e.tensor_copy(excl_all[:, j, :], cs_ps[...]),
               reads=["cs_ps"], writes=["excl_all"])
        S.emit("dve", lambda e, j=j: e.tensor_copy(cntstage[:, j, :], cnt_ps[...]),
               reads=["cnt_ps"], writes=["cntstage"])
    S.dma("pool", lambda e: e.dma_start(out=cnt_in.rearrange("(o j) e -> o j e", o=1),
          in_=cntstage[...]), "cnt_st",
          reads=["cntstage"], writes=["cnt_in"])
    S.coll(lambda e: e.collective_compute("AllGather", ALU.bypass,
           replica_groups=[list(range(NC))], ins=[cnt_in.opt()], outs=[cnt_out.opt()]),
           "cc_cnt", reads=["cnt_in"], writes=["cnt_out"])
    S.coll(lambda e: e.collective_compute("AllGather", ALU.bypass,
           replica_groups=[list(range(NC))], ins=[xnag_in.opt()], outs=[xnag_out.opt()]),
           "cc_xn", reads=["xnag_in"], writes=["xnag_out"])

    if upto <= 1:
        return _finish()

    es_r1.close()
    es_r2 = ExitStack()
    Q_all = sb("Q_all", [128, NH, TL], BF16, stack=es_r2, side="right")
    K_own = sb("K_own", [128, TL], BF16, stack=es_r2, side="right")
    V_own = sb("V_own", [128, NT, HD], BF16, stack=es_r2, side="right")
    wbufA = sb("wbufA", [128, FC, 128], BF16, stack=es_r2, side="right")
    wbufB = sb("wbufB", [128, FC, 128], BF16, stack=es_r2, side="right")
    K_sb = sb("K_sb", [128, NC, TL], BF16, stack=es_r2, side="right")
    V_sb = sb("V_sb", [128, NC, NT * HD], BF16, stack=es_r2, side="right")
    E0 = sb("E0", [128, 1024], BF16, stack=es_r2, side="right")
    E1 = sb("E1", [128, 1024], BF16, stack=es_r2, side="right")
    E2 = sb("E2", [128, 1024], BF16, stack=es_r2, side="right")
    Ed = sb("Ed", [128, 1024], BF16, stack=es_r2, side="right")
    rbc_sb = sb("rbc_sb", [128, 1024], F32, stack=es_r2, side="right")
    es_r2b = ExitStack()
    V_fm_stage = sb("V_fm_stage", [128, TL], BF16, stack=es_r2b, side="right")

    for oc in range(OCQ):
        wb, wt = (wbufA, ("wq", 0)) if oc % 2 == 0 else (wbufB, ("wq", 1))
        S.dma("pool", lambda e, oc=oc, wb=wb: e.dma_start(out=wb[...], in_=WTA[oc]),
              "wq_ld" + str(oc % 2), writes=[wt])
        for th in range(2):
            par = (oc * 2 + th) % 2
            ps = qkv_psA if par == 0 else qkv_psB
            pt = ("qkv_ps", par)
            for fcc in range(FC):
                S.emit("pe", lambda e, th=th, fcc=fcc, ps=ps, wb=wb: e.matmul(
                    ps[...], wb[:, fcc, :], xn_fm[:, fcc, th * 512:(th + 1) * 512],
                    start=(fcc == 0), stop=(fcc == FC - 1)),
                    reads=([wt, "xn_fm"] if fcc in (0, FC - 1) else []),
                    writes=([pt] if fcc in (0, FC - 1) else []),
                    inc=(fcc == FC - 1))
            if oc < NH:
                dsts, dtag = (lambda oc=oc, th=th: Q_all[:, oc, th * 512:(th + 1) * 512]), "Q_all"
            elif oc == NH:
                dsts, dtag = (lambda oc=oc, th=th: K_own[:, th * 512:(th + 1) * 512]), "K_own"
            else:
                dsts, dtag = (lambda oc=oc, th=th: V_fm_stage[:, th * 512:(th + 1) * 512]), "V_fm"
            S.emit("dve", lambda e, dsts=dsts, ps=ps, oc=oc: e.tensor_scalar(
                dsts(), ps[...], battn_sb[:, oc:oc + 1], None, ALU.add),
                reads=[pt, "consts"], writes=[dtag])
    for j in range(NT):
        S.emit("pe", lambda e, j=j: e.transpose(
            vt_ps[...], V_fm_stage[:, j * 128:(j + 1) * 128], ident_sb[...]),
            reads=["V_fm", "consts"], writes=["vt_ps"])
        S.emit("dve", lambda e, j=j: e.tensor_copy(V_own[:, j, :], vt_ps[...]),
               reads=["vt_ps"], writes=["V_own"])
    S.dma("pool", lambda e: e.dma_start(out=kag_in, in_=K_own[...]), "k_st",
          reads=["K_own"], writes=["kag_in"])
    S.dma("pool", lambda e: e.dma_start(
        out=vag_in, in_=V_own[...].rearrange("p j d -> p (j d)")), "v_st",
        reads=["V_own"], writes=["vag_in"])
    S.coll(lambda e: e.collective_compute("AllGather", ALU.bypass,
           replica_groups=[list(range(NC))], ins=[kag_in.opt()], outs=[kag_out.opt()]),
           "cc_k", reads=["kag_in"], writes=["kag_out"])
    S.coll(lambda e: e.collective_compute("AllGather", ALU.bypass,
           replica_groups=[list(range(NC))], ins=[vag_in.opt()], outs=[vag_out.opt()]),
           "cc_v", reads=["vag_in"], writes=["vag_out"])
    for r in range(NC):
        S.dma("sp", lambda e, r=r: e.dma_start(
            out=K_sb[:, r, :], in_=kag_out[r * 128:(r + 1) * 128, :]), "ksb_ld",
            reads=["kag_out"])
        S.dma("sp", lambda e, r=r: e.dma_start(
            out=V_sb[:, r, :], in_=vag_out.rearrange("(r p) f -> r p f", p=128)[r]),
            "vsb_ld", reads=["vag_out"])
    S.group_commit("ksb_ld", ["K_sb"])
    S.group_commit("vsb_ld", ["V_sb"])

    if upto <= 2:
        return _finish()

    S.dma("pool", lambda e: e.dma_start(out=cnts_sb[...], in_=cnt_out), "cnts_ld",
          reads=["cnt_out"], writes=["cnts_sb"])
    S.emit("pe", lambda e: e.matmul(base_ps[...], psm_sb[...], cnts_sb[...],
           start=True, stop=True), reads=["consts", "cnts_sb"], writes=["base_ps"])
    S.emit("dve", lambda e: e.tensor_copy(base_sb[...], base_ps[...]),
           reads=["base_ps"], writes=["base_sb"])
    for j in range(NT):
        S.emit("pe", lambda e, j=j: e.matmul(bc_ps[...], ejm_sb[:, j * 128:(j + 1) * 128],
               base_sb[...], start=True, stop=True),
               reads=["consts", "base_sb"], writes=["bc_ps"])
        S.emit("dve", lambda e, j=j: e.tensor_copy(bc_sb[:, j, :], bc_ps[...]),
               reads=["bc_ps"], writes=["bc_sb"])
    def rank_unit0(j):
        S.emit("dve", lambda e, j=j: e.tensor_tensor(rank_all[:, j, :],
               excl_all[:, j, :], bc_sb[:, j, :], ALU.add),
               reads=["excl_all", "bc_sb"], writes=["rank_all"])
        S.emit("dve", lambda e, j=j: e.tensor_scalar(r_kept[...], rank_all[:, j, :],
               1023.5, None, ALU.is_le), reads=["rank_all"], writes=["r_kept"])
        S.emit("dve", lambda e, j=j: e.scalar_tensor_tensor(r_brank[...],
               r_kept[...], -1.0e6, rank_all[:, j, :], ALU.mult, ALU.add),
               reads=["r_kept", "rank_all"], writes=["r_brank"])
        S.emit("dve", lambda e: e.tensor_scalar(r_brank[...], r_brank[...], 1.0e6,
               None, ALU.add), writes=["r_brank"])
        S.emit("dve", lambda e, j=j: e.tensor_tensor(r_wk[...], w_all[:, j, :],
               r_kept[...], ALU.mult), reads=["w_all", "r_kept"],
               writes=["r_wk"])

    def rank_unitk(j, k):
        S.emit("dve", lambda e, j=j, k=k: e.tensor_scalar(r_oh[...], iota_sb[...],
               e4f[:, j, k:k + 1], None, ALU.is_equal),
               reads=["consts", "e4f"], writes=["r_oh"])
        S.emit("dve", lambda e: e.tensor_tensor(r_ex[...], r_oh[...], r_wk[...],
               ALU.mult), reads=["r_oh", "r_wk"], writes=["r_ex"])
        S.emit("dve", lambda e, j=j, k=k: e.tensor_reduce(w4k[:, j, k:k + 1],
               r_ex[...], AX.X, ALU.add), reads=["r_ex"], writes=["w4k"])
        S.emit("dve", lambda e: e.tensor_tensor(r_ex[...], r_oh[...], r_brank[...],
               ALU.mult), reads=["r_oh", "r_brank"], writes=["r_ex"])
        S.emit("dve", lambda e: e.tensor_reduce(r_t1[...], r_ex[...], AX.X, ALU.add),
               reads=["r_ex"], writes=["r_t1"])
        S.emit("dve", lambda e, j=j, k=k: e.scalar_tensor_tensor(r_t1[...],
               e4f[:, j, k:k + 1], 1024.0, r_t1[...], ALU.mult, ALU.add),
               reads=["e4f"], writes=["r_t1"])
        S.emit("dve", lambda e: e.tensor_scalar(r_t1[...], r_t1[...], 0.25,
               None, ALU.add), writes=["r_t1"])
        S.emit("dve", lambda e, j=j, k=k: e.tensor_copy(idxs[:, j, k:k + 1],
               r_t1[...]), reads=["r_t1"], writes=["idxs"])
        S.emit("dve", lambda e: e.tensor_tensor(r_t2[...], r_oh[...],
               vals_sb[...], ALU.mult), reads=["r_oh", "consts"],
               writes=["r_t2x"])
        S.emit("dve", lambda e: e.tensor_reduce(r_t3[...], r_t2[...], AX.X,
               ALU.add), reads=["r_t2x"], writes=["r_t3"])
        S.emit("dve", lambda e, j=j, k=k: e.tensor_tensor(
               idxs2[:, j, k:k + 1], r_t3[...], r_t1[...], ALU.add),
               reads=["r_t3", "r_t1"], writes=["idxs2"])

    rank_units = []
    for j in range(NT):
        rank_units.append(("t", j, 0))
        for k in range(4):
            rank_units.append(("k", j, k))

    if debug:
        S.dma("sp", lambda e: e.dma_start(out=dbg["d_cnt"], in_=cnt_out), "dbg",
              reads=["cnt_out"])
        S.dma("sp", lambda e: e.dma_start(
            out=dbg["d_q"], in_=Q_all[...].rearrange("p h t -> p (h t)")), "dbg",
            reads=["Q_all"])
        S.dma("sp", lambda e: e.dma_start(out=dbg["d_k"], in_=K_own[...]), "dbg",
              reads=["K_own"])

    if upto <= 3:
        return _finish()

    es_p1.close()
    es_r2b.close()
    S.barrier("attn_in", ("pe", "act", "dve"))

    es_p4 = ExitStack()
    s_psA = es_p4.enter_context(nc.psum_tensor("s_psA", [128, 1024], F32))
    s_psB = es_p4.enter_context(nc.psum_tensor("s_psB", [128, 1024], F32))
    ctx_ps = es_p4.enter_context(nc.psum_tensor("ctx_ps", [128, 1024], F32))
    den_ps = es_p4.enter_context(nc.psum_tensor("den_ps", [128, 1024], F32))

    def colof(j, i):
        return 4 * j * (j + 1) + i

    def emit_scores_exp(j, hg, i, ntile):
        diag = (i == ntile)
        par = i % 2
        sp_ = s_psA if par == 0 else s_psB
        spt = ("s_ps", par)
        Ebuf = (E0, E1, E2)[i % 3]
        Et = ("E", i % 3)
        if diag:
            kap = K_own[:, j * 128:(j + 1) * 128]
            vap = V_own[:, j, :]
            ktags = ["K_own", "V_own"]
            biasap = DIAGB
        else:
            R, J = i % 8, i // 8
            kap = K_sb[:, R, J * 128:(J + 1) * 128]
            vap = V_sb[:, R, J * HD:(J + 1) * HD]
            ktags = ["K_sb", "V_sb"]
            biasap = cf_sb[:, colof(j, i):colof(j, i) + 1]
        S.emit("pe", lambda e, kap=kap, j=j, hg=hg, sp_=sp_: e.matmul(
            sp_[:, 0:512], kap,
            Q_all[:, hg * 8:hg * 8 + 4, j * 128:(j + 1) * 128],
            start=True, stop=True),
            reads=ktags + ["Q_all"], writes=[spt], inc=False)
        S.emit("pe", lambda e, kap=kap, j=j, hg=hg, sp_=sp_: e.matmul(
            sp_[:, 512:1024], kap,
            Q_all[:, hg * 8 + 4:hg * 8 + 8, j * 128:(j + 1) * 128],
            start=True, stop=True),
            reads=ktags + ["Q_all"], writes=[spt])
        S.emit("act", lambda e, sp_=sp_, Ebuf=Ebuf, biasap=biasap: e.activation(
            Ebuf[...], sp_[...], AF.Exp, bias=biasap, scale=SCALE),
            reads=[spt, "consts"], writes=[Et])
        srcb, stag = Ebuf, Et
        if diag:
            S.emit("dve", lambda e, Ebuf=Ebuf: e.tensor_tensor(
                Ed[...], Ebuf[...], tri_sb[...], ALU.mult),
                reads=[Et, "consts"], writes=["Ed"])
            srcb, stag = Ed, "Ed"
        return (srcb, stag, vap, ktags, i == 0, diag)

    def emit_den_ctx(v):
        srcb, stag, vap, ktags, first, last = v
        S.emit("pe", lambda e, srcb=srcb, first=first, last=last: e.matmul(
            den_ps[:, 0:512], onesm_sb[...], srcb[:, 0:512],
            start=first, stop=last),
            reads=["consts", stag], writes=["den_ps"], inc=False)
        S.emit("pe", lambda e, srcb=srcb, first=first, last=last: e.matmul(
            den_ps[:, 512:1024], onesm_sb[...], srcb[:, 512:1024],
            start=first, stop=last),
            reads=["consts", stag],
            writes=(["den_ps"] if last else []), inc=last)
        S.emit("pe", lambda e, vap=vap, srcb=srcb, first=first, last=last: e.matmul(
            ctx_ps[:, 0:512], vap, srcb[:, 0:512], start=first, stop=last),
            reads=ktags + [stag], writes=["ctx_ps"], inc=False)
        S.emit("pe", lambda e, vap=vap, srcb=srcb, first=first, last=last: e.matmul(
            ctx_ps[:, 512:1024], vap, srcb[:, 512:1024], start=first, stop=last),
            reads=ktags + [stag],
            writes=(["ctx_ps"] if last else []), inc=True)

    for j in range(NT):
        for hg in range(2):
            ntile = 8 * j + 8
            pending = []
            for i in range(ntile + 1):
                pending.append(emit_scores_exp(j, hg, i, ntile))
                if len(pending) > 2:
                    emit_den_ctx(pending.pop(0))
            for v in pending:
                emit_den_ctx(v)
            S.emit("dve", lambda e: e.reciprocal(rbc_sb[...], den_ps[...]),
                   reads=["den_ps"], writes=["rbc"])
            S.emit("dve", lambda e, j=j, hg=hg: e.tensor_tensor(
                ctx_all[:, hg * 8:(hg + 1) * 8, j * 128:(j + 1) * 128],
                ctx_ps[...].rearrange("p (h q) -> p h q", h=8),
                rbc_sb[...].rearrange("p (h q) -> p h q", h=8), ALU.mult),
                reads=["ctx_ps", "rbc"], writes=["ctx_all"])
            if j * 2 + hg >= 1:
                for _ in range(6):
                    if rank_units:
                        u = rank_units.pop(0)
                        if u[0] == "t":
                            rank_unit0(u[1])
                        else:
                            rank_unitk(u[1], u[2])

    S.dma("pool", lambda e: e.dma_start(out=wpA[:, 0:FC, :], in_=WTP[0]),
          "proj_wl0", writes=[("wp", 0)])
    S.dma("pool", lambda e: e.dma_start(out=wpB[:, 0:FC, :], in_=WTP[1]),
          "proj_wl1", writes=[("wp", 1)])
    S.dma("pool", lambda e: e.dma_start(out=sgA[...], in_=SWG[0]),
          "swg_ld0", writes=[("sg", 0)])
    S.dma("pool", lambda e: e.dma_start(out=suA[...], in_=SWU[0]),
          "swu_ld0", writes=[("su", 0)])
    S.dma("pool", lambda e: e.dma_start(out=sgB[...], in_=SWG[1]),
          "swg_ld1", writes=[("sg", 1)])
    S.dma("pool", lambda e: e.dma_start(out=suB[...], in_=SWU[1]),
          "swu_ld1", writes=[("su", 1)])

    while rank_units:
        u = rank_units.pop(0)
        if u[0] == "t":
            rank_unit0(u[1])
        else:
            rank_unitk(u[1], u[2])

    S.emit("pool", lambda e: e.memset(zero_u32[...], 0), writes=["zero_u32"])
    S.dma("pool", lambda e: e.dma_start(
        out=buf_in.rearrange("(p k) one -> p k one", p=128), in_=zero_u32[...]
        .rearrange("p (k one) -> p k one", one=1)),
        "bufz", reads=["zero_u32"], writes=["buf_in"])
    bnd = {}

    def mkreg(e):
        if "r" not in bnd:
            bnd["r"] = e.to_reg(E * CAP - 1)
            bnd["r2"] = e.to_reg(T - 1)
            bnd["r3"] = e.to_reg(2 * NC * CAP - 1)
    sc_tags = []
    for j in range(NT):
        for k in range(4):
            tg = ("bsc", j, k)
            sc_tags.append(tg)

            def scat(e, j=j, k=k):
                mkreg(e)
                return e.indirect_dma_start(
                    out=buf_in,
                    out_offset=bass.IndirectOffsetOnAxis(ap=idxs[:, j, k:k + 1], axis=0),
                    in_=rowid_sb[:, j:j + 1], in_offset=None,
                    bounds_check=bnd["r"], oob_is_err=False)
            S.dma("pool", scat, "buf_sc", reads=["idxs", "consts", "buf_in"],
                  writes=[tg])
    S.coll(lambda e: e.collective_compute("AllReduce", ALU.add,
           replica_groups=[list(range(NC))], ins=[buf_in.opt()], outs=[buf_out.opt()]),
           "cc_buf", reads=["buf_in"] + sc_tags, writes=["buf_out"])

    for g in range(EL * 8):
        def gmb(e, g=g):
            mkreg(e)
            return e.indirect_dma_start(
                out=mybuf[:, g:g + 1], out_offset=None,
                in_=buf_out,
                in_offset=bass.IndirectOffsetOnAxis(ap=bufidx_sb[:, g:g + 1], axis=0),
                bounds_check=bnd["r"], oob_is_err=False)
        S.dma("pool", gmb, "mybuf_g", reads=["buf_out", "consts"])
    S.group_commit("mybuf_g", ["mybuf"])
    if debug:
        S.dma("sp", lambda e: e.dma_start(
            out=dbg["d_w"], in_=w_all[...].rearrange("p j e -> p (j e)")), "dbg",
            reads=["w_all"])
        S.dma("sp", lambda e: e.dma_start(
            out=dbg["d_rank"], in_=rank_all[...].rearrange("p j e -> p (j e)")), "dbg",
            reads=["rank_all"])
        S.dma("sp", lambda e: e.dma_start(
            out=dbg["d_bc"], in_=bc_sb[...].rearrange("p j e -> p (j e)")), "dbg",
            reads=["bc_sb"])
        S.dma("sp", lambda e: e.dma_start(
            out=dbg["d_excl"], in_=excl_all[...].rearrange("p j e -> p (j e)")), "dbg",
            reads=["excl_all"])
        S.dma("sp", lambda e: e.dma_start(
            out=dbg["d_idx"], in_=idxs[...].rearrange("p j k -> p (j k)")), "dbg",
            reads=["idxs"])
        S.dma("sp", lambda e: e.dma_start(
            out=dbg["d_idx2"], in_=idxs2[...].rearrange("p j k -> p (j k)")), "dbg",
            reads=["idxs2"])
        S.dma("sp", lambda e: e.dma_start(
            out=dbg["d_w4"], in_=w4k[...].rearrange("p j k -> p (j k)")), "dbg",
            reads=["w4k"])
        S.dma("sp", lambda e: e.dma_start(out=dbg["d_buf"], in_=buf_out), "dbg",
              reads=["buf_out"])
    for g in range(EL * 8):
        xg, xt = (xgA, ("xg", 0)) if g % 2 == 0 else (xgB, ("xg", 1))

        def gxg(e, g=g, xg=xg):
            mkreg(e)
            return e.indirect_dma_start(
                out=xg[...], out_offset=None,
                in_=xnag_out,
                in_offset=bass.IndirectOffsetOnAxis(ap=mybuf[:, g:g + 1], axis=0),
                bounds_check=bnd["r2"], oob_is_err=False)
        S.dma("pool", gxg, "xg_g" + str(g % 2), reads=["mybuf", "xnag_out"],
              writes=[xt])
        S.dma("pool", lambda e, g=g, xg=xg: e.dma_start(
            out=xg_dram[g * 128:(g + 1) * 128, :], in_=xg[...]),
            "xg_st", reads=[xt], writes=["xg_dram"])




    if debug:
        S.dma("sp", lambda e: e.dma_start(
            out=dbg["d_ctx"], in_=ctx_all[...].rearrange("p h t -> p (h t)")), "dbg",
            reads=["ctx_all"])

    if upto <= 4:
        return _finish()

    es_p4.close()
    es_r2.close()
    S.barrier("ffn_in", ("pe", "act", "dve", "sp"))

    es_p5 = ExitStack()
    g_psA = es_p5.enter_context(nc.psum_tensor("g_psA", [128, 512], F32))
    g_psB = es_p5.enter_context(nc.psum_tensor("g_psB", [128, 512], F32))
    u_psA = es_p5.enter_context(nc.psum_tensor("u_psA", [128, 512], F32))
    u_psB = es_p5.enter_context(nc.psum_tensor("u_psB", [128, 512], F32))
    tp_psA = es_p5.enter_context(nc.psum_tensor("tp_psA", [128, 1024], BF16))[:, 0:128]
    tp_psB = es_p5.enter_context(nc.psum_tensor("tp_psB", [128, 1024], BF16))[:, 0:128]

    es_r3a = ExitStack()
    stg_all = sb("stg_all", [128, NT, D], BF16, stack=es_r3a, side="right")
    es_r3c = ExitStack()
    bproj_sb = sb("bproj_sb", [1, D], F32, stack=es_r3c, side="right")
    bias_bc = sb("bias_bc", [128, D], F32, stack=es_r3c, side="right")

    S.dma("sp", lambda e: e.dma_start(out=bproj_sb[...], in_=BPROJ), "bp_ld",
          writes=["bproj"])
    for osl in range(4):
        ps, pt = (g_psA, ("g_ps", 0)) if osl % 2 == 0 else (g_psB, ("g_ps", 1))
        S.emit("pe", lambda e, ps=ps, osl=osl: e.matmul(
            ps[...], onesf_sb[...], bproj_sb[:, osl * 512:(osl + 1) * 512],
            start=True, stop=True), reads=["consts", "bproj"], writes=[pt])
        S.emit("act", lambda e, ps=ps, osl=osl: e.activation(
            bias_bc[:, osl * 512:(osl + 1) * 512], ps[...], AF.Copy, bias=0.0,
            scale=1.0), reads=[pt], writes=["bias_bc"])

    def fm_phase(name, noc, nic, slab_fn, rhs_fn, rtags, nth=2, prefetched=0):
        post_q = []

        def flush_post():
            fmsb, ft, oc, th = post_q.pop(0)
            for tt in range(4):
                tp, tpt = (tp_psA, ("tp", 0)) if tt % 2 == 0 else (tp_psB, ("tp", 1))
                ttg = th * 4 + tt
                S.emit("pe", lambda e, tp=tp, fmsb=fmsb, tt=tt: e.transpose(
                    tp[...], fmsb[:, tt * 128:(tt + 1) * 128], ident_sb[...]),
                    reads=[ft, "consts"], writes=[tpt])
                S.emit("act", lambda e, tp=tp, oc=oc, ttg=ttg: e.activation(
                    stg_all[:, ttg, oc * 128:(oc + 1) * 128], tp[...], AF.Copy,
                    bias=0.0, scale=1.0),
                    reads=[tpt], writes=[("stg", ttg, oc)])

        for oc in range(noc):
            wb, wt = (wpA, ("wp", 0)) if oc % 2 == 0 else (wpB, ("wp", 1))
            if oc >= prefetched:
                S.dma("pool", lambda e, oc=oc, wb=wb: slab_fn(e, wb, oc),
                      name + "_wl" + str(oc % 2), writes=[wt])
            for th in range(nth):
                par = (oc * nth + th) % 2
                ps = g_psA if par == 0 else g_psB
                pt = ("g_ps", par)
                for ic in range(nic):
                    S.emit("pe", lambda e, ps=ps, wb=wb, ic=ic, th=th: e.matmul(
                        ps[...], wb[:, ic, :], rhs_fn(ic, th),
                        start=(ic == 0), stop=(ic == nic - 1)),
                        reads=([wt] + rtags if ic in (0, nic - 1) else []),
                        writes=([pt] if ic in (0, nic - 1) else []),
                        inc=(ic == nic - 1))
                fmsb, ft = (fmsbA, ("fmsb", 0)) if par == 0 else (fmsbB, ("fmsb", 1))
                S.emit("dve", lambda e, fmsb=fmsb, ps=ps: e.tensor_copy(
                    fmsb[...], ps[...]), reads=[pt], writes=[ft])
                post_q.append((fmsb, ft, oc, th))
                if len(post_q) > 1:
                    flush_post()
        while post_q:
            flush_post()

    fm_phase("proj", OC, FC,
             lambda e, wb, oc: e.dma_start(out=wb[:, 0:FC, :], in_=WTP[oc]),
             lambda ic, th: ctx_all[:, ic, th * 512:(th + 1) * 512],
             ["ctx_all"], prefetched=2)
    for ttg in range(NT):
        if sub <= 3:
            break
        S.dma("pool", lambda e, ttg=ttg: e.dma_start(
            out=acc_dram[ttg * 128:(ttg + 1) * 128, :], in_=stg_all[:, ttg, :],
            max_dma_last_dim=2048),
            "acc_proj", reads=[("stg", ttg, oc2) for oc2 in range(OC)],
            writes=[("acc", ttg)])
        if sub <= 4:
            continue
        S.dma("pool", lambda e, ttg=ttg: e.dma_start(
            out=acc_dram[ttg * 128:(ttg + 1) * 128, :],
            in_=X[ttg * 128:(ttg + 1) * 128, :], accum_op=ALU.add,
            max_dma_last_dim=4096),
            "acc_x", reads=[("acc", ttg)], writes=[("acc", ttg)])
        S.dma("pool", lambda e, ttg=ttg: e.dma_start(
            out=acc_dram[ttg * 128:(ttg + 1) * 128, :], in_=bias_bc[...],
            accum_op=ALU.add, max_dma_last_dim=4096),
            "acc_b", reads=[("acc", ttg), "bias_bc"], writes=[("acc", ttg)])

    es_ctx.close()
    if upto <= 5:
        return _finish()

    es_r3b = ExitStack()
    mid_sb = sb("mid_sb", [128, ICS, TL], BF16, stack=es_r3b, side="right")

    for ic in range(ICS):
        gb, gbt = (sgA, ("sg", 0)) if ic % 2 == 0 else (sgB, ("sg", 1))
        ub, ubt = (suA, ("su", 0)) if ic % 2 == 0 else (suB, ("su", 1))
        if ic >= 2:
            S.dma("pool", lambda e, ic=ic, gb=gb: e.dma_start(out=gb[...], in_=SWG[ic]),
                  "swg_ld" + str(ic % 2), writes=[gbt])
            S.dma("pool", lambda e, ic=ic, ub=ub: e.dma_start(out=ub[...], in_=SWU[ic]),
                  "swu_ld" + str(ic % 2), writes=[ubt])
        for th in range(2):
            gps, gpt = (g_psA, ("g_ps", 0)) if th == 0 else (g_psB, ("g_ps", 1))
            ups, upt = (u_psA, ("u_ps", 0)) if th == 0 else (u_psB, ("u_ps", 1))
            for fcc in range(FC):
                S.emit("pe", lambda e, gps=gps, gb=gb, fcc=fcc, th=th: e.matmul(
                    gps[...], gb[:, fcc, :], xn_fm[:, fcc, th * 512:(th + 1) * 512],
                    start=(fcc == 0), stop=(fcc == FC - 1)),
                    reads=([gbt, "xn_fm"] if fcc in (0, FC - 1) else []),
                    writes=([("g_ps", th)] if fcc in (0, FC - 1) else []),
                    inc=(fcc == FC - 1))
            for fcc in range(FC):
                S.emit("pe", lambda e, ups=ups, ub=ub, fcc=fcc, th=th: e.matmul(
                    ups[...], ub[:, fcc, :], xn_fm[:, fcc, th * 512:(th + 1) * 512],
                    start=(fcc == 0), stop=(fcc == FC - 1)),
                    reads=([ubt, "xn_fm"] if fcc in (0, FC - 1) else []),
                    writes=([("u_ps", th)] if fcc in (0, FC - 1) else []),
                    inc=(fcc == FC - 1))
            S.emit("act", lambda e, gps=gps: e.activation(sil_sb[...], gps[...],
                   AF.Silu, bias=ZCOL, scale=1.0),
                   reads=[gpt, "consts"], writes=["sil"])
            S.emit("dve", lambda e, ic=ic, th=th, ups=ups: e.tensor_tensor(
                mid_sb[:, ic, th * 512:(th + 1) * 512], sil_sb[...], ups[...],
                ALU.mult), reads=["sil", upt], writes=[("mid", ic)])

    fm_phase("shd", OC, ICS,
             lambda e, wb, oc: e.dma_start(out=wb[...], in_=SWD[oc]),
             lambda ic, th: mid_sb[:, ic, th * 512:(th + 1) * 512],
             [("mid", icx) for icx in range(ICS)])
    for ttg in range(NT):
        S.dma("pool", lambda e, ttg=ttg: e.dma_start(
            out=acc_dram[ttg * 128:(ttg + 1) * 128, :], in_=stg_all[:, ttg, :],
            accum_op=ALU.add, max_dma_last_dim=2048),
            "acc_sh", reads=[("stg", ttg, oc2) for oc2 in range(OC)] + [("acc", ttg)],
            writes=[("acc", ttg)])

    if upto <= 6:
        return _finish()

    es_r3b.close()
    es_r3c.close()
    S.barrier("exp_in")

    es_r4 = ExitStack()
    xgfmA = sb("xgfmA", [128, FC, 512], BF16, stack=es_r4, side="right")
    xgfmB = sb("xgfmB", [128, FC, 512], BF16, stack=es_r4, side="right")
    emid = sb("emid", [128, ICE, 512], BF16, stack=es_r4, side="right")
    wgA_ = sb("wgA_", [128, FC, 128], BF16, stack=es_r4, side="right")
    wgB_ = sb("wgB_", [128, FC, 128], BF16, stack=es_r4, side="right")
    wuA_ = sb("wuA_", [128, FC, 128], BF16, stack=es_r4, side="right")
    wuB_ = sb("wuB_", [128, FC, 128], BF16, stack=es_r4, side="right")
    wdA_ = sb("wdA_", [128, ICE, 128], BF16, stack=es_r4, side="right")
    wdB_ = sb("wdB_", [128, ICE, 128], BF16, stack=es_r4, side="right")

    if debug:
        S.dma("sp", lambda e: e.dma_start(out=dbg["d_xg"], in_=xg_dram), "dbg",
              reads=["xg_dram"])

    for el in range(EL):
        for sc in range(2):
            ch = el * 2 + sc
            xgfm = xgfmA if ch % 2 == 0 else xgfmB
            xft = ("xgfm", ch % 2)
            row0 = el * CAP + sc * 512
            for fcc in range(FC):
                S.dma("sp", lambda e, fcc=fcc, row0=row0, xgfm=xgfm: e.dma_start_transpose(
                    xgfm[:, fcc, :], xg_dram[row0:row0 + 512, fcc * 128:(fcc + 1) * 128]),
                    "xgfm_ld" + str(ch % 2), reads=["xg_dram"],
                    writes=([xft] if fcc == 0 else []))
            S.group_commit("xgfm_ld" + str(ch % 2), [xft])
            for ic in range(ICE):
                par = ic % 2
                gb, gbt = (wgA_, ("ewg", 0)) if par == 0 else (wgB_, ("ewg", 1))
                ub, ubt = (wuA_, ("ewu", 0)) if par == 0 else (wuB_, ("ewu", 1))
                S.dma("pool", lambda e, el=el, ic=ic, gb=gb: e.dma_start(
                    out=gb[...], in_=WG[el, ic]), "ewg_ld" + str(par), writes=[gbt])
                S.dma("pool", lambda e, el=el, ic=ic, ub=ub: e.dma_start(
                    out=ub[...], in_=WU[el, ic]), "ewu_ld" + str(par), writes=[ubt])
                gps, gpt = (g_psA, ("g_ps", 0)) if par == 0 else (g_psB, ("g_ps", 1))
                ups, upt = (u_psA, ("u_ps", 0)) if par == 0 else (u_psB, ("u_ps", 1))
                for fcc in range(FC):
                    S.emit("pe", lambda e, gps=gps, gb=gb, fcc=fcc, xgfm=xgfm: e.matmul(
                        gps[...], gb[:, fcc, :], xgfm[:, fcc, :],
                        start=(fcc == 0), stop=(fcc == FC - 1)),
                        reads=([gbt, xft] if fcc in (0, FC - 1) else []),
                        writes=([gpt] if fcc in (0, FC - 1) else []),
                        inc=(fcc == FC - 1))
                for fcc in range(FC):
                    S.emit("pe", lambda e, ups=ups, ub=ub, fcc=fcc, xgfm=xgfm: e.matmul(
                        ups[...], ub[:, fcc, :], xgfm[:, fcc, :],
                        start=(fcc == 0), stop=(fcc == FC - 1)),
                        reads=([ubt, xft] if fcc in (0, FC - 1) else []),
                        writes=([upt] if fcc in (0, FC - 1) else []),
                        inc=(fcc == FC - 1))
                S.emit("act", lambda e, gps=gps: e.activation(sil_sb[...], gps[...],
                       AF.Silu, bias=ZCOL, scale=1.0),
                       reads=[gpt, "consts"], writes=["sil"])
                S.emit("dve", lambda e, ic=ic, ups=ups: e.tensor_tensor(
                    emid[:, ic, :], sil_sb[...], ups[...], ALU.mult),
                    reads=["sil", upt], writes=[("emid", ic)])
            epost = []

            def eflush():
                fmsb, ft, oc = epost.pop(0)
                for tt in range(4):
                    tp, tpt = (tp_psA, ("tp", 0)) if tt % 2 == 0 else (tp_psB, ("tp", 1))
                    S.emit("pe", lambda e, tp=tp, fmsb=fmsb, tt=tt: e.transpose(
                        tp[...], fmsb[:, tt * 128:(tt + 1) * 128], ident_sb[...]),
                        reads=[ft, "consts"], writes=[tpt])
                    S.emit("act", lambda e, tp=tp, oc=oc, tt=tt: e.activation(
                        stg_all[:, tt, oc * 128:(oc + 1) * 128], tp[...], AF.Copy,
                        bias=0.0, scale=1.0),
                        reads=[tpt], writes=[("stg", tt, oc)])

            for oc in range(OC):
                par = oc % 2
                db, dbt = (wdA_, ("ewd", 0)) if par == 0 else (wdB_, ("ewd", 1))
                S.dma("pool", lambda e, el=el, oc=oc, db=db: e.dma_start(
                    out=db[...], in_=WD[el, oc]), "ewd_ld" + str(par), writes=[dbt])
                fps, fpt = (g_psA, ("g_ps", 0)) if par == 0 else (g_psB, ("g_ps", 1))
                for ic in range(ICE):
                    S.emit("pe", lambda e, fps=fps, db=db, ic=ic: e.matmul(
                        fps[...], db[:, ic, :], emid[:, ic, :],
                        start=(ic == 0), stop=(ic == ICE - 1)),
                        reads=([dbt] + [("emid", icx) for icx in range(ICE)]
                               if ic in (0, ICE - 1) else []),
                        writes=([fpt] if ic in (0, ICE - 1) else []),
                        inc=(ic == ICE - 1))
                fmsb, ft = (fmsbA, ("fmsb", 0)) if par == 0 else (fmsbB, ("fmsb", 1))
                S.emit("dve", lambda e, fmsb=fmsb, fps=fps: e.tensor_copy(
                    fmsb[...], fps[...]), reads=[fpt], writes=[ft])
                epost.append((fmsb, ft, oc))
                if len(epost) > 1:
                    eflush()
            while epost:
                eflush()
            for tt in range(4):
                S.dma("pool", lambda e, tt=tt, row0=row0: e.dma_start(
                    out=fout_in[row0 + tt * 128:row0 + (tt + 1) * 128, :],
                    in_=stg_all[:, tt, :]), "fout_st",
                    reads=[("stg", tt, oc2) for oc2 in range(OC)],
                    writes=[("fout_in", el)])
        S.coll(lambda e, el=el: e.collective_compute(
            "AllGather", ALU.bypass, replica_groups=[list(range(NC))],
            ins=[fout_in[el * CAP:(el + 1) * CAP, :].opt()],
            outs=[fout_ag2[el].opt()]),
            "cc_fout", reads=[("fout_in", el)], writes=[("fout_ag", el)])

    if debug:
        S.dma("sp", lambda e: e.dma_start(out=dbg["d_fout"], in_=fout_in), "dbg",
              reads=[("fout_in", 0), ("fout_in", 1)])
        S.dma("sp", lambda e: e.dma_start(
            out=dbg["d_fag"], in_=fout_ag2.rearrange("a r d -> (a r) d")), "dbg",
            reads=[("fout_ag", 0), ("fout_ag", 1)])
        S.dma("sp", lambda e: e.dma_start(out=dbg["d_acc"], in_=acc_dram), "dbg",
              reads=[("acc", ttg) for ttg in range(NT)])

    if upto <= 7:
        return _finish()

    es_r4.close()
    es_r3a.close()
    S.barrier("asm_in")

    es_r5 = ExitStack()
    moe_g = [[sb(f"moe_g{k}_{p}", [128, D], BF16, stack=es_r5, side="right")
              for p in range(2)] for k in range(4)]
    acc_sbA = sb("acc_sbA", [128, D], F32, stack=es_r5, side="right")
    acc_sbB = sb("acc_sbB", [128, D], F32, stack=es_r5, side="right")
    out_sbA = sb("out_sbA", [128, D], F32, stack=es_r5, side="right")
    out_sbB = sb("out_sbB", [128, D], F32, stack=es_r5, side="right")

    fout_flat = fout_ag2.rearrange("a r d -> (a r) d")
    for k in range(4):
        for p in range(2):
            S.emit("dve", lambda e, k=k, p=p: e.memset(moe_g[k][p][...], 0.0),
                   writes=[("moe", k, p)])

    for j in range(NT):
        p = j % 2
        acc, at = (acc_sbA, ("acc_sb", 0)) if p == 0 else (acc_sbB, ("acc_sb", 1))
        out, ot = (out_sbA, ("out_sb", 0)) if p == 0 else (out_sbB, ("out_sb", 1))
        S.dma("sp", lambda e, j=j, acc=acc: e.dma_start(
            out=acc[...], in_=acc_dram[j * 128:(j + 1) * 128, :]),
            "acc_ld" + str(p), reads=[("acc", j)], writes=[at])
        for k in range(4):
            def gmo(e, j=j, k=k, p=p):
                mkreg(e)
                return e.indirect_dma_start(
                    out=moe_g[k][p][...], out_offset=None,
                    in_=fout_flat,
                    in_offset=bass.IndirectOffsetOnAxis(ap=idxs2[:, j, k:k + 1], axis=0),
                    bounds_check=bnd["r3"], oob_is_err=False)
            S.dma("pool", gmo, "moe_g" + str(k) + str(p),
                  reads=[("fout_ag", 0), ("fout_ag", 1), "idxs2"],
                  writes=[("moe", k, p)])
        S.emit("dve", lambda e, j=j, acc=acc, out=out, p=p: e.scalar_tensor_tensor(
            out[...], moe_g[0][p][...], w4k[:, j, 0:1], acc[...],
            ALU.mult, ALU.add),
            reads=[("moe", 0, p), "w4k", at], writes=[ot])
        for k in range(1, 4):
            S.emit("dve", lambda e, j=j, k=k, out=out, p=p: e.scalar_tensor_tensor(
                out[...], moe_g[k][p][...], w4k[:, j, k:k + 1], out[...],
                ALU.mult, ALU.add),
                reads=[("moe", k, p), "w4k"], writes=[ot])
        S.dma("sp", lambda e, j=j, out=out: e.dma_start(
            out=Y[j * 128:(j + 1) * 128, :], in_=out[...]), "y_st",
            reads=[ot], writes=[("y", j)])
    S.final_wait("sp", "y_st")
    if debug:
        S.final_wait("sp", "dbg")

    S.verify()
    with nc.Block() as block:
        S.replay(block)
    es_r5.close()
    es_p5.close()
    es.close()
    return nc, dbg


def _host_prep(c, inputs):
    x = np.asarray(inputs["L0_attn_hidden_states"]).reshape(T, D)
    caw = np.asarray(inputs["L0_attn_c_attn_w"])
    cab = np.asarray(inputs["L0_attn_c_attn_b"])
    cpw = np.asarray(inputs["L0_attn_c_proj_w"])
    cpb = np.asarray(inputs["L0_attn_c_proj_b"])
    gw = np.asarray(inputs["L0_ffn_gate_w"])
    wg = np.asarray(inputs["L0_ffn_w_gate"])
    wu = np.asarray(inputs["L0_ffn_w_up"])
    wd = np.asarray(inputs["L0_ffn_w_down"])
    swg = np.asarray(inputs["L0_ffn_sw_gate"])
    swu = np.asarray(inputs["L0_ffn_sw_up"])
    swd = np.asarray(inputs["L0_ffn_sw_down"])

    blocks = np.arange(NT) * NC + c
    m = {}
    m["x"] = np.ascontiguousarray(x.reshape(64, 128, D)[blocks].reshape(TL, D))
    m["wta"] = np.ascontiguousarray(caw.reshape(OCQ, 128, FC, 128).transpose(0, 3, 2, 1))
    m["wtp"] = np.ascontiguousarray(cpw.reshape(OC, 128, FC, 128).transpose(0, 3, 2, 1))
    m["gt"] = np.ascontiguousarray(gw.reshape(E, FC, 128).transpose(2, 1, 0))
    m["battn"] = np.ascontiguousarray(cab.reshape(OCQ, 128).T)
    m["bproj"] = np.ascontiguousarray(cpb.reshape(1, D))
    m["swg"] = np.ascontiguousarray(swg.reshape(FC, 128, ICS, 128).transpose(2, 1, 0, 3))
    m["swu"] = np.ascontiguousarray(swu.reshape(FC, 128, ICS, 128).transpose(2, 1, 0, 3))
    m["swd"] = np.ascontiguousarray(swd.reshape(ICS, 128, OC, 128).transpose(2, 1, 0, 3))
    m["wg"] = np.ascontiguousarray(
        wg[2 * c:2 * c + 2].reshape(EL, FC, 128, ICE, 128).transpose(0, 3, 2, 1, 4))
    m["wu"] = np.ascontiguousarray(
        wu[2 * c:2 * c + 2].reshape(EL, FC, 128, ICE, 128).transpose(0, 3, 2, 1, 4))
    m["wd"] = np.ascontiguousarray(
        wd[2 * c:2 * c + 2].reshape(EL, ICE, 128, OC, 128).transpose(0, 3, 2, 1, 4))

    tri = np.triu(np.ones((128, 128), np.float32))
    m["tri"] = np.ascontiguousarray(np.tile(tri, (1, 8))).astype(ml_dtypes.bfloat16)
    m["ltri"] = np.ascontiguousarray(
        np.tril(np.ones((128, 128), np.float32), -1).T).astype(ml_dtypes.bfloat16)
    m["ident"] = np.eye(128, dtype=ml_dtypes.bfloat16)
    m["onesb"] = np.ones((128, 1), ml_dtypes.bfloat16)
    m["onesm"] = np.ones((128, 128), ml_dtypes.bfloat16)
    m["onesf"] = np.ones((1, 128), np.float32)

    cfv = np.zeros((128, NMAIN + 3), np.float32)
    for j in range(NT):
        for i in range(8 * j + 8):
            cfv[:, 4 * j * (j + 1) + i] = -12.0 if i < 8 * j + c else -112.0
    cfv[:, NMAIN] = -12.0
    cfv[:, NMAIN + 1] = 1e-5
    cfv[:, NMAIN + 2] = 0.0
    m["cf"] = cfv

    psm = np.zeros((64, NT), np.float32)
    for im in range(64):
        R, jj = im // NT, im % NT
        g = NC * jj + R
        for j in range(NT):
            if g < NC * j + c:
                psm[im, j] = 1.0
    m["psm"] = psm.astype(ml_dtypes.bfloat16)

    ejm = np.zeros((NT, NT * 128), np.float32)
    for j in range(NT):
        ejm[j, j * 128:(j + 1) * 128] = 1.0
    m["ejm"] = ejm
    m["iota16"] = np.tile(np.arange(E, dtype=np.float32), (128, 1))
    ev = np.arange(E)
    v16 = (ev % 2) * 8192 + (ev // 2) * 1024 - ev * 1024
    m["vals16"] = np.tile(v16.astype(np.float32), (128, 1))
    rid = np.zeros((128, NT), np.uint32)
    for j in range(NT):
        rid[:, j] = c * TL + j * 128 + np.arange(128)
    m["rowid"] = rid
    bix = np.zeros((128, EL * 8), np.uint32)
    for g in range(EL * 8):
        bix[:, g] = 2048 * c + g * 128 + np.arange(128)
    m["bufidx"] = bix
    return m


_CACHE = {}


def kernel(**inputs):
    if "nc" not in _CACHE:
        _CACHE["nc"] = build(debug=False)[0]
    nc = _CACHE["nc"]
    in_maps = [_host_prep(c, inputs) for c in range(NC)]
    res = run_bass_kernel_spmd(nc, in_maps, core_ids=list(range(NC)))
    out = np.zeros((1, T, D), np.float32)
    for c in range(NC):
        y = res.results[c]["y"]
        blocks = np.arange(NT) * NC + c
        out[0].reshape(64, 128, D)[blocks] = y.reshape(NT, 128, D)
    return out
